# version 04
# baseline (speedup 1.0000x reference)
import contextlib
import numpy as np
import ml_dtypes
import concourse.bass as bass
import concourse.mybir as mybir
from concourse.bass_utils import run_bass_kernel_spmd

F32 = mybir.dt.float32
BF16 = mybir.dt.bfloat16
AF = mybir.ActivationFunctionType
ALU = mybir.AluOpType
AX = mybir.AxisListType

NCORE = 8
TMAIN = 1024
THALO = 128
D = 2048
KD = 16
OUT_IN = 8512
INTER = 4096
CONVD = 4352
XBCD = 4416
NCT = 35
NN = 128
HH = 64
PH = 64
CHK = 256
NCHU = 4
EE = 8192
WSZ = float(D * EE)
EPS_LN = 1e-5
EPS_RMS = 1e-8
EPS_Q = 1e-8

ENGS = ("tensor", "vector", "scalar", "gpsimd", "sync")


class Sem:
    __slots__ = ("h", "name", "n")

    def __init__(self, h, name):
        self.h = h
        self.name = name
        self.n = 0


class Ph:
    def __init__(self):
        self.q = {e: [] for e in ENGS}

    def add(self, eng, fn):
        self.q[eng].append(fn)

    def emit(self, block):
        for e in ENGS:
            if self.q[e]:
                getattr(block, e)(lambda eng, fl=tuple(self.q[e]): [f(eng) for f in fl])
        self.q = {e: [] for e in ENGS}


class B:
    def __init__(self, mk):
        self.mk = mk
        self.streams = {}
        self.lastwait = {}
        self.guards = {}

    def q(self, name):
        s = self.streams.get(name)
        if s is None:
            s = Sem(self.mk(name), name)
            self.streams[name] = s
        return s

    def wait(self, ph, eng, sem, tok):
        if not tok:
            return
        key = (eng, sem.name)
        if self.lastwait.get(key, 0) >= tok:
            return
        self.lastwait[key] = tok
        ph.add(eng, lambda e, _h=sem.h, _t=tok: e.wait_ge(_h, _t))

    SELFSEM = {"vector": "selfDV", "scalar": "selfAC", "gpsimd": "selfPO"}

    def op(self, ph, eng, fn, inc=None, k=1):
        ss = self.SELFSEM.get(eng)
        if ss is not None:
            last = getattr(self, "_lastdone", None)
            if last is None:
                last = self._lastdone = {}
            lp = last.get(eng)
            if lp is not None:
                ph.add(eng, lambda e, _h=lp[0], _p=lp[1]: e.wait_ge(_h, _p))
            if inc is not None:
                inc.n += k
                tok = inc.n
                last[eng] = (inc.h, tok)
                ph.add(eng, lambda e, _f=fn, _h=inc.h, _k=k: _f(e).then_inc(_h, _k))
                return tok
            sem = self.q(ss)
            sem.n += 1
            last[eng] = (sem.h, sem.n)
            ph.add(eng, lambda e, _f=fn, _sh=sem.h: _f(e).then_inc(_sh, 1))
            return None
        if inc is not None:
            inc.n += k
            tok = inc.n
            ph.add(eng, lambda e, _f=fn, _h=inc.h, _k=k: _f(e).then_inc(_h, _k))
            return tok
        ph.add(eng, fn)
        return None

    def fence(self, ph, comp_sems, dma_streams):
        pairs = [(s, s.n) for s in comp_sems]
        pairs += [(self.q(n), self.q(n).n) for n in dma_streams if n in self.streams]
        for eng in ENGS:
            for s, t in pairs:
                self.wait(ph, eng, s, t)

    def op_cc(self, ph, fn, sem):
        sem.n += 1
        ph.add("gpsimd", lambda e, _f=fn, _h=sem.h: _f(e).then_inc(_h))
        return sem.n

    def dma(self, ph, eng, stream, dst, src, **kw):
        sem = self.q(stream) if isinstance(stream, str) else stream
        sem.n += 16
        tok = sem.n
        ph.add(eng, lambda e, _d=dst, _s=src, _h=sem.h, _kw=kw:
               e.dma_start(out=_d, in_=_s, **_kw).then_inc(_h, 16))
        return tok

    def guard(self, ph, eng, key):
        for s, t in self.guards.get(key, ()):
            self.wait(ph, eng, s, t)

    def release(self, key, *pairs):
        self.guards[key] = [p for p in pairs if p[1]]


def build_nc():
    nc = bass.Bass("TRN2", target_bir_lowering=False)
    ph = Ph()

    ap = lambda t: t.ap()
    x_ext = ap(nc.dram_tensor("x_ext", [THALO + TMAIN, D], F32, kind="ExternalInput"))
    w_in = ap(nc.dram_tensor("w_in", [D, OUT_IN], F32, kind="ExternalInput"))
    w_out = ap(nc.dram_tensor("w_out", [INTER, D], F32, kind="ExternalInput"))
    wu_s = ap(nc.dram_tensor("wu_s", [D // NCORE, EE], F32, kind="ExternalInput"))
    wg_s = ap(nc.dram_tensor("wg_s", [D // NCORE, EE], F32, kind="ExternalInput"))
    wd_s = ap(nc.dram_tensor("wd_s", [EE // NCORE, D], F32, kind="ExternalInput"))
    conv_wt = ap(nc.dram_tensor("conv_wt", [CONVD, 4], F32, kind="ExternalInput"))
    conv_bt = ap(nc.dram_tensor("conv_bt", [CONVD, 1], F32, kind="ExternalInput"))
    dt_bias = ap(nc.dram_tensor("dt_bias", [HH, 1], F32, kind="ExternalInput"))
    a_log = ap(nc.dram_tensor("a_log", [HH, 1], F32, kind="ExternalInput"))
    d_row = ap(nc.dram_tensor("d_row", [1, INTER], F32, kind="ExternalInput"))
    nw_row = ap(nc.dram_tensor("nw_row", [1, INTER], F32, kind="ExternalInput"))
    ident_b = ap(nc.dram_tensor("ident_b", [128, 128], BF16, kind="ExternalInput"))
    ident_f = ap(nc.dram_tensor("ident_f", [128, 128], F32, kind="ExternalInput"))
    triu_in = ap(nc.dram_tensor("triu_in", [128, 128], F32, kind="ExternalInput"))
    selt_in = ap(nc.dram_tensor("selt_in", [8, 8], F32, kind="ExternalInput"))
    pmask_in = ap(nc.dram_tensor("pmask_in", [8, 1], F32, kind="ExternalInput"))
    out_d = ap(nc.dram_tensor("out_d", [TMAIN, D], F32, kind="ExternalOutput"))

    x_tm_d = ap(nc.dram_tensor("x_tm_d", [TMAIN, INTER], BF16))
    s_st_d = ap(nc.dram_tensor("s_st_d", [NCHU * 128, INTER], BF16))
    y_d = ap(nc.dram_tensor("y_d", [TMAIN, INTER], BF16))
    z_d = ap(nc.dram_tensor("z_d", [TMAIN, INTER], BF16))
    s_d = ap(nc.dram_tensor("s_d", [TMAIN, D], BF16))
    cc1_i = nc.dram_tensor("cc1_i", [4, 1], F32)
    cc1_o = nc.dram_tensor("cc1_o", [4, 1], F32)
    tau_u_s = nc.dram_tensor("tau_u_s", [D // NCORE, EE], BF16)
    tau_g_s = nc.dram_tensor("tau_g_s", [D // NCORE, EE], BF16)
    tau_d_s = nc.dram_tensor("tau_d_s", [EE // NCORE, D], BF16)
    tau_u = nc.dram_tensor("tau_u", [D, EE], BF16)
    tau_g = nc.dram_tensor("tau_g", [D, EE], BF16)
    tau_d = nc.dram_tensor("tau_d", [EE, D], BF16)
    scr_d = nc.dram_tensor("scr_d", [4, 1], F32)
    w8_d = nc.dram_tensor("w8_d", [8, HH], F32)
    csf_d = nc.dram_tensor("csf_d", [NCHU * HH, CHK], F32)
    st_i = nc.dram_tensor("st_i", [129, INTER], F32)
    st_o = nc.dram_tensor("st_o", [NCORE * 129, INTER], F32)

    stack = contextlib.ExitStack()
    b = B(lambda name: stack.enter_context(nc.semaphore(name)))
    sem = {}
    for name in ("cPE", "cDV", "cAC", "cPO",
                 "cc1", "cc2", "cc2b", "cc2c", "cc3", "qOUT"):
        sem[name] = Sem(stack.enter_context(nc.semaphore(name)), name)
    cPE, cDV, cAC, cPO = sem["cPE"], sem["cDV"], sem["cAC"], sem["cPO"]
    cc1, cc2, cc3, qOUT = sem["cc1"], sem["cc2"], sem["cc3"], sem["qOUT"]
    cc2b, cc2c = sem["cc2b"], sem["cc2c"]

    sb = lambda n, s, d: stack.enter_context(nc.sbuf_tensor(n, s, d))
    pst = lambda n, s, d: stack.enter_context(nc.psum_tensor(n, s, d))

    xn_fm = sb("xn_fm", [128, KD * TMAIN], BF16)
    xn_halo = sb("xn_halo", [128, KD * 4], BF16)
    bfm = sb("bfm", [128, TMAIN], BF16)
    cfm = sb("cfm", [128, TMAIN], BF16)
    btm = sb("btm", [128, NCHU * 2 * 128], BF16)
    dt_fm = sb("dt_fm", [HH, TMAIN], F32)
    csneg_fm = sb("csneg_fm", [HH, TMAIN], F32)
    csneg_tm = sb("csneg_tm", [128, NCHU * 2 * HH], F32)
    dt_tm = sb("dt_tm", [128, NCHU * 2 * HH], F32)
    dtdec_tm = sb("dtdec_tm", [128, NCHU * 2 * HH], F32)
    dec_bc = sb("dec_bc", [128, NCHU * HH], F32)
    identb = sb("identb", [128, 128], BF16)
    identf = sb("identf", [128, 128], F32)
    triu = sb("triu", [128, 128], F32)
    ta = sb("ta", [128, 256], F32)
    tb_c = sb("tb_c", [128, 256], F32)
    onesf = sb("onesf", [128, 128], F32)
    alog_c = sb("alog_c", [HH, 1], F32)
    expa_c = sb("expa_c", [HH, 1], F32)
    dtb_c = sb("dtb_c", [HH, 1], F32)
    selt = sb("selt", [8, 8], F32)
    pmask = sb("pmask", [8, 1], F32)
    sc4 = sb("sc4", [4, 1], F32)
    scrow = sb("scrow", [1, 4], F32)
    scbc = sb("scbc", [128, 4], F32)
    sgsd_c = sb("sgsd_c", [128, 1], F32)
    negrow = sb("negrow", [1, HH], F32)
    decrow = sb("decrow", [1, HH], F32)
    lcrow = sb("lcrow", [1, HH], F32)
    cpool = sb("cpool", [128, 4], F32)
    _cvals = [EPS_LN, 1.0, EPS_RMS]
    for _j, _v in enumerate(_cvals):
        nc.const_aps.register(F32, _v, cpool[:, _j:_j + 1])

    ps_a = pst("ps_a", [128, 512], F32)
    ps_b = pst("ps_b", [128, 512], F32)
    ps_c = pst("ps_c", [128, 512], F32)
    ps_d = pst("ps_d", [128, 512], F32)
    ps_e = pst("ps_e", [128, 512], F32)
    ps_bc = pst("ps_bc", [128, 512], F32)
    ps_tb = pst("ps_tb", [128, 1024], BF16)
    ps_tb2 = pst("ps_tb2", [128, 1024], BF16)

    block = stack.enter_context(nc.Block())

    pname = {id(ps_a): "ps_a", id(ps_b): "ps_b", id(ps_c): "ps_c", id(ps_d): "ps_d",
             id(ps_e): "ps_e", id(ps_bc): "ps_bc", id(ps_tb): "ps_tb",
             id(ps_tb2): "ps_tb2"}
    tbanks = [ps_tb, ps_tb2]

    def pguard(p, eng="tensor"):
        b.guard(ph, eng, pname[id(p)])

    def prel(p, s, t):
        b.release(pname[id(p)], (s, t))

    b.dma(ph, "sync", "cn", identb[:, :], ident_b)
    b.dma(ph, "sync", "cn", identf[:, :], ident_f)
    b.dma(ph, "sync", "cn", triu[:, :], triu_in)
    b.dma(ph, "sync", "cn", alog_c[:, :], a_log)
    b.dma(ph, "sync", "cn", dtb_c[:, :], dt_bias)
    b.dma(ph, "sync", "cn", selt[:, :], selt_in)
    t0 = b.dma(ph, "sync", "cn", pmask[:, :], pmask_in)
    b.wait(ph, "vector", b.q("cn"), t0)
    for _j, _v in enumerate(_cvals):
        b.op(ph, "vector", lambda e, _jj=_j, _vv=_v: e.memset(cpool[:, _jj:_jj + 1], _vv))
    b.op(ph, "vector", lambda e: e.memset(onesf[:, :], 1.0))
    b.op(ph, "vector", lambda e: e.tensor_copy(ta[:, 0:128], triu[:, :]))
    b.op(ph, "vector", lambda e: e.memset(ta[:, 128:256], 1.0))
    b.op(ph, "vector", lambda e: e.memset(tb_c[:, 0:128], 0.0))
    t_consts = b.op(ph, "vector", lambda e: e.tensor_copy(tb_c[:, 128:256], triu[:, :]),
                    inc=cDV)
    b.wait(ph, "scalar", b.q("cn"), t0)
    t_expa = b.op(ph, "scalar", lambda e: e.activation(expa_c[:, :], alog_c[:, :], AF.Exp),
                  inc=cAC)

    _es1 = contextlib.ExitStack()
    with _es1:
        xin = _es1.enter_context(nc.sbuf_tensor("xin", [128, 2 * D], F32))
        lnscr = _es1.enter_context(nc.sbuf_tensor("lnscr", [128, D], F32))
        lncols = _es1.enter_context(nc.sbuf_tensor("lncols", [128, 16], F32))
        xnt = _es1.enter_context(nc.sbuf_tensor("xnt", [128, 2 * D], BF16))

        x_tok = [0] * 9
        x_tok[0] = b.dma(ph, "sync", "x10", xin[:, 0:D], x_ext[0:128, :])
        x_tok[1] = b.dma(ph, "sync", "x11", xin[:, D:2 * D], x_ext[128:256, :])
        for tt in range(9):
            par = tt % 2
            xi = xin[:, par * D:(par + 1) * D]
            xo = xnt[:, par * D:(par + 1) * D]
            sm = lncols[:, 0:1]; mcol = lncols[:, 1:2]; sq = lncols[:, 2:3]
            var = lncols[:, 3:4]; msq = lncols[:, 4:5]; rstd = lncols[:, 5:6]
            nbias = lncols[:, 6:7]
            b.wait(ph, "vector", b.q(f"x1{par}"), x_tok[tt])
            b.op(ph, "vector", lambda e, _x=xi, _o=sm: e.reduce_sum(_o, _x, axis=AX.X))
            b.op(ph, "vector", lambda e, _o=mcol, _i=sm:
                 e.tensor_scalar(_o, _i, 1.0 / D, None, ALU.mult))
            b.wait(ph, "scalar", b.q(f"x1{par}"), x_tok[tt])
            t_sq = b.op(ph, "scalar", lambda e, _x=xi, _a=sq:
                        e.activation(lnscr[:, :], _x, AF.Square, accum_out=_a), inc=cAC)
            b.wait(ph, "vector", cAC, t_sq)
            b.op(ph, "vector", lambda e, _o=var, _i=sq:
                 e.tensor_scalar(_o, _i, 1.0 / D, None, ALU.mult))
            b.op(ph, "vector", lambda e, _o=msq, _m=mcol:
                 e.tensor_tensor(_o, _m, _m, ALU.mult))
            t_var = b.op(ph, "vector", lambda e, _o=var, _m=msq:
                         e.tensor_tensor(_o, _o, _m, ALU.subtract), inc=cDV)
            b.wait(ph, "scalar", cDV, t_var)
            b.op(ph, "scalar", lambda e, _o=var: e.activation(_o, _o, AF.Ln, bias=EPS_LN))
            t_rs = b.op(ph, "scalar", lambda e, _o=rstd, _i=var:
                        e.activation(_o, _i, AF.Exp, scale=-0.5), inc=cAC)
            b.wait(ph, "vector", cAC, t_rs)
            t_nb = b.op(ph, "vector", lambda e, _o=nbias, _m=mcol, _r=rstd:
                        e.scalar_tensor_tensor(_o, _m, -1.0, _r, ALU.mult, ALU.mult),
                        inc=cDV)
            b.guard(ph, "vector", ("xnt", par))
            t_xn = b.op(ph, "vector", lambda e, _x=xi, _o=xo, _r=rstd, _nb=nbias:
                        e.tensor_scalar(_o, _x, _r, _nb, ALU.mult, ALU.add), inc=cDV)
            if tt + 2 < 9:
                b.wait(ph, "sync", cDV, t_xn)
                x_tok[tt + 2] = b.dma(ph, "sync", f"x1{par}", xi,
                                      x_ext[(tt + 2) * 128:(tt + 3) * 128, :])
            b.wait(ph, "tensor", cDV, t_xn)
            last_pe = 0
            for k in range(KD):
                bk = tbanks[k % 2]
                pguard(bk)
                t_pe = b.op(ph, "tensor",
                            lambda e, _i=xo[:, k * 128:(k + 1) * 128], _bk=bk:
                            e.transpose(_bk[:, 0:128], _i, identb[:, :]), inc=cPE)
                last_pe = t_pe
                b.wait(ph, "vector", cPE, t_pe)
                if tt == 0:
                    dst = xn_halo[:, k * 4:k * 4 + 3]
                    src = bk[:, 125:128]
                else:
                    dst = xn_fm[:, k * TMAIN + (tt - 1) * 128:k * TMAIN + tt * 128]
                    src = bk[:, 0:128]
                t_ev = b.op(ph, "vector", lambda e, _o=dst, _s=src:
                            e.tensor_copy(_o, _s), inc=cDV)
                prel(bk, cDV, t_ev)
            b.release(("xnt", par), (cPE, last_pe))

        _es2 = contextlib.ExitStack()
        with _es2:
            wt1 = _es2.enter_context(nc.sbuf_tensor("wt1", [128, 2 * KD * 128], BF16))
            pre = _es2.enter_context(nc.sbuf_tensor("pre", [128, 2 * 1027], F32))
            cacc = _es2.enter_context(nc.sbuf_tensor("cacc", [128, 4 * TMAIN], F32))
            cvo = _es2.enter_context(nc.sbuf_tensor("cvo", [128, 2 * TMAIN], BF16))
            xstg = _es2.enter_context(nc.sbuf_tensor("xstg", [128, 2 * TMAIN], BF16))
            cw = _es2.enter_context(nc.sbuf_tensor("cw", [128, 2 * 4], F32))
            cb = _es2.enter_context(nc.sbuf_tensor("cb", [128, 2], F32))
            dtraw = _es2.enter_context(nc.sbuf_tensor("dtraw", [HH, TMAIN], F32))
            dtscr = _es2.enter_context(nc.sbuf_tensor("dtscr", [HH, TMAIN], F32))

            w_in_r = w_in.rearrange("(k p) c -> p k c", p=128)

            def wt_dma(ct):
                ncols = 128 if ct < 34 else 64
                par = ct % 2
                return b.dma(ph, "gpsimd", f"wm{par}",
                             wt1[:, par * KD * 128:par * KD * 128 + KD * ncols]
                             .rearrange("p (k c) -> p k c", c=ncols),
                             w_in_r[:, :, INTER + ct * 128:INTER + ct * 128 + ncols])

            wt_tok = [0] * NCT
            wt_tok[0] = wt_dma(0)
            wt_tok[1] = wt_dma(1)
            pe_mm_tok = [0] * NCT
            conv_tok = [0] * NCT
            cv_tok = [0] * NCT
            stg_dma = [0] * NCT
            cw_tok = [0] * NCT
            cb_tok = [0] * NCT
            cw_tok[0] = b.dma(ph, "scalar", "cw0", cw[:, 0:4], conv_wt[0:128, :])
            cb_tok[0] = b.dma(ph, "scalar", "cb0", cb[:, 0:1], conv_bt[0:128, :])
            cw_tok[1] = b.dma(ph, "scalar", "cw1", cw[:, 4:8], conv_wt[128:256, :])
            cb_tok[1] = b.dma(ph, "scalar", "cb1", cb[:, 1:2], conv_bt[128:256, :])

            for ct in range(NCT):
                par = ct % 2
                ncols = 128 if ct < 34 else 64
                bank0 = [ps_a, ps_c][par]
                bank1 = [ps_b, ps_d][par]
                b.wait(ph, "tensor", b.q(f"wm{par}"), wt_tok[ct])
                pguard(bank0)
                pguard(bank1)
                pguard(ps_e)
                t_mm = 0
                for k in range(KD):
                    w_ap = wt1[:, par * KD * 128 + k * ncols:par * KD * 128 + (k + 1) * ncols]
                    st = (k == 0); sp = (k == KD - 1)
                    b.op(ph, "tensor", lambda e, _w=w_ap, _o=bank0[0:ncols, :],
                         _r=xn_fm[:, k * TMAIN:k * TMAIN + 512], _st=st, _sp=sp:
                         e.matmul(_o, _w, _r, start=_st, stop=_sp))
                    b.op(ph, "tensor", lambda e, _w=w_ap, _o=bank1[0:ncols, :],
                         _r=xn_fm[:, k * TMAIN + 512:k * TMAIN + 1024], _st=st, _sp=sp:
                         e.matmul(_o, _w, _r, start=_st, stop=_sp))
                    t_mm = b.op(ph, "tensor", lambda e, _w=w_ap, _o=ps_e[0:ncols, 0:3],
                                _r=xn_halo[:, k * 4:k * 4 + 3], _st=st, _sp=sp:
                                e.matmul(_o, _w, _r, start=_st, stop=_sp),
                                inc=(cPE if sp else None))
                pe_mm_tok[ct] = t_mm
                if ct + 2 < NCT:
                    b.wait(ph, "gpsimd", cPE, t_mm)
                    wt_tok[ct + 2] = wt_dma(ct + 2)
                b.wait(ph, "scalar", cPE, t_mm)
                if ct < 34:
                    prb = pre[:, par * 1027:(par + 1) * 1027]
                    b.guard(ph, "scalar", ("pre", par))
                    t_evh = b.op(ph, "scalar", lambda e, _o=prb[:, 0:3]:
                                 e.copy(_o, ps_e[:, 0:3]), inc=cAC)
                    prel(ps_e, cAC, t_evh)
                    b.op(ph, "scalar", lambda e, _o=prb[:, 3:515], _s=bank0[:, :]:
                         e.copy(_o, _s))
                    t_evp = b.op(ph, "scalar", lambda e, _o=prb[:, 515:1027], _s=bank1[:, :]:
                                 e.copy(_o, _s), inc=cAC)
                    prel(bank0, cAC, t_evp)
                    prel(bank1, cAC, t_evp)
                    a0 = cacc[:, (par * 2) * TMAIN:(par * 2 + 1) * TMAIN]
                    a1 = cacc[:, (par * 2 + 1) * TMAIN:(par * 2 + 2) * TMAIN]
                    b.wait(ph, "vector", cAC, t_evp)
                    b.wait(ph, "vector", b.q(f"cw{par}"), cw_tok[ct])
                    if ct >= 2:
                        b.wait(ph, "vector", cAC, conv_tok[ct - 2])
                    wc = cw[:, par * 4:par * 4 + 4]
                    b.op(ph, "vector", lambda e, _o=a0, _i=prb[:, 0:1024], _w=wc[:, 0:1]:
                         e.tensor_scalar(_o, _i, _w, None, ALU.mult))
                    b.op(ph, "vector", lambda e, _o=a1, _i=prb[:, 1:1025], _w=wc[:, 1:2],
                         _a=a0: e.scalar_tensor_tensor(_o, _i, _w, _a, ALU.mult, ALU.add))
                    b.op(ph, "vector", lambda e, _o=a0, _i=prb[:, 2:1026], _w=wc[:, 2:3],
                         _a=a1: e.scalar_tensor_tensor(_o, _i, _w, _a, ALU.mult, ALU.add))
                    cv_tok[ct] = b.op(ph, "vector",
                                      lambda e, _o=a1, _i=prb[:, 3:1027], _w=wc[:, 3:4],
                                      _a=a0:
                                      e.scalar_tensor_tensor(_o, _i, _w, _a, ALU.mult,
                                                             ALU.add), inc=cDV)
                    b.release(("pre", par), (cDV, cv_tok[ct]))
                    if ct < 32:
                        cdst = cvo[:, par * TMAIN:(par + 1) * TMAIN]
                        b.guard(ph, "scalar", ("cvo", par))
                    elif ct == 32:
                        cdst = bfm[:, :]
                    else:
                        cdst = cfm[:, :]
                    b.wait(ph, "scalar", cDV, cv_tok[ct])
                    b.wait(ph, "scalar", b.q(f"cb{par}"), cb_tok[ct])
                    conv_tok[ct] = b.op(ph, "scalar",
                                        lambda e, _o=cdst, _i=a1, _b=cb[:, par:par + 1]:
                                        e.activation(_o, _i, AF.Silu, bias=_b), inc=cAC)
                    if ct + 2 < 34:
                        b.wait(ph, "scalar", cDV, cv_tok[ct])
                        cw_tok[ct + 2] = b.dma(ph, "scalar", f"cw{par}", cw[:, par * 4:par * 4 + 4],
                                               conv_wt[(ct + 2) * 128:(ct + 3) * 128, :])
                        b.wait(ph, "scalar", cAC, conv_tok[ct])
                        cb_tok[ct + 2] = b.dma(ph, "scalar", f"cb{par}", cb[:, par:par + 1],
                                               conv_bt[(ct + 2) * 128:(ct + 3) * 128, :])
                else:
                    b.op(ph, "scalar", lambda e: e.copy(dtraw[:, 0:512], bank0[0:64, :]))
                    t_evp = b.op(ph, "scalar",
                                 lambda e: e.copy(dtraw[:, 512:1024], bank1[0:64, :]),
                                 inc=cAC)
                    prel(bank0, cAC, t_evp)
                    prel(bank1, cAC, t_evp)
                    t_evh = b.op(ph, "scalar", lambda e: e.copy(lnscr[:, 0:3], ps_e[:, 0:3]),
                                 inc=cAC)
                    prel(ps_e, cAC, t_evh)
                    conv_tok[ct] = t_evp

                if ct < 32:
                    b.wait(ph, "tensor", cAC, conv_tok[ct])
                    b.guard(ph, "vector", ("xstg", par))
                    last_pe = 0
                    ev_last = 0
                    for tt in range(8):
                        bk = tbanks[tt % 2]
                        pguard(bk)
                        t_pe = b.op(ph, "tensor",
                                    lambda e, _i=cvo[:, par * TMAIN + tt * 128:
                                                     par * TMAIN + (tt + 1) * 128], _bk=bk:
                                    e.transpose(_bk[:, 0:128], _i, identb[:, :]), inc=cPE)
                        last_pe = t_pe
                        b.wait(ph, "vector", cPE, t_pe)
                        ev_last = b.op(ph, "vector",
                                       lambda e, _o=xstg[:, par * TMAIN + tt * 128:
                                                         par * TMAIN + (tt + 1) * 128],
                                       _bk=bk: e.tensor_copy(_o, _bk[:, 0:128]), inc=cDV)
                        prel(bk, cDV, ev_last)
                    b.release(("cvo", par), (cPE, last_pe))
                    b.wait(ph, "sync", cDV, ev_last)
                    stg_dma[ct] = b.dma(ph, "sync", f"sxw{par}",
                                        x_tm_d.rearrange("(t p) (c q) -> p t q c",
                                                         p=128, q=128)[:, :, :, ct],
                                        xstg[:, par * TMAIN:(par + 1) * TMAIN]
                                        .rearrange("p (t q) -> p t q", q=128))
                    b.release(("xstg", par), (b.q(f"sxw{par}"), stg_dma[ct]))
            x_tm_fin = (b.q("sxw0").n, b.q("sxw1").n)

            b.wait(ph, "scalar", cAC, conv_tok[34])
            b.op(ph, "scalar", lambda e: e.activation(dtscr[:, :], dtraw[:, :], AF.Abs,
                                                      bias=dtb_c[:, :]))
            b.op(ph, "scalar", lambda e: e.activation(dtscr[:, :], dtscr[:, :], AF.Exp,
                                                      scale=-1.0))
            t_l = b.op(ph, "scalar", lambda e: e.activation(dtscr[:, :], dtscr[:, :],
                                                            AF.Ln, bias=1.0), inc=cAC)
            b.wait(ph, "vector", cAC, t_l)
            b.op(ph, "vector", lambda e: e.tensor_scalar(dt_fm[:, :], dtraw[:, :],
                                                         dtb_c[:, :], 0.0, ALU.add, ALU.max))
            b.op(ph, "vector", lambda e: e.tensor_tensor(dt_fm[:, :], dt_fm[:, :],
                                                         dtscr[:, :], ALU.add))
            dt_done = b.op(ph, "vector", lambda e: e.tensor_scalar(dt_fm[:, :], dt_fm[:, :],
                                                                   100.0, None, ALU.min),
                           inc=cDV)
            ph.emit(block)

        _es1.close()
        b.fence(ph, [cPE, cDV, cAC, cPO], ["sxw0", "sxw1"])
        _es3 = contextlib.ExitStack()
        with _es3:
            qtl = _es3.enter_context(nc.sbuf_tensor("qtl", [128, 2 * D], F32))
            qto = _es3.enter_context(nc.sbuf_tensor("qto", [128, 2 * D], BF16))
            qtn = _es3.enter_context(nc.sbuf_tensor("qtn", [128, D], BF16))
            qac = _es3.enter_context(nc.sbuf_tensor("qac", [128, 26], F32))
            qsc = _es3.enter_context(nc.sbuf_tensor("qsc", [128, 4], F32))
            cc1s = _es3.enter_context(nc.sbuf_tensor("cc1s", [4, 1], F32))

            tiles = []
            for (wsrc, tdst, npc, ncc, wi) in (
                    (wu_s, tau_u_s, 2, 4, 0), (wg_s, tau_g_s, 2, 4, 1),
                    (wd_s, tau_d_s, 8, 1, 2)):
                for pc in range(npc):
                    for cc in range(ncc):
                        tiles.append((wsrc, tdst, pc, cc, wi))

            def qld(i):
                wsrc, tdst, pc, cc, wi = tiles[i]
                par = i % 2
                return b.dma(ph, "scalar", f"ql{par}", qtl[:, par * D:(par + 1) * D],
                             wsrc[pc * 128:(pc + 1) * 128, cc * D:(cc + 1) * D])

            ld_tok = [0] * len(tiles)
            abs_tok = [0] * len(tiles)
            ld_tok[0] = qld(0)
            ld_tok[1] = qld(1)
            for i in range(len(tiles)):
                par = i % 2
                b.wait(ph, "scalar", b.q(f"ql{par}"), ld_tok[i])
                abs_tok[i] = b.op(ph, "scalar",
                                  lambda e, _i=qtl[:, par * D:(par + 1) * D],
                                  _a=qac[:, i:i + 1]:
                                  e.activation(_i, _i, AF.Abs, accum_out=_a), inc=cAC)
                if i + 2 < len(tiles):
                    b.wait(ph, "scalar", cAC, abs_tok[i])
                    ld_tok[i + 2] = qld(i + 2)
            b.wait(ph, "vector", cAC, abs_tok[-1])
            for wi in range(3):
                b.op(ph, "vector", lambda e, _o=qsc[:, wi:wi + 1],
                     _i=qac[:, wi * 8:wi * 8 + 8]: e.reduce_sum(_o, _i, axis=AX.X))
            t_q = b.op(ph, "vector", lambda e: e.memset(qsc[:, 3:4], 0.0), inc=cDV)
            b.wait(ph, "tensor", cDV, t_q)
            pguard(ps_bc)
            t_pr = b.op(ph, "tensor", lambda e: e.matmul(ps_bc[0:4, 0:1], qsc[:, :],
                                                         onesf[:, 0:1], start=True,
                                                         stop=True), inc=cPE)
            b.wait(ph, "vector", cPE, t_pr)
            t_ev = b.op(ph, "vector", lambda e: e.tensor_copy(cc1s[:, :], ps_bc[0:4, 0:1]),
                        inc=cDV)
            prel(ps_bc, cDV, t_ev)
            b.wait(ph, "gpsimd", cDV, t_ev)
            t_st = b.dma(ph, "gpsimd", "ccs", cc1_i.ap()[:, :], cc1s[:, :])
            b.wait(ph, "gpsimd", b.q("ccs"), t_st)
            b.op_cc(ph, lambda e: e.collective_compute(
                "AllReduce", ALU.add, replica_groups=[list(range(NCORE))],
                ins=[cc1_i.ap().opt()], outs=[cc1_o.ap().opt()]), cc1)
            b.wait(ph, "gpsimd", cc1, 1)
            t_rb = b.dma(ph, "gpsimd", "ccs", sc4[:, :], cc1_o.ap()[:, :])
            b.wait(ph, "vector", b.q("ccs"), t_rb)
            t_sc = b.op(ph, "vector", lambda e: e.tensor_scalar(sc4[:, :], sc4[:, :],
                                                                1.0 / WSZ, EPS_Q, ALU.mult,
                                                                ALU.add), inc=cDV)
            b.wait(ph, "scalar", cDV, t_sc)
            t_sw = b.dma(ph, "scalar", "scr", scr_d.ap()[:, :], sc4[:, :])
            b.wait(ph, "scalar", b.q("scr"), t_sw)
            t_scr = b.dma(ph, "scalar", "scr", scrow[:, :],
                          scr_d.ap().rearrange("a o -> o a"))
            b.wait(ph, "tensor", b.q("scr"), t_scr)
            pguard(ps_bc)
            t_bc2 = b.op(ph, "tensor", lambda e: e.matmul(ps_bc[:, 0:4], onesf[0:1, :],
                                                          scrow[0:1, :], start=True,
                                                          stop=True), inc=cPE)
            b.wait(ph, "vector", cPE, t_bc2)
            t_ev = b.op(ph, "vector", lambda e: e.tensor_copy(scbc[:, :], ps_bc[:, 0:4]),
                        inc=cDV)
            prel(ps_bc, cDV, t_ev)
            b.op(ph, "vector", lambda e:
                 e.tensor_tensor(sgsd_c[:, :], scbc[:, 1:2], scbc[:, 2:3], ALU.mult))
            scales_done = b.op(ph, "vector", lambda e:
                               e.reciprocal(qsc[:, 0:3], scbc[:, 0:3]), inc=cDV)
            st_tok2 = [0] * len(tiles)
            ld_tok[0] = qld(0)
            ld_tok[1] = qld(1)
            for i, (wsrc, tdst, pc, cc, wi) in enumerate(tiles):
                par = i % 2
                scol = qsc[:, wi:wi + 1]
                qo = qto[:, par * D:(par + 1) * D]
                b.wait(ph, "vector", b.q(f"ql{par}"), ld_tok[i])
                b.wait(ph, "vector", cDV, scales_done)
                if i >= 2:
                    b.wait(ph, "vector", b.q(f"ts{par}"), st_tok2[i - 2])
                b.op(ph, "vector", lambda e, _i=qtl[:, par * D:(par + 1) * D], _s=scol,
                     _o=qo: e.tensor_scalar(_o, _i, _s, 0.5, ALU.mult, ALU.is_gt))
                b.op(ph, "vector", lambda e, _i=qtl[:, par * D:(par + 1) * D], _s=scol:
                     e.tensor_scalar(qtn[:, :], _i, _s, -0.5, ALU.mult, ALU.is_lt))
                tau_tok = b.op(ph, "vector", lambda e, _o=qo:
                               e.tensor_tensor(_o, _o, qtn[:, :], ALU.subtract), inc=cDV)
                if i + 2 < len(tiles):
                    b.wait(ph, "scalar", cDV, tau_tok)
                    ld_tok[i + 2] = qld(i + 2)
                b.wait(ph, "sync", cDV, tau_tok)
                st_tok2[i] = b.dma(ph, "sync", f"ts{par}",
                                   tdst.ap()[pc * 128:(pc + 1) * 128, cc * D:(cc + 1) * D],
                                   qo)
            b.wait(ph, "gpsimd", b.q("ts0"), b.q("ts0").n)
            b.wait(ph, "gpsimd", b.q("ts1"), b.q("ts1").n)
            b.op_cc(ph, lambda e: e.collective_compute(
                "AllGather", ALU.bypass, replica_groups=[list(range(NCORE))],
                ins=[tau_u_s.ap().opt()], outs=[tau_u.ap().opt()]), cc2)
            b.op_cc(ph, lambda e: e.collective_compute(
                "AllGather", ALU.bypass, replica_groups=[list(range(NCORE))],
                ins=[tau_g_s.ap().opt()], outs=[tau_g.ap().opt()]), cc2b)
            b.op_cc(ph, lambda e: e.collective_compute(
                "AllGather", ALU.bypass, replica_groups=[list(range(NCORE))],
                ins=[tau_d_s.ap().opt()], outs=[tau_d.ap().opt()]), cc2c)
            ph.emit(block)

        b.fence(ph, [cPE, cDV, cAC, cPO], ["ts0", "ts1"])
        _es4 = contextlib.ExitStack()
        with _es4:
            nda = _es4.enter_context(nc.sbuf_tensor("nda", [HH, CHK], F32))
            ndat = _es4.enter_context(nc.sbuf_tensor("ndat", [128, 2 * HH], F32))
            decf = _es4.enter_context(nc.sbuf_tensor("decf", [HH, CHK], F32))
            ddf = _es4.enter_context(nc.sbuf_tensor("ddf", [HH, CHK], F32))
            ncol = _es4.enter_context(nc.sbuf_tensor("ncol", [HH, 8], F32))
            xch = _es4.enter_context(nc.sbuf_tensor("xch", [128, 2 * INTER], BF16))
            xdd = _es4.enter_context(nc.sbuf_tensor("xdd", [128, 2 * INTER], BF16))
            stile = _es4.enter_context(nc.sbuf_tensor("stile", [128, INTER], BF16))
            run = _es4.enter_context(nc.sbuf_tensor("run", [128, INTER], F32))
            zrow = _es4.enter_context(nc.sbuf_tensor("zrow", [1, INTER], F32))

            b.wait(ph, "vector", cDV, t_consts)
            run_tok = b.op(ph, "vector", lambda e: e.memset(run[:, :], 0.0), inc=cDV)
            b.op(ph, "vector", lambda e: e.memset(lcrow[:, :], 0.0))
            t_zr = b.op(ph, "vector", lambda e: e.memset(zrow[:, :], 0.0), inc=cDV)
            b.wait(ph, "sync", cDV, t_zr)
            b.dma(ph, "sync", "sti", st_i.ap()[128:129, HH:INTER], zrow[0:1, HH:INTER])
            s_dma_toks = []
            xdd_tok = 0

            def fp32_T(src, dst_psum_region, inc_sem=None):
                kdim = src.shape[0]
                return b.op(ph, "tensor", lambda e, _s=src, _o=dst_psum_region, _k=kdim:
                            e.transpose(_o, _s, identf[0:_k, 0:_k]), inc=inc_sem)

            for c in range(NCHU):
                b.wait(ph, "scalar", b.q("sxw0"), x_tm_fin[0])
                b.wait(ph, "scalar", b.q("sxw1"), x_tm_fin[1])
                b.wait(ph, "scalar", cDV, xdd_tok)
                xch_tok = [0, 0]
                for t in range(2):
                    xch_tok[t] = b.dma(ph, "scalar", "xr3", xch[:, t * INTER:(t + 1) * INTER],
                                       x_tm_d[(c * 2 + t) * 128:(c * 2 + t + 1) * 128, :])
                b.wait(ph, "vector", cDV, dt_done)
                b.wait(ph, "vector", cAC, t_expa)
                t_nda = b.op(ph, "vector", lambda e, _c=c:
                             e.tensor_scalar(nda[:, :], dt_fm[:, _c * CHK:(_c + 1) * CHK],
                                             expa_c[:, :], None, ALU.mult), inc=cDV)
                b.wait(ph, "tensor", cDV, t_nda)
                for t in range(2):
                    pguard(ps_bc)
                    t1 = fp32_T(nda[:, t * 128:(t + 1) * 128], ps_bc[:, 0:64], cPE)
                    pguard(ps_e)
                    t2 = fp32_T(dt_fm[:, c * CHK + t * 128:c * CHK + (t + 1) * 128],
                                ps_e[:, 0:64], cPE)
                    b.wait(ph, "vector", cPE, t1)
                    t_e1 = b.op(ph, "vector", lambda e, _t=t:
                                e.tensor_copy(ndat[:, _t * 64:(_t + 1) * 64], ps_bc[:, 0:64]),
                                inc=cDV)
                    prel(ps_bc, cDV, t_e1)
                    b.wait(ph, "vector", cPE, t2)
                    t_e2 = b.op(ph, "vector", lambda e, _t=t, _c=c:
                                e.tensor_copy(dt_tm[:, (_c * 2 + _t) * HH:(_c * 2 + _t + 1) * HH],
                                              ps_e[:, 0:64]), inc=cDV)
                    prel(ps_e, cDV, t_e2)
                b.wait(ph, "tensor", cDV, t_e1)
                pguard(ps_e)
                b.op(ph, "tensor", lambda e: e.matmul(ps_e[0:HH, 0:CHK], ndat[:, 0:HH],
                                                      ta[:, :], start=True, stop=False))
                b.op(ph, "tensor", lambda e: e.matmul(ps_e[0:HH, 0:CHK], ndat[:, HH:128],
                                                      tb_c[:, :], start=False, stop=True))
                b.op(ph, "tensor", lambda e: e.matmul(ps_e[:, 256:320], triu[:, :],
                                                      ndat[:, 0:HH], start=True, stop=True))
                b.op(ph, "tensor", lambda e: e.matmul(ps_e[:, 320:384], onesf[:, :],
                                                      ndat[:, 0:HH], start=True, stop=False))
                t_cs = b.op(ph, "tensor", lambda e: e.matmul(ps_e[:, 320:384], triu[:, :],
                                                             ndat[:, HH:128], start=False,
                                                             stop=True), inc=cPE)
                b.wait(ph, "vector", cPE, t_cs)
                t_csfm = b.op(ph, "vector", lambda e, _c=c:
                              e.tensor_copy(csneg_fm[:, _c * CHK:(_c + 1) * CHK],
                                            ps_e[0:HH, 0:CHK]), inc=cDV)
                b.wait(ph, "sync", cDV, t_csfm)
                b.dma(ph, "sync", "csf", csf_d.ap()[c * HH:(c + 1) * HH, :],
                      csneg_fm[:, c * CHK:(c + 1) * CHK])
                t_cst = b.op(ph, "vector", lambda e, _c=c:
                             e.tensor_copy(csneg_tm[:, _c * 2 * HH:(_c + 1) * 2 * HH],
                                           ps_e[:, 256:384]), inc=cDV)
                prel(ps_e, cDV, t_cst)
                t_nc = b.op(ph, "vector", lambda e, _c=c:
                            e.tensor_scalar(ncol[:, _c:_c + 1],
                                            csneg_fm[:, _c * CHK + 255:_c * CHK + 256],
                                            -1.0, None, ALU.mult), inc=cDV)
                b.wait(ph, "sync", cDV, t_cst)
                b.guard(ph, "sync", ("ngr",))
                t_nr = b.dma(ph, "sync", "ngr", negrow[:, :],
                             csneg_tm[127:128, (c * 2 + 1) * HH:(c * 2 + 2) * HH])
                b.wait(ph, "scalar", b.q("ngr"), t_nr)
                t_dr = b.op(ph, "scalar", lambda e: e.activation(decrow[:, :], negrow[:, :],
                                                                 AF.Exp, scale=-1.0), inc=cAC)
                b.wait(ph, "vector", b.q("ngr"), t_nr)
                t_lcr = b.op(ph, "vector", lambda e:
                             e.scalar_tensor_tensor(lcrow[:, :], negrow[:, :], -1.0, lcrow[:, :],
                                                    ALU.mult, ALU.add), inc=cDV)
                b.release(("ngr",), (cAC, t_dr), (cDV, t_lcr))
                b.wait(ph, "tensor", cAC, t_dr)
                pguard(ps_bc)
                t_db = b.op(ph, "tensor", lambda e: e.matmul(ps_bc[:, 0:HH], onesf[0:1, :],
                                                             decrow[0:1, :], start=True,
                                                             stop=True), inc=cPE)
                b.wait(ph, "vector", cPE, t_db)
                t_dbe = b.op(ph, "vector", lambda e, _c=c:
                             e.tensor_copy(dec_bc[:, _c * HH:(_c + 1) * HH], ps_bc[:, 0:HH]),
                             inc=cDV)
                prel(ps_bc, cDV, t_dbe)
                b.wait(ph, "scalar", cDV, t_nc)
                t_df = b.op(ph, "scalar", lambda e, _c=c:
                            e.activation(decf[:, :], csneg_fm[:, _c * CHK:(_c + 1) * CHK],
                                         AF.Exp, bias=ncol[:, _c:_c + 1]), inc=cAC)
                b.wait(ph, "vector", cAC, t_df)
                t_dd = b.op(ph, "vector", lambda e, _c=c:
                            e.tensor_tensor(ddf[:, :], dt_fm[:, _c * CHK:(_c + 1) * CHK],
                                            decf[:, :], ALU.mult), inc=cDV)
                b.wait(ph, "tensor", cDV, t_dd)
                for t in range(2):
                    pguard(ps_bc)
                    t3 = fp32_T(ddf[:, t * 128:(t + 1) * 128], ps_bc[:, 0:64], cPE)
                    b.wait(ph, "vector", cPE, t3)
                    t_dde = b.op(ph, "vector", lambda e, _t=t, _c=c:
                                 e.tensor_copy(dtdec_tm[:, (_c * 2 + _t) * HH:
                                                        (_c * 2 + _t + 1) * HH],
                                               ps_bc[:, 0:64]), inc=cDV)
                    prel(ps_bc, cDV, t_dde)
                b.wait(ph, "tensor", cAC, conv_tok[32])
                for t in range(2):
                    bk = tbanks[t]
                    pguard(bk)
                    t_bt = b.op(ph, "tensor", lambda e, _c=c, _t=t, _bk=bk:
                                e.transpose(_bk[:, 0:128],
                                            bfm[:, _c * CHK + _t * 128:_c * CHK + (_t + 1) * 128],
                                            identb[:, :]), inc=cPE)
                    b.wait(ph, "vector", cPE, t_bt)
                    t_bte = b.op(ph, "vector", lambda e, _c=c, _t=t, _bk=bk:
                                 e.tensor_copy(btm[:, _c * 256 + _t * 128:_c * 256 + (_t + 1) * 128],
                                               _bk[:, 0:128]), inc=cDV)
                    prel(bk, cDV, t_bte)
                b.wait(ph, "vector", b.q("xr3"), xch_tok[1])
                b.wait(ph, "vector", cDV, t_dde)
                b.guard(ph, "vector", ("xdd",))
                for t in range(2):
                    for h in range(HH):
                        xdd_tok = b.op(ph, "vector", lambda e, _t=t, _h=h, _c=c:
                                       e.tensor_scalar(
                                           xdd[:, _t * INTER + _h * PH:_t * INTER + (_h + 1) * PH],
                                           xch[:, _t * INTER + _h * PH:_t * INTER + (_h + 1) * PH],
                                           dtdec_tm[:, (_c * 2 + _t) * HH + _h:
                                                    (_c * 2 + _t) * HH + _h + 1],
                                           None, ALU.mult),
                                       inc=(cDV if (t == 1 and h == HH - 1) else None))
                b.wait(ph, "tensor", cDV, xdd_tok)
                b.wait(ph, "tensor", cDV, t_bte)
                st_ev_last = 0
                for hg in range(8):
                    bank = [ps_c, ps_d][hg % 2]
                    pguard(bank)
                    t_smm = 0
                    for hh in range(8):
                        h = hg * 8 + hh
                        for t in range(2):
                            t_smm = b.op(ph, "tensor",
                                         lambda e, _h=h, _t=t, _c=c, _bk=bank, _hh=hh:
                                         e.matmul(_bk[:, _hh * PH:(_hh + 1) * PH],
                                                  btm[:, (_c * 2 + _t) * 128:(_c * 2 + _t + 1) * 128],
                                                  xdd[:, _t * INTER + _h * PH:_t * INTER + (_h + 1) * PH],
                                                  start=(_t == 0), stop=(_t == 1)),
                                         inc=(cPE if (hh == 7 and t == 1) else None))
                    b.wait(ph, "scalar", cPE, t_smm)
                    b.guard(ph, "scalar", ("stile", hg))
                    t_se = b.op(ph, "scalar", lambda e, _hg=hg, _bk=bank:
                                e.copy(stile[:, _hg * 512:(_hg + 1) * 512], _bk[:, :]),
                                inc=cAC)
                    prel(bank, cAC, t_se)
                    st_ev_last = t_se
                b.release(("xdd",), (cPE, t_smm))
                b.wait(ph, "vector", cAC, st_ev_last)
                b.wait(ph, "vector", cDV, t_dbe)
                for h in range(HH):
                    run_tok = b.op(ph, "vector", lambda e, _h=h, _c=c:
                                   e.scalar_tensor_tensor(run[:, _h * PH:(_h + 1) * PH],
                                                          run[:, _h * PH:(_h + 1) * PH],
                                                          dec_bc[:, _c * HH + _h:_c * HH + _h + 1],
                                                          stile[:, _h * PH:(_h + 1) * PH],
                                                          ALU.mult, ALU.add),
                                   inc=(cDV if h == HH - 1 else None))
                b.wait(ph, "sync", cAC, st_ev_last)
                t_sd = b.dma(ph, "sync", "ssw", s_st_d[c * 128:(c + 1) * 128, :], stile[:, :])
                s_dma_toks.append(t_sd)
                for hg in range(8):
                    b.release(("stile", hg), (b.q("ssw"), t_sd), (cDV, run_tok))
            b.wait(ph, "sync", cDV, run_tok)
            t_acc = b.dma(ph, "sync", "sti", st_i.ap()[0:128, :], run[:, :])
            b.wait(ph, "sync", cDV, cDV.n)
            t_lc2 = b.dma(ph, "sync", "sti", st_i.ap()[128:129, 0:HH], lcrow[:, :])
            b.wait(ph, "gpsimd", b.q("sti"), t_lc2)
            b.op_cc(ph, lambda e: e.collective_compute(
                "AllGather", ALU.bypass, replica_groups=[list(range(NCORE))],
                ins=[st_i.ap().opt()], outs=[st_o.ap().opt()]), cc3)
            ph.emit(block)

        b.fence(ph, [cPE, cDV, cAC, cPO], ["sti", "ssw", "csf"])
        _es5 = contextlib.ExitStack()
        with _es5:
            la8 = _es5.enter_context(nc.sbuf_tensor("la8", [8, HH], F32))
            w8 = _es5.enter_context(nc.sbuf_tensor("w8", [8, HH], F32))
            w8r = _es5.enter_context(nc.sbuf_tensor("w8r", [1, 8 * HH], F32))
            wbc = _es5.enter_context(nc.sbuf_tensor("wbc", [128, 8 * HH], F32))
            dbc = _es5.enter_context(nc.sbuf_tensor("dbc", [128, INTER], BF16))
            gac = _es5.enter_context(nc.sbuf_tensor("gac", [128, INTER], F32))
            st = _es5.enter_context(nc.sbuf_tensor("st", [128, INTER], F32))
            stb = _es5.enter_context(nc.sbuf_tensor("stb", [128, INTER], BF16))
            xch4 = _es5.enter_context(nc.sbuf_tensor("xch4", [128, 2 * INTER], BF16))
            xdt4 = _es5.enter_context(nc.sbuf_tensor("xdt4", [128, 2 * INTER], BF16))
            sch4 = _es5.enter_context(nc.sbuf_tensor("sch4", [128, INTER], BF16))
            ytl = _es5.enter_context(nc.sbuf_tensor("ytl", [128, 2 * INTER], BF16))
            gtm = _es5.enter_context(nc.sbuf_tensor("gtm", [128, 384], BF16))
            csr = _es5.enter_context(nc.sbuf_tensor("csr", [1, 2 * 2048], F32))
            seg = _es5.enter_context(nc.sbuf_tensor("seg", [128, 2 * 384], F32))
            etl = _es5.enter_context(nc.sbuf_tensor("etl", [128, 2 * 384], BF16))
            evt = _es5.enter_context(nc.sbuf_tensor("evt", [128, 2 * 256], BF16))
            gl = _es5.enter_context(nc.sbuf_tensor("gl", [128, 2 * 384], BF16))
            glv = _es5.enter_context(nc.sbuf_tensor("glv", [128, 2 * 256], BF16))

            bcast_bank = [ps_bc, ps_e]

            t_dr2 = b.dma(ph, "scalar", "d4", csr[0:1, 0:INTER], d_row)
            b.wait(ph, "tensor", b.q("d4"), t_dr2)
            dbc_last = 0
            for j in range(8):
                bk = bcast_bank[j % 2]
                pguard(bk)
                t_dj = b.op(ph, "tensor", lambda e, _j=j, _bk=bk:
                            e.matmul(_bk[:, :], onesf[0:1, :],
                                     csr[0:1, _j * 512:(_j + 1) * 512],
                                     start=True, stop=True), inc=cPE)
                b.wait(ph, "vector", cPE, t_dj)
                dbc_last = b.op(ph, "vector", lambda e, _j=j, _bk=bk:
                                e.tensor_copy(dbc[:, _j * 512:(_j + 1) * 512], _bk[:, :]),
                                inc=cDV)
                prel(bk, cDV, dbc_last)
            b.wait(ph, "scalar", cc3, 1)
            t_la = b.dma(ph, "scalar", "la", la8[:, :],
                         st_o.ap().rearrange("(r q) f -> r q f", q=129)[:, 128, 0:HH])
            b.wait(ph, "tensor", b.q("la"), t_la)
            pguard(ps_bc)
            t_ld8 = b.op(ph, "tensor", lambda e: e.matmul(ps_bc[0:8, 0:HH], selt[:, :],
                                                          la8[:, :], start=True, stop=True),
                         inc=cPE)
            b.wait(ph, "scalar", cPE, t_ld8)
            t_w8 = b.op(ph, "scalar", lambda e: e.activation(w8[:, :], ps_bc[0:8, 0:HH],
                                                             AF.Exp), inc=cAC)
            prel(ps_bc, cAC, t_w8)
            b.wait(ph, "vector", cAC, t_w8)
            t_w8m = b.op(ph, "vector", lambda e: e.tensor_scalar(w8[:, :], w8[:, :],
                                                                 pmask[:, :], None, ALU.mult),
                         inc=cDV)
            b.wait(ph, "sync", cDV, t_w8m)
            t_w8w = b.dma(ph, "sync", "w8", w8_d.ap()[:, :], w8[:, :])
            b.wait(ph, "sync", b.q("w8"), t_w8w)
            t_w8r = b.dma(ph, "sync", "w8", w8r[0:1, :],
                          w8_d.ap().rearrange("r h -> (r h)").rearrange("(o f) -> o f", o=1))
            b.wait(ph, "tensor", b.q("w8"), t_w8r)
            wbc_last = 0
            for j in range(4):
                bk = bcast_bank[j % 2]
                pguard(bk)
                t_wj = b.op(ph, "tensor", lambda e, _j=j, _bk=bk:
                            e.matmul(_bk[:, 0:128], onesf[0:1, :],
                                     w8r[0:1, _j * 128:(_j + 1) * 128], start=True,
                                     stop=True), inc=cPE)
                b.wait(ph, "vector", cPE, t_wj)
                wbc_last = b.op(ph, "vector", lambda e, _j=j, _bk=bk:
                                e.tensor_copy(wbc[:, _j * 128:(_j + 1) * 128], _bk[:, 0:128]),
                                inc=cDV)
                prel(bk, cDV, wbc_last)
            st_tok = b.op(ph, "vector", lambda e: e.memset(st[:, :], 0.0), inc=cDV)
            for i in range(NCORE):
                t_gr = b.dma(ph, "scalar", "gac", gac[:, :], st_o.ap()[i * 129:i * 129 + 128, :])
                b.wait(ph, "vector", b.q("gac"), t_gr)
                b.wait(ph, "vector", cDV, wbc_last)
                for h in range(HH):
                    st_tok = b.op(ph, "vector", lambda e, _i=i, _h=h:
                                  e.scalar_tensor_tensor(st[:, _h * PH:(_h + 1) * PH],
                                                         gac[:, _h * PH:(_h + 1) * PH],
                                                         wbc[:, _i * HH + _h:_i * HH + _h + 1],
                                                         st[:, _h * PH:(_h + 1) * PH],
                                                         ALU.mult, ALU.add),
                                  inc=(cDV if h == HH - 1 else None))
                b.wait(ph, "scalar", cDV, st_tok)

            y_dma_toks = []
            for c in range(NCHU):
                b.guard(ph, "vector", ("stb",))
                b.wait(ph, "vector", cDV, st_tok)
                t_stb = b.op(ph, "vector", lambda e: e.tensor_copy(stb[:, :], st[:, :]),
                             inc=cDV)
                b.guard(ph, "scalar", ("xch4",))
                xc_tok = [0, 0]
                for t in range(2):
                    xc_tok[t] = b.dma(ph, "scalar", "xr4", xch4[:, t * INTER:(t + 1) * INTER],
                                      x_tm_d[(c * 2 + t) * 128:(c * 2 + t + 1) * 128, :])
                b.wait(ph, "scalar", b.q("ssw"), b.q("ssw").n)
                b.guard(ph, "scalar", ("sch4",))
                sc_tok = b.dma(ph, "scalar", "ssr", sch4[:, :], s_st_d[c * 128:(c + 1) * 128, :])
                b.wait(ph, "vector", b.q("xr4"), xc_tok[1])
                b.guard(ph, "vector", ("xdt4",))
                xdt_tok = 0
                for t in range(2):
                    for h in range(HH):
                        xdt_tok = b.op(ph, "vector", lambda e, _t=t, _h=h, _c=c:
                                       e.tensor_scalar(
                                           xdt4[:, _t * INTER + _h * PH:_t * INTER + (_h + 1) * PH],
                                           xch4[:, _t * INTER + _h * PH:_t * INTER + (_h + 1) * PH],
                                           dt_tm[:, (_c * 2 + _t) * HH + _h:
                                                 (_c * 2 + _t) * HH + _h + 1],
                                           None, ALU.mult),
                                       inc=(cDV if (t == 1 and h == HH - 1) else None))
                b.wait(ph, "vector", cDV, dbc_last)
                b.guard(ph, "vector", ("ytl",))
                yi_tok = 0
                for t in range(2):
                    yi_tok = b.op(ph, "vector", lambda e, _t=t:
                                  e.tensor_tensor(ytl[:, _t * INTER:(_t + 1) * INTER],
                                                  xch4[:, _t * INTER:(_t + 1) * INTER],
                                                  dbc[:, :], ALU.mult), inc=cDV)
                b.release(("xch4",), (cDV, yi_tok))
                pguard(ps_e)
                b.wait(ph, "tensor", cAC, conv_tok[33])
                t_gmm = 0
                for sh in range(2):
                    t_gmm = b.op(ph, "tensor", lambda e, _c=c, _sh=sh:
                                 e.matmul(ps_e[:, _sh * 256:(_sh + 1) * 256],
                                          bfm[:, _c * CHK + _sh * 128:_c * CHK + (_sh + 1) * 128],
                                          cfm[:, _c * CHK:(_c + 1) * CHK],
                                          start=True, stop=True),
                                 inc=(cPE if sh == 1 else None))
                b.wait(ph, "vector", cPE, t_gmm)
                b.guard(ph, "vector", ("gtm",))
                b.op(ph, "vector", lambda e: e.tensor_tensor(gtm[:, 0:128], ps_e[:, 0:128],
                                                             triu[:, :], ALU.mult))
                b.op(ph, "vector", lambda e: e.tensor_copy(gtm[:, 128:256], ps_e[:, 128:256]))
                t_gt = b.op(ph, "vector", lambda e:
                            e.tensor_tensor(gtm[:, 256:384], ps_e[:, 384:512], triu[:, :],
                                            ALU.mult), inc=cDV)
                prel(ps_e, cDV, t_gt)
                cs_tok = [0, 0]
                t_ymm = 0
                for hg in range(8):
                    par2 = hg % 2
                    b.guard(ph, "sync", ("csr", par2))
                    b.wait(ph, "sync", b.q("csf"), b.q("csf").n)
                    cs_tok[par2] = b.dma(ph, "sync", f"cs{par2}",
                                         csr[0:1, par2 * 2048:(par2 + 1) * 2048],
                                         csf_d.ap()[c * HH + hg * 8:c * HH + (hg + 1) * 8, :]
                                         .rearrange("h l -> (h l)")
                                         .rearrange("(o f) -> o f", o=1))
                    bank0 = [ps_a, ps_c][par2]
                    bank1 = [ps_b, ps_d][par2]
                    pguard(bank0)
                    pguard(bank1)
                    last_bcm = 0
                    for hh in range(8):
                        h = hg * 8 + hh
                        par = hh % 2
                        bcb = bcast_bank[par]
                        b.wait(ph, "tensor", b.q(f"cs{par2}"), cs_tok[par2])
                        pguard(bcb)
                        t_bcm = b.op(ph, "tensor", lambda e, _hh=hh, _par2=par2, _bk=bcb:
                                     e.matmul(_bk[:, 0:256], onesf[0:1, :],
                                              csr[0:1, _par2 * 2048 + _hh * 256:
                                                  _par2 * 2048 + (_hh + 1) * 256],
                                              start=True, stop=True), inc=cPE)
                        last_bcm = t_bcm
                        b.wait(ph, "vector", cPE, t_bcm)
                        b.guard(ph, "vector", ("seg", par))
                        b.op(ph, "vector", lambda e, _par=par, _c=c, _h=h, _bk=bcb:
                             e.tensor_scalar(seg[:, _par * 384:_par * 384 + 256],
                                             _bk[:, 0:256],
                                             csneg_tm[:, (_c * 2) * HH + _h:(_c * 2) * HH + _h + 1],
                                             0.0, ALU.subtract, ALU.max))
                        t_seg = b.op(ph, "vector", lambda e, _par=par, _c=c, _h=h, _bk=bcb:
                                     e.tensor_scalar(seg[:, _par * 384 + 256:_par * 384 + 384],
                                                     _bk[:, 128:256],
                                                     csneg_tm[:, (_c * 2 + 1) * HH + _h:
                                                              (_c * 2 + 1) * HH + _h + 1],
                                                     0.0, ALU.subtract, ALU.max), inc=cDV)
                        b.wait(ph, "scalar", cDV, t_seg)
                        b.guard(ph, "scalar", ("etl", par))
                        b.op(ph, "scalar", lambda e, _par=par:
                             e.activation(etl[:, _par * 384:(_par + 1) * 384],
                                          seg[:, _par * 384:(_par + 1) * 384], AF.Exp,
                                          scale=-1.0))
                        b.guard(ph, "scalar", ("evt", par))
                        t_ev2 = b.op(ph, "scalar", lambda e, _par=par, _bk=bcb:
                                     e.activation(evt[:, _par * 256:(_par + 1) * 256],
                                                  _bk[:, 0:256], AF.Exp, scale=-1.0),
                                     inc=cAC)
                        b.release(("seg", par), (cAC, t_ev2))
                        prel(bcb, cAC, t_ev2)
                        b.wait(ph, "gpsimd", cAC, t_ev2)
                        b.wait(ph, "gpsimd", cDV, t_gt)
                        b.guard(ph, "gpsimd", ("gl", par))
                        b.op(ph, "gpsimd", lambda e, _par=par:
                             e.tensor_tensor(gl[:, _par * 384:(_par + 1) * 384],
                                             etl[:, _par * 384:(_par + 1) * 384],
                                             gtm[:, :], ALU.mult))
                        t_glv = b.op(ph, "gpsimd", lambda e, _par=par, _c=c:
                                     e.tensor_tensor(glv[:, _par * 256:(_par + 1) * 256],
                                                     evt[:, _par * 256:(_par + 1) * 256],
                                                     cfm[:, _c * CHK:(_c + 1) * CHK],
                                                     ALU.mult), inc=cPO)
                        b.release(("etl", par), (cPO, t_glv))
                        b.release(("evt", par), (cPO, t_glv))
                        b.wait(ph, "tensor", cPO, t_glv)
                        b.wait(ph, "tensor", cDV, xdt_tok)
                        b.wait(ph, "tensor", cDV, t_stb)
                        b.op(ph, "tensor", lambda e, _par=par, _h=h, _bk=bank0, _hh=hh:
                             e.matmul(_bk[:, _hh * PH:(_hh + 1) * PH],
                                      gl[:, _par * 384:_par * 384 + 128],
                                      xdt4[:, _h * PH:(_h + 1) * PH],
                                      start=True, stop=False))
                        b.op(ph, "tensor", lambda e, _par=par, _h=h, _bk=bank0, _hh=hh:
                             e.matmul(_bk[:, _hh * PH:(_hh + 1) * PH],
                                      glv[:, _par * 256:_par * 256 + 128],
                                      stb[:, _h * PH:(_h + 1) * PH], start=False,
                                      stop=True))
                        b.op(ph, "tensor", lambda e, _par=par, _h=h, _bk=bank1, _hh=hh:
                             e.matmul(_bk[:, _hh * PH:(_hh + 1) * PH],
                                      gl[:, _par * 384 + 128:_par * 384 + 256],
                                      xdt4[:, _h * PH:(_h + 1) * PH],
                                      start=True, stop=False))
                        b.op(ph, "tensor", lambda e, _par=par, _h=h, _bk=bank1, _hh=hh:
                             e.matmul(_bk[:, _hh * PH:(_hh + 1) * PH],
                                      gl[:, _par * 384 + 256:_par * 384 + 384],
                                      xdt4[:, INTER + _h * PH:INTER + (_h + 1) * PH],
                                      start=False, stop=False))
                        t_ymm = b.op(ph, "tensor", lambda e, _par=par, _h=h, _bk=bank1, _hh=hh:
                                     e.matmul(_bk[:, _hh * PH:(_hh + 1) * PH],
                                              glv[:, _par * 256 + 128:_par * 256 + 256],
                                              stb[:, _h * PH:(_h + 1) * PH], start=False,
                                              stop=True), inc=cPE)
                        b.release(("gl", par), (cPE, t_ymm))
                    b.release(("csr", par2), (cPE, last_bcm))
                    b.wait(ph, "vector", cPE, t_ymm)
                    b.wait(ph, "vector", cDV, yi_tok)
                    b.op(ph, "vector", lambda e, _hg=hg, _bk=bank0:
                         e.tensor_tensor(ytl[:, _hg * 512:(_hg + 1) * 512],
                                         ytl[:, _hg * 512:(_hg + 1) * 512], _bk[:, :],
                                         ALU.add))
                    t_ye = b.op(ph, "vector", lambda e, _hg=hg, _bk=bank1:
                                e.tensor_tensor(ytl[:, INTER + _hg * 512:INTER + (_hg + 1) * 512],
                                                ytl[:, INTER + _hg * 512:INTER + (_hg + 1) * 512],
                                                _bk[:, :], ALU.add), inc=cDV)
                    prel(bank0, cDV, t_ye)
                    prel(bank1, cDV, t_ye)
                b.release(("stb",), (cPE, t_ymm))
                b.release(("xdt4",), (cPE, t_ymm))
                b.release(("gtm",), (cPO, t_glv))
                b.wait(ph, "gpsimd", cDV, cDV.n)
                ty0 = b.dma(ph, "gpsimd", "syw", y_d[(c * 2) * 128:(c * 2 + 1) * 128, :],
                            ytl[:, 0:INTER])
                ty1 = b.dma(ph, "gpsimd", "syw", y_d[(c * 2 + 1) * 128:(c * 2 + 2) * 128, :],
                            ytl[:, INTER:2 * INTER])
                y_dma_toks += [ty0, ty1]
                b.release(("ytl",), (b.q("syw"), ty1))
                if c < NCHU - 1:
                    b.wait(ph, "vector", b.q("ssr"), sc_tok)
                    b.wait(ph, "vector", cPE, t_ymm)
                    for h in range(HH):
                        st_tok = b.op(ph, "vector", lambda e, _h=h, _c=c:
                                      e.scalar_tensor_tensor(st[:, _h * PH:(_h + 1) * PH],
                                                             st[:, _h * PH:(_h + 1) * PH],
                                                             dec_bc[:, _c * HH + _h:
                                                                    _c * HH + _h + 1],
                                                             sch4[:, _h * PH:(_h + 1) * PH],
                                                             ALU.mult, ALU.add),
                                      inc=(cDV if h == HH - 1 else None))
                    b.release(("sch4",), (cDV, st_tok))
            y_done = y_dma_toks[-1]
            ph.emit(block)

        b.fence(ph, [cPE, cDV, cAC, cPO], ["syw"])
        _es6 = contextlib.ExitStack()
        with _es6:
            wz = _es6.enter_context(nc.sbuf_tensor("wz", [128, 2 * KD * 512], BF16))
            zst = _es6.enter_context(nc.sbuf_tensor("zst", [128, 2 * 512], BF16))

            def wz_dma(zc):
                par = zc % 2
                return b.dma(ph, "gpsimd", f"wz{par}",
                             wz[:, par * KD * 512:(par + 1) * KD * 512]
                             .rearrange("p (k c) -> p k c", k=KD),
                             w_in_r[:, :, zc * 512:(zc + 1) * 512])

            wz_tok = [0] * 8
            wz_tok[0] = wz_dma(0)
            wz_tok[1] = wz_dma(1)
            z_dma = []
            for zc in range(8):
                par = zc % 2
                b.wait(ph, "tensor", b.q(f"wz{par}"), wz_tok[zc])
                t_zl = 0
                for tt in range(8):
                    bank = [ps_a, ps_b][tt % 2]
                    pguard(bank)
                    for k in range(KD):
                        t_zl = b.op(ph, "tensor", lambda e, _k=k, _tt=tt, _par=par, _bk=bank:
                                    e.matmul(_bk[:, :],
                                             xn_fm[:, _k * TMAIN + _tt * 128:
                                                   _k * TMAIN + (_tt + 1) * 128],
                                             wz[:, _par * KD * 512 + _k * 512:
                                                _par * KD * 512 + (_k + 1) * 512],
                                             start=(_k == 0), stop=(_k == KD - 1)),
                                    inc=(cPE if k == KD - 1 else None))
                    b.wait(ph, "vector", cPE, t_zl)
                    b.guard(ph, "vector", ("zst", tt % 2))
                    t_ze = b.op(ph, "vector", lambda e, _tt=tt, _bk=bank:
                                e.tensor_copy(zst[:, (_tt % 2) * 512:(_tt % 2 + 1) * 512],
                                              _bk[:, :]), inc=cDV)
                    prel(bank, cDV, t_ze)
                    b.wait(ph, "sync", cDV, t_ze)
                    t_zd = b.dma(ph, "sync", f"szw{tt % 2}",
                                 z_d[tt * 128:(tt + 1) * 128, zc * 512:(zc + 1) * 512],
                                 zst[:, (tt % 2) * 512:(tt % 2 + 1) * 512])
                    z_dma.append(t_zd)
                    b.release(("zst", tt % 2), (b.q(f"szw{tt % 2}"), t_zd))
                if zc + 2 < 8:
                    b.wait(ph, "gpsimd", cPE, t_zl)
                    wz_tok[zc + 2] = wz_dma(zc + 2)
            z_fin = (b.q("szw0").n, b.q("szw1").n)
            ph.emit(block)

        b.fence(ph, [cPE, cDV, cAC, cPO], ["szw0", "szw1"])
        _es7 = contextlib.ExitStack()
        with _es7:
            ztl = _es7.enter_context(nc.sbuf_tensor("ztl", [128, INTER], F32))
            ytl6 = _es7.enter_context(nc.sbuf_tensor("ytl6", [128, INTER], F32))
            sil = _es7.enter_context(nc.sbuf_tensor("sil", [128, INTER], F32))
            nwb = _es7.enter_context(nc.sbuf_tensor("nwb", [128, INTER], BF16))
            nc6 = _es7.enter_context(nc.sbuf_tensor("nc6", [128, 8], F32))
            yn = _es7.enter_context(nc.sbuf_tensor("yn", [128, INTER], BF16))
            ynf = _es7.enter_context(nc.sbuf_tensor("ynf", [128, 32 * 512], BF16))
            wo = _es7.enter_context(nc.sbuf_tensor("wo", [128, 2 * 8 * 512], BF16))
            stg6 = _es7.enter_context(nc.sbuf_tensor("stg6", [128, 2048], BF16))

            t_ms = b.op(ph, "vector", lambda e: e.memset(sil[:, :], 0.0), inc=cDV)
            b.wait(ph, "scalar", cDV, t_ms)
            t_nw = b.dma(ph, "scalar", "nw", sil[0:1, 0:INTER], nw_row)
            b.wait(ph, "tensor", b.q("nw"), t_nw)
            nwb_last = 0
            nwb_pe = 0
            for j in range(8):
                bk = [ps_bc, ps_e][j % 2]
                pguard(bk)
                t_nj = b.op(ph, "tensor", lambda e, _j=j, _bk=bk:
                            e.matmul(_bk[:, :], onesf[0:1, :],
                                     sil[0:1, _j * 512:(_j + 1) * 512], start=True,
                                     stop=True), inc=cPE)
                nwb_pe = t_nj
                b.wait(ph, "vector", cPE, t_nj)
                nwb_last = b.op(ph, "vector", lambda e, _j=j, _bk=bk:
                                e.tensor_copy(nwb[:, _j * 512:(_j + 1) * 512], _bk[:, :]),
                                inc=cDV)
                prel(bk, cDV, nwb_last)

            w_out_r = w_out.rearrange("(k p) c -> p k c", p=128)
            s_dma6 = []
            for grp in range(2):
                tr_done = 0
                for ti in range(4):
                    tt = grp * 4 + ti
                    b.wait(ph, "gpsimd", b.q("szw0"), z_fin[0])
                    b.wait(ph, "gpsimd", b.q("szw1"), z_fin[1])
                    b.wait(ph, "gpsimd", b.q("syw"), y_done)
                    b.guard(ph, "gpsimd", ("ztl",))
                    t_z = b.dma(ph, "gpsimd", "zr6", ztl[:, :], z_d[tt * 128:(tt + 1) * 128, :])
                    b.guard(ph, "gpsimd", ("ytl6",))
                    t_y = b.dma(ph, "gpsimd", "yr6", ytl6[:, :], y_d[tt * 128:(tt + 1) * 128, :])
                    b.wait(ph, "scalar", b.q("zr6"), t_z)
                    b.wait(ph, "scalar", cPE, nwb_pe)
                    b.guard(ph, "scalar", ("sil",))
                    t_sil = b.op(ph, "scalar", lambda e: e.activation(sil[:, :], ztl[:, :],
                                                                      AF.Silu), inc=cAC)
                    b.release(("ztl",), (cAC, t_sil))
                    b.wait(ph, "vector", cAC, t_sil)
                    b.wait(ph, "vector", b.q("yr6"), t_y)
                    t_yf = b.op(ph, "vector", lambda e: e.tensor_tensor(ytl6[:, :],
                                                                        ytl6[:, :],
                                                                        sil[:, :], ALU.mult),
                                inc=cDV)
                    b.wait(ph, "scalar", cDV, t_yf)
                    t_sq6 = b.op(ph, "scalar", lambda e:
                                 e.activation(sil[:, :], ytl6[:, :], AF.Square,
                                              accum_out=nc6[:, 0:1]), inc=cAC)
                    b.release(("sil",), (cAC, t_sq6))
                    b.wait(ph, "vector", cAC, t_sq6)
                    t_v6 = b.op(ph, "vector", lambda e:
                                e.tensor_scalar(nc6[:, 1:2], nc6[:, 0:1], 1.0 / INTER,
                                                None, ALU.mult), inc=cDV)
                    b.wait(ph, "scalar", cDV, t_v6)
                    b.op(ph, "scalar", lambda e: e.activation(nc6[:, 2:3], nc6[:, 1:2],
                                                              AF.Ln, bias=EPS_RMS))
                    t_rs6 = b.op(ph, "scalar", lambda e: e.activation(nc6[:, 3:4],
                                                                      nc6[:, 2:3], AF.Exp,
                                                                      scale=-0.5), inc=cAC)
                    b.wait(ph, "vector", cAC, t_rs6)
                    b.wait(ph, "vector", cDV, nwb_last)
                    b.guard(ph, "vector", ("yn",))
                    t_yn = b.op(ph, "vector", lambda e:
                                e.scalar_tensor_tensor(yn[:, :], ytl6[:, :], nc6[:, 3:4],
                                                       nwb[:, :], ALU.mult, ALU.mult),
                                inc=cDV)
                    b.release(("ytl6",), (cDV, t_yn))
                    b.wait(ph, "tensor", cDV, t_yn)
                    b.guard(ph, "tensor", ("ynf",))
                    last_pe = 0
                    for k in range(32):
                        bk = tbanks[k % 2]
                        pguard(bk)
                        t_pe6 = b.op(ph, "tensor", lambda e, _k=k, _bk=bk:
                                     e.transpose(_bk[:, 0:128], yn[:, _k * 128:(_k + 1) * 128],
                                                 identb[:, :]), inc=cPE)
                        last_pe = t_pe6
                        b.wait(ph, "vector", cPE, t_pe6)
                        t_ev6 = b.op(ph, "vector", lambda e, _k=k, _ti=ti, _bk=bk:
                                     e.tensor_copy(ynf[:, _k * 512 + _ti * 128:
                                                       _k * 512 + (_ti + 1) * 128],
                                                   _bk[:, 0:128]), inc=cDV)
                        prel(bk, cDV, t_ev6)
                        tr_done = t_ev6
                    b.release(("yn",), (cPE, last_pe))
                b.wait(ph, "tensor", cDV, tr_done)
                banks6 = [ps_a, ps_b, ps_c, ps_d]
                order = [(dtile, kg) for dtile in range(4) for kg in range(4)]

                def wo_dma(idx):
                    dtile, kg = order[idx]
                    par = idx % 2
                    return b.dma(ph, "gpsimd", f"wo{par}",
                                 wo[:, par * 8 * 512:(par + 1) * 8 * 512]
                                 .rearrange("p (k c) -> p k c", k=8),
                                 w_out_r[:, kg * 8:(kg + 1) * 8,
                                         dtile * 512:(dtile + 1) * 512])

                wo_tok = {0: wo_dma(0), 1: wo_dma(1)}
                grp_tok = {}
                m2_last = 0
                for idx, (dtile, kg) in enumerate(order):
                    par = idx % 2
                    if kg == 0:
                        for p in banks6:
                            pguard(p)
                    b.wait(ph, "tensor", b.q(f"wo{par}"), wo_tok[idx])
                    for kk in range(8):
                        k = kg * 8 + kk
                        for ti in range(4):
                            m2_last = b.op(ph, "tensor",
                                           lambda e, _k=k, _kk=kk, _ti=ti, _par=par,
                                           _bk=banks6[ti], _st=(k == 0), _sp=(k == 31):
                                           e.matmul(_bk[:, :],
                                                    ynf[:, _k * 512 + _ti * 128:
                                                        _k * 512 + (_ti + 1) * 128],
                                                    wo[:, _par * 8 * 512 + _kk * 512:
                                                       _par * 8 * 512 + (_kk + 1) * 512],
                                                    start=_st, stop=_sp),
                                           inc=(cPE if (kk == 7 and ti == 3) else None))
                    grp_tok[idx] = m2_last
                    if idx + 2 < len(order):
                        b.wait(ph, "gpsimd", cPE, m2_last)
                        wo_tok[idx + 2] = wo_dma(idx + 2)
                    if kg == 3:
                        b.wait(ph, "vector", cPE, m2_last)
                        b.guard(ph, "vector", ("stg6",))
                        t_se6 = 0
                        for ti in range(4):
                            t_se6 = b.op(ph, "vector", lambda e, _ti=ti, _bk=banks6[ti]:
                                         e.tensor_copy(stg6[:, _ti * 512:(_ti + 1) * 512],
                                                       _bk[:, :]), inc=cDV)
                            prel(banks6[ti], cDV, t_se6)
                        b.wait(ph, "sync", cDV, t_se6)
                        t_s6 = b.dma(ph, "sync", "ssd",
                                     s_d.rearrange("(u p) q -> p u q", p=128)
                                     [:, grp * 4:(grp + 1) * 4, dtile * 512:(dtile + 1) * 512],
                                     stg6[:, :].rearrange("p (u q) -> p u q", u=4))
                        s_dma6.append(t_s6)
                        b.release(("stg6",), (b.q("ssd"), t_s6))
                b.release(("ynf",), (cPE, m2_last))
            s_done6 = s_dma6[-1]
            ph.emit(block)

        b.fence(ph, [cPE, cDV, cAC, cPO], ["ssd"])
        _es8 = contextlib.ExitStack()
        with _es8:
            tut = _es8.enter_context(nc.sbuf_tensor("tut", [128, 2 * KD * 128], BF16))
            tgt = _es8.enter_context(nc.sbuf_tensor("tgt", [128, 2 * KD * 128], BF16))
            hid = _es8.enter_context(nc.sbuf_tensor("hid", [128, 64 * 512], BF16))
            ups = _es8.enter_context(nc.sbuf_tensor("ups", [128, 2 * 512], F32))
            tdt = _es8.enter_context(nc.sbuf_tensor("tdt", [128, 2 * 8 * 512], BF16))
            gt7 = _es8.enter_context(nc.sbuf_tensor("gt7", [128, 512], BF16))
            st7 = _es8.enter_context(nc.sbuf_tensor("st7", [128, 2 * 512], BF16))
            xr7 = _es8.enter_context(nc.sbuf_tensor("xr7", [128, 2 * 512], F32))
            ot7 = _es8.enter_context(nc.sbuf_tensor("ot7", [128, 512], F32))

            tau_u_r = tau_u.ap().rearrange("(k p) c -> p k c", p=128)
            tau_g_r = tau_g.ap().rearrange("(k p) c -> p k c", p=128)
            tau_d_r = tau_d.ap().rearrange("(g p) c -> p g c", p=128)
            b.wait(ph, "sync", cc2, 1)
            b.wait(ph, "sync", cc2b, 1)
            b.wait(ph, "sync", cc2c, 1)

            def tu_dma(eg):
                par = eg % 2
                return (b.dma(ph, "sync", f"tu{par}",
                              tut[:, par * KD * 128:(par + 1) * KD * 128]
                              .rearrange("p (k c) -> p k c", k=KD),
                              tau_u_r[:, :, eg * 128:(eg + 1) * 128]),
                        b.dma(ph, "sync", f"tg{par}",
                              tgt[:, par * KD * 128:(par + 1) * KD * 128]
                              .rearrange("p (k c) -> p k c", k=KD),
                              tau_g_r[:, :, eg * 128:(eg + 1) * 128]))

            m4_last = 0
            for slab in range(2):
                tu_tok = [None] * 64
                tu_tok[0] = tu_dma(0)
                tu_tok[1] = tu_dma(1)
                if slab == 1:
                    b.wait(ph, "vector", cPE, m4_last)
                hid_done = 0
                for eg in range(64):
                    par = eg % 2
                    ubank = [ps_a, ps_c][par]
                    gbank = [ps_b, ps_d][par]
                    b.wait(ph, "tensor", b.q(f"tu{par}"), tu_tok[eg][0])
                    b.wait(ph, "tensor", b.q(f"tg{par}"), tu_tok[eg][1])
                    pguard(ubank)
                    pguard(gbank)
                    t_ug = 0
                    for k in range(KD):
                        rhs = xn_fm[:, k * TMAIN + slab * 512:k * TMAIN + slab * 512 + 512]
                        b.op(ph, "tensor", lambda e, _k=k, _par=par, _bk=ubank, _r=rhs:
                             e.matmul(_bk[:, :],
                                      tut[:, _par * KD * 128 + _k * 128:
                                          _par * KD * 128 + (_k + 1) * 128],
                                      _r, start=(_k == 0), stop=(_k == KD - 1)))
                        t_ug = b.op(ph, "tensor", lambda e, _k=k, _par=par, _bk=gbank, _r=rhs:
                                    e.matmul(_bk[:, :],
                                             tgt[:, _par * KD * 128 + _k * 128:
                                                 _par * KD * 128 + (_k + 1) * 128],
                                             _r, start=(_k == 0), stop=(_k == KD - 1)),
                                    inc=(cPE if k == KD - 1 else None))
                    if eg + 2 < 64:
                        b.wait(ph, "sync", cPE, t_ug)
                        tu_tok[eg + 2] = tu_dma(eg + 2)
                    b.wait(ph, "scalar", cPE, t_ug)
                    b.wait(ph, "scalar", cDV, scales_done)
                    b.guard(ph, "scalar", ("ups", par))
                    t_us = b.op(ph, "scalar", lambda e, _par=par, _bk=ubank:
                                e.activation(ups[:, _par * 512:(_par + 1) * 512], _bk[:, :],
                                             AF.Silu, scale=scbc[:, 0:1]), inc=cAC)
                    prel(ubank, cAC, t_us)
                    b.wait(ph, "vector", cAC, t_us)
                    hid_done = b.op(ph, "vector", lambda e, _par=par, _eg=eg, _bk=gbank:
                                    e.tensor_tensor(hid[:, _eg * 512:(_eg + 1) * 512],
                                                    ups[:, _par * 512:(_par + 1) * 512],
                                                    _bk[:, :], ALU.mult), inc=cDV)
                    prel(gbank, cDV, hid_done)
                    b.release(("ups", par), (cDV, hid_done))
                banks7 = [ps_a, ps_b, ps_c, ps_d]
                b.wait(ph, "tensor", cDV, hid_done)

                def td_dma(dtile, egr):
                    par = egr % 2
                    return b.dma(ph, "sync", f"td{par}",
                                 tdt[:, par * 8 * 512:(par + 1) * 8 * 512]
                                 .rearrange("p (k c) -> p k c", k=8),
                                 tau_d_r[:, egr * 8:(egr + 1) * 8,
                                         dtile * 512:(dtile + 1) * 512])

                for dtile in range(4):
                    td_tok = [0] * 8
                    td_tok[0] = td_dma(dtile, 0)
                    td_tok[1] = td_dma(dtile, 1)
                    for p in banks7:
                        pguard(p)
                    for egr in range(8):
                        par = egr % 2
                        b.wait(ph, "tensor", b.q(f"td{par}"), td_tok[egr])
                        for ee_ in range(8):
                            e_idx = egr * 8 + ee_
                            for ts in range(4):
                                m4_last = b.op(ph, "tensor",
                                               lambda e, _e=e_idx, _ee=ee_, _ts=ts, _par=par,
                                               _bk=banks7[ts], _st=(e_idx == 0),
                                               _sp=(e_idx == 63):
                                               e.matmul(_bk[:, :],
                                                        hid[:, _e * 512 + _ts * 128:
                                                            _e * 512 + (_ts + 1) * 128],
                                                        tdt[:, _par * 8 * 512 + _ee * 512:
                                                            _par * 8 * 512 + (_ee + 1) * 512],
                                                        start=_st, stop=_sp),
                                               inc=(cPE if (ee_ == 7 and ts == 3) else None))
                        if egr + 2 < 8:
                            b.wait(ph, "sync", cPE, m4_last)
                            td_tok[egr + 2] = td_dma(dtile, egr + 2)
                    for ts in range(4):
                        tt_g = slab * 4 + ts
                        b.wait(ph, "scalar", b.q("ssd"), s_done6)
                        b.guard(ph, "scalar", ("st7", ts % 2))
                        t_s7 = b.dma(ph, "scalar", f"s7{ts % 2}",
                                     st7[:, (ts % 2) * 512:(ts % 2 + 1) * 512],
                                     s_d[tt_g * 128:(tt_g + 1) * 128,
                                         dtile * 512:(dtile + 1) * 512])
                        b.guard(ph, "scalar", ("xr7", ts % 2))
                        t_x7 = b.dma(ph, "scalar", f"x7{ts % 2}",
                                     xr7[:, (ts % 2) * 512:(ts % 2 + 1) * 512],
                                     x_ext[THALO + tt_g * 128:THALO + (tt_g + 1) * 128,
                                           dtile * 512:(dtile + 1) * 512])
                        b.wait(ph, "scalar", cPE, m4_last)
                        b.guard(ph, "scalar", ("gt7",))
                        t_g7 = b.op(ph, "scalar", lambda e, _bk=banks7[ts]:
                                    e.activation(gt7[:, :], _bk[:, :], AF.Silu,
                                                 scale=sgsd_c[:, 0:1]), inc=cAC)
                        prel(banks7[ts], cAC, t_g7)
                        b.wait(ph, "vector", cAC, t_g7)
                        b.wait(ph, "vector", b.q(f"s7{ts % 2}"), t_s7)
                        b.wait(ph, "vector", b.q(f"x7{ts % 2}"), t_x7)
                        b.guard(ph, "vector", ("ot7",))
                        t_m1 = b.op(ph, "vector", lambda e, _ts=ts:
                                    e.tensor_tensor(ot7[:, :], gt7[:, :],
                                                    st7[:, (_ts % 2) * 512:(_ts % 2 + 1) * 512],
                                                    ALU.mult), inc=cDV)
                        b.release(("gt7",), (cDV, t_m1))
                        b.release(("st7", ts % 2), (cDV, t_m1))
                        t_o7 = b.op(ph, "vector", lambda e, _ts=ts:
                                    e.tensor_tensor(ot7[:, :], ot7[:, :],
                                                    xr7[:, (_ts % 2) * 512:(_ts % 2 + 1) * 512],
                                                    ALU.add), inc=cDV)
                        b.release(("xr7", ts % 2), (cDV, t_o7))
                        b.wait(ph, "sync", cDV, t_o7)
                        t_od = b.dma(ph, "sync", qOUT,
                                     out_d[tt_g * 128:(tt_g + 1) * 128,
                                           dtile * 512:(dtile + 1) * 512], ot7[:, :])
                        b.release(("ot7",), (qOUT, t_od))
            b.wait(ph, "sync", qOUT, qOUT.n)
            ph.emit(block)

    stack.close()
    return nc


_NC_CACHE = None


def kernel(L0_attn_hidden_states, L0_attn_W_in, L0_attn_conv_w, L0_attn_conv_b,
           L0_attn_dt_bias, L0_attn_A_log, L0_attn_D, L0_attn_norm_weight,
           L0_attn_W_out, L0_ffn_W_up, L0_ffn_W_gate, L0_ffn_W_down):
    global _NC_CACHE
    x = np.ascontiguousarray(np.asarray(L0_attn_hidden_states, np.float32))
    w_in = np.ascontiguousarray(np.asarray(L0_attn_W_in, np.float32))
    w_out = np.ascontiguousarray(np.asarray(L0_attn_W_out, np.float32))
    wu_t = np.ascontiguousarray(np.asarray(L0_ffn_W_up, np.float32).T)
    wg_t = np.ascontiguousarray(np.asarray(L0_ffn_W_gate, np.float32).T)
    wd_t = np.ascontiguousarray(np.asarray(L0_ffn_W_down, np.float32).T)
    conv_wt = np.ascontiguousarray(np.asarray(L0_attn_conv_w, np.float32).T)
    conv_bt = np.ascontiguousarray(np.asarray(L0_attn_conv_b, np.float32).reshape(CONVD, 1))
    dt_b = np.ascontiguousarray(np.asarray(L0_attn_dt_bias, np.float32).reshape(HH, 1))
    a_log = np.ascontiguousarray(np.asarray(L0_attn_A_log, np.float32).reshape(HH, 1))
    d_row = np.ascontiguousarray(
        np.repeat(np.asarray(L0_attn_D, np.float32), PH).reshape(1, INTER))
    nw_row = np.ascontiguousarray(np.asarray(L0_attn_norm_weight, np.float32)
                                  .reshape(1, INTER))
    ident_b = np.eye(128, dtype=ml_dtypes.bfloat16)
    ident_f = np.eye(128, dtype=np.float32)
    triu = np.ascontiguousarray(np.triu(np.ones((128, 128), np.float32)))

    if _NC_CACHE is None:
        _NC_CACHE = build_nc()
    nc = _NC_CACHE

    in_maps = []
    for c in range(NCORE):
        bb, j = divmod(c, 4)
        xs = x[bb, j * TMAIN:(j + 1) * TMAIN]
        if j > 0:
            halo = x[bb, j * TMAIN - THALO:j * TMAIN]
        else:
            halo = np.zeros((THALO, D), np.float32)
        x_e = np.ascontiguousarray(np.concatenate([halo, xs], axis=0))
        selt = np.zeros((8, 8), np.float32)
        pmask = np.zeros((8, 1), np.float32)
        for i in range(8):
            ib, ij = divmod(i, 4)
            if ib == bb and ij < j:
                pmask[i, 0] = 1.0
                for m in range(8):
                    mb, mj = divmod(m, 4)
                    if mb == bb and ij < mj < j:
                        selt[m, i] = 1.0
        in_maps.append({
            "x_ext": x_e, "w_in": w_in, "w_out": w_out,
            "wu_s": np.ascontiguousarray(wu_t[c * (D // NCORE):(c + 1) * (D // NCORE)]),
            "wg_s": np.ascontiguousarray(wg_t[c * (D // NCORE):(c + 1) * (D // NCORE)]),
            "wd_s": np.ascontiguousarray(wd_t[c * (EE // NCORE):(c + 1) * (EE // NCORE)]),
            "conv_wt": conv_wt, "conv_bt": conv_bt, "dt_bias": dt_b, "a_log": a_log,
            "d_row": d_row, "nw_row": nw_row,
            "ident_b": ident_b, "ident_f": ident_f, "triu_in": triu,
            "selt_in": selt, "pmask_in": pmask,
        })

    res = run_bass_kernel_spmd(nc, in_maps, core_ids=list(range(NCORE)))
    outs = [res.results[c]["out_d"] for c in range(NCORE)]
    full = np.stack([np.concatenate(outs[0:4], axis=0),
                     np.concatenate(outs[4:8], axis=0)], axis=0)
    return full.astype(np.float32)


# version 05
# speedup vs baseline: 1.0496x; 1.0496x over previous
import contextlib
import numpy as np
import ml_dtypes
import concourse.bass as bass
import concourse.mybir as mybir
from concourse.bass_utils import run_bass_kernel_spmd

F32 = mybir.dt.float32
BF16 = mybir.dt.bfloat16
AF = mybir.ActivationFunctionType
ALU = mybir.AluOpType
AX = mybir.AxisListType

NCORE = 8
TMAIN = 1024
THALO = 128
D = 2048
KD = 16
OUT_IN = 8512
INTER = 4096
CONVD = 4352
XBCD = 4416
NCT = 35
NN = 128
HH = 64
PH = 64
CHK = 256
NCHU = 4
EE = 8192
WSZ = float(D * EE)
EPS_LN = 1e-5
EPS_RMS = 1e-8
EPS_Q = 1e-8

ENGS = ("tensor", "vector", "scalar", "gpsimd", "sync")


class Sem:
    __slots__ = ("h", "name", "n")

    def __init__(self, h, name):
        self.h = h
        self.name = name
        self.n = 0


class Ph:
    def __init__(self):
        self.q = {e: [] for e in ENGS}

    def add(self, eng, fn):
        self.q[eng].append(fn)

    def emit(self, block):
        for e in ENGS:
            if self.q[e]:
                getattr(block, e)(lambda eng, fl=tuple(self.q[e]): [f(eng) for f in fl])
        self.q = {e: [] for e in ENGS}


class B:
    def __init__(self, mk):
        self.mk = mk
        self.streams = {}
        self.lastwait = {}
        self.guards = {}

    def q(self, name):
        s = self.streams.get(name)
        if s is None:
            s = Sem(self.mk(name), name)
            self.streams[name] = s
        return s

    def wait(self, ph, eng, sem, tok):
        if not tok:
            return
        key = (eng, sem.name)
        if self.lastwait.get(key, 0) >= tok:
            return
        self.lastwait[key] = tok
        ph.add(eng, lambda e, _h=sem.h, _t=tok: e.wait_ge(_h, _t))

    SELFSEM = {"vector": "selfDV", "scalar": "selfAC", "gpsimd": "selfPO"}

    def op(self, ph, eng, fn, inc=None, k=1):
        ss = self.SELFSEM.get(eng)
        if ss is not None:
            last = getattr(self, "_lastdone", None)
            if last is None:
                last = self._lastdone = {}
            lp = last.get(eng)
            if lp is not None:
                ph.add(eng, lambda e, _h=lp[0], _p=lp[1]: e.wait_ge(_h, _p))
            if inc is not None:
                inc.n += k
                tok = inc.n
                last[eng] = (inc.h, tok)
                ph.add(eng, lambda e, _f=fn, _h=inc.h, _k=k: _f(e).then_inc(_h, _k))
                return tok
            sem = self.q(ss)
            sem.n += 1
            last[eng] = (sem.h, sem.n)
            ph.add(eng, lambda e, _f=fn, _sh=sem.h: _f(e).then_inc(_sh, 1))
            return None
        if inc is not None:
            inc.n += k
            tok = inc.n
            ph.add(eng, lambda e, _f=fn, _h=inc.h, _k=k: _f(e).then_inc(_h, _k))
            return tok
        ph.add(eng, fn)
        return None

    def fence(self, ph, comp_sems, dma_streams):
        pairs = [(s, s.n) for s in comp_sems]
        pairs += [(self.q(n), self.q(n).n) for n in dma_streams if n in self.streams]
        for eng in ENGS:
            for s, t in pairs:
                self.wait(ph, eng, s, t)

    def op_cc(self, ph, fn, sem):
        sem.n += 1
        ph.add("gpsimd", lambda e, _f=fn, _h=sem.h: _f(e).then_inc(_h))
        return sem.n

    def dma(self, ph, eng, stream, dst, src, **kw):
        sem = self.q(stream) if isinstance(stream, str) else stream
        sem.n += 16
        tok = sem.n
        ph.add(eng, lambda e, _d=dst, _s=src, _h=sem.h, _kw=kw:
               e.dma_start(out=_d, in_=_s, **_kw).then_inc(_h, 16))
        return tok

    def guard(self, ph, eng, key):
        for s, t in self.guards.get(key, ()):
            self.wait(ph, eng, s, t)

    def release(self, key, *pairs):
        self.guards[key] = [p for p in pairs if p[1]]


def build_nc():
    nc = bass.Bass("TRN2", target_bir_lowering=False)
    ph = Ph()

    ap = lambda t: t.ap()
    x_ext = ap(nc.dram_tensor("x_ext", [THALO + TMAIN, D], F32, kind="ExternalInput"))
    w_in = ap(nc.dram_tensor("w_in", [D, OUT_IN], F32, kind="ExternalInput"))
    w_out = ap(nc.dram_tensor("w_out", [INTER, D], F32, kind="ExternalInput"))
    wu_s = ap(nc.dram_tensor("wu_s", [D // NCORE, EE], F32, kind="ExternalInput"))
    wg_s = ap(nc.dram_tensor("wg_s", [D // NCORE, EE], F32, kind="ExternalInput"))
    wd_s = ap(nc.dram_tensor("wd_s", [EE // NCORE, D], F32, kind="ExternalInput"))
    conv_wt = ap(nc.dram_tensor("conv_wt", [CONVD, 4], F32, kind="ExternalInput"))
    conv_bt = ap(nc.dram_tensor("conv_bt", [CONVD, 1], F32, kind="ExternalInput"))
    dt_bias = ap(nc.dram_tensor("dt_bias", [HH, 1], F32, kind="ExternalInput"))
    a_log = ap(nc.dram_tensor("a_log", [HH, 1], F32, kind="ExternalInput"))
    d_row = ap(nc.dram_tensor("d_row", [1, INTER], F32, kind="ExternalInput"))
    nw_row = ap(nc.dram_tensor("nw_row", [1, INTER], F32, kind="ExternalInput"))
    ident_b = ap(nc.dram_tensor("ident_b", [128, 128], BF16, kind="ExternalInput"))
    ident_f = ap(nc.dram_tensor("ident_f", [128, 128], F32, kind="ExternalInput"))
    triu_in = ap(nc.dram_tensor("triu_in", [128, 128], F32, kind="ExternalInput"))
    selt_in = ap(nc.dram_tensor("selt_in", [8, 8], F32, kind="ExternalInput"))
    pmask_in = ap(nc.dram_tensor("pmask_in", [8, 1], F32, kind="ExternalInput"))
    out_d = ap(nc.dram_tensor("out_d", [TMAIN, D], F32, kind="ExternalOutput"))

    x_tm_d = ap(nc.dram_tensor("x_tm_d", [TMAIN, INTER], BF16))
    s_st_d = ap(nc.dram_tensor("s_st_d", [NCHU * 128, INTER], BF16))
    y_d = ap(nc.dram_tensor("y_d", [TMAIN, INTER], BF16))
    z_d = ap(nc.dram_tensor("z_d", [TMAIN, INTER], BF16))
    s_d = ap(nc.dram_tensor("s_d", [TMAIN, D], BF16))
    cc1_i = nc.dram_tensor("cc1_i", [4, 1], F32)
    cc1_o = nc.dram_tensor("cc1_o", [4, 1], F32)
    tau_u_s = nc.dram_tensor("tau_u_s", [D // NCORE, EE], BF16)
    tau_g_s = nc.dram_tensor("tau_g_s", [D // NCORE, EE], BF16)
    tau_d_s = nc.dram_tensor("tau_d_s", [EE // NCORE, D], BF16)
    tau_u = nc.dram_tensor("tau_u", [D, EE], BF16)
    tau_g = nc.dram_tensor("tau_g", [D, EE], BF16)
    tau_d = nc.dram_tensor("tau_d", [EE, D], BF16)
    scr_d = nc.dram_tensor("scr_d", [4, 1], F32)
    w8_d = nc.dram_tensor("w8_d", [8, HH], F32)
    csf_d = nc.dram_tensor("csf_d", [NCHU * HH, CHK], F32)
    st_i = nc.dram_tensor("st_i", [129, INTER], F32)
    st_o = nc.dram_tensor("st_o", [NCORE * 129, INTER], F32)

    stack = contextlib.ExitStack()
    b = B(lambda name: stack.enter_context(nc.semaphore(name)))
    sem = {}
    for name in ("cPE", "cDV", "cAC", "cPO",
                 "cc1", "cc2", "cc2b", "cc2c", "cc3", "qOUT"):
        sem[name] = Sem(stack.enter_context(nc.semaphore(name)), name)
    cPE, cDV, cAC, cPO = sem["cPE"], sem["cDV"], sem["cAC"], sem["cPO"]
    cc1, cc2, cc3, qOUT = sem["cc1"], sem["cc2"], sem["cc3"], sem["qOUT"]
    cc2b, cc2c = sem["cc2b"], sem["cc2c"]

    sb = lambda n, s, d: stack.enter_context(nc.sbuf_tensor(n, s, d))
    pst = lambda n, s, d: stack.enter_context(nc.psum_tensor(n, s, d))

    xn_fm = sb("xn_fm", [128, KD * TMAIN], BF16)
    xn_halo = sb("xn_halo", [128, KD * 4], BF16)
    bfm = sb("bfm", [128, TMAIN], BF16)
    cfm = sb("cfm", [128, TMAIN], BF16)
    btm = sb("btm", [128, NCHU * 2 * 128], BF16)
    dt_fm = sb("dt_fm", [HH, TMAIN], F32)
    csneg_fm = sb("csneg_fm", [HH, TMAIN], F32)
    csneg_tm = sb("csneg_tm", [128, NCHU * 2 * HH], F32)
    dt_tm = sb("dt_tm", [128, NCHU * 2 * HH], F32)
    dtdec_tm = sb("dtdec_tm", [128, NCHU * 2 * HH], F32)
    dec_bc = sb("dec_bc", [128, NCHU * HH], F32)
    identb = sb("identb", [128, 128], BF16)
    identf = sb("identf", [128, 128], F32)
    triu = sb("triu", [128, 128], F32)
    ta = sb("ta", [128, 256], F32)
    tb_c = sb("tb_c", [128, 256], F32)
    onesf = sb("onesf", [128, 128], F32)
    alog_c = sb("alog_c", [HH, 1], F32)
    expa_c = sb("expa_c", [HH, 1], F32)
    dtb_c = sb("dtb_c", [HH, 1], F32)
    selt = sb("selt", [8, 8], F32)
    pmask = sb("pmask", [8, 1], F32)
    sc4 = sb("sc4", [4, 1], F32)
    scrow = sb("scrow", [1, 4], F32)
    scbc = sb("scbc", [128, 4], F32)
    sgsd_c = sb("sgsd_c", [128, 1], F32)
    negrow = sb("negrow", [1, HH], F32)
    decrow = sb("decrow", [1, HH], F32)
    lcrow = sb("lcrow", [1, HH], F32)
    cpool = sb("cpool", [128, 4], F32)
    _cvals = [EPS_LN, 1.0, EPS_RMS]
    for _j, _v in enumerate(_cvals):
        nc.const_aps.register(F32, _v, cpool[:, _j:_j + 1])

    ps_a = pst("ps_a", [128, 512], F32)
    ps_b = pst("ps_b", [128, 512], F32)
    ps_c = pst("ps_c", [128, 512], F32)
    ps_d = pst("ps_d", [128, 512], F32)
    ps_e = pst("ps_e", [128, 512], F32)
    ps_bc = pst("ps_bc", [128, 512], F32)
    ps_tb = pst("ps_tb", [128, 1024], BF16)
    ps_tb2 = pst("ps_tb2", [128, 1024], BF16)

    block = stack.enter_context(nc.Block())

    pname = {id(ps_a): "ps_a", id(ps_b): "ps_b", id(ps_c): "ps_c", id(ps_d): "ps_d",
             id(ps_e): "ps_e", id(ps_bc): "ps_bc", id(ps_tb): "ps_tb",
             id(ps_tb2): "ps_tb2"}
    tbanks = [ps_tb, ps_tb2]

    def pguard(p, eng="tensor"):
        b.guard(ph, eng, pname[id(p)])

    def prel(p, s, t):
        b.release(pname[id(p)], (s, t))

    b.dma(ph, "sync", "cn", identb[:, :], ident_b)
    b.dma(ph, "sync", "cn", identf[:, :], ident_f)
    b.dma(ph, "sync", "cn", triu[:, :], triu_in)
    b.dma(ph, "sync", "cn", alog_c[:, :], a_log)
    b.dma(ph, "sync", "cn", dtb_c[:, :], dt_bias)
    b.dma(ph, "sync", "cn", selt[:, :], selt_in)
    t0 = b.dma(ph, "sync", "cn", pmask[:, :], pmask_in)
    b.wait(ph, "vector", b.q("cn"), t0)
    for _j, _v in enumerate(_cvals):
        b.op(ph, "vector", lambda e, _jj=_j, _vv=_v: e.memset(cpool[:, _jj:_jj + 1], _vv))
    b.op(ph, "vector", lambda e: e.memset(onesf[:, :], 1.0))
    b.op(ph, "vector", lambda e: e.tensor_copy(ta[:, 0:128], triu[:, :]))
    b.op(ph, "vector", lambda e: e.memset(ta[:, 128:256], 1.0))
    b.op(ph, "vector", lambda e: e.memset(tb_c[:, 0:128], 0.0))
    t_consts = b.op(ph, "vector", lambda e: e.tensor_copy(tb_c[:, 128:256], triu[:, :]),
                    inc=cDV)
    b.wait(ph, "scalar", b.q("cn"), t0)
    t_expa = b.op(ph, "scalar", lambda e: e.activation(expa_c[:, :], alog_c[:, :], AF.Exp),
                  inc=cAC)

    _es1 = contextlib.ExitStack()
    with _es1:
        xin = _es1.enter_context(nc.sbuf_tensor("xin", [128, 2 * D], F32))
        lnscr = _es1.enter_context(nc.sbuf_tensor("lnscr", [128, D], F32))
        lncols = _es1.enter_context(nc.sbuf_tensor("lncols", [128, 16], F32))
        xnt = _es1.enter_context(nc.sbuf_tensor("xnt", [128, 2 * D], BF16))

        x_tok = [0] * 9
        x_tok[0] = b.dma(ph, "sync", "x10", xin[:, 0:D], x_ext[0:128, :])
        x_tok[1] = b.dma(ph, "sync", "x11", xin[:, D:2 * D], x_ext[128:256, :])
        for tt in range(9):
            par = tt % 2
            xi = xin[:, par * D:(par + 1) * D]
            xo = xnt[:, par * D:(par + 1) * D]
            sm = lncols[:, 0:1]; mcol = lncols[:, 1:2]; sq = lncols[:, 2:3]
            var = lncols[:, 3:4]; msq = lncols[:, 4:5]; rstd = lncols[:, 5:6]
            nbias = lncols[:, 6:7]
            b.wait(ph, "vector", b.q(f"x1{par}"), x_tok[tt])
            b.op(ph, "vector", lambda e, _x=xi, _o=sm: e.reduce_sum(_o, _x, axis=AX.X))
            b.op(ph, "vector", lambda e, _o=mcol, _i=sm:
                 e.tensor_scalar(_o, _i, 1.0 / D, None, ALU.mult))
            b.wait(ph, "scalar", b.q(f"x1{par}"), x_tok[tt])
            t_sq = b.op(ph, "scalar", lambda e, _x=xi, _a=sq:
                        e.activation(lnscr[:, :], _x, AF.Square, accum_out=_a), inc=cAC)
            b.wait(ph, "vector", cAC, t_sq)
            b.op(ph, "vector", lambda e, _o=var, _i=sq:
                 e.tensor_scalar(_o, _i, 1.0 / D, None, ALU.mult))
            b.op(ph, "vector", lambda e, _o=msq, _m=mcol:
                 e.tensor_tensor(_o, _m, _m, ALU.mult))
            t_var = b.op(ph, "vector", lambda e, _o=var, _m=msq:
                         e.tensor_tensor(_o, _o, _m, ALU.subtract), inc=cDV)
            b.wait(ph, "scalar", cDV, t_var)
            b.op(ph, "scalar", lambda e, _o=var: e.activation(_o, _o, AF.Ln, bias=EPS_LN))
            t_rs = b.op(ph, "scalar", lambda e, _o=rstd, _i=var:
                        e.activation(_o, _i, AF.Exp, scale=-0.5), inc=cAC)
            b.wait(ph, "vector", cAC, t_rs)
            t_nb = b.op(ph, "vector", lambda e, _o=nbias, _m=mcol, _r=rstd:
                        e.scalar_tensor_tensor(_o, _m, -1.0, _r, ALU.mult, ALU.mult),
                        inc=cDV)
            b.guard(ph, "vector", ("xnt", par))
            t_xn = b.op(ph, "vector", lambda e, _x=xi, _o=xo, _r=rstd, _nb=nbias:
                        e.tensor_scalar(_o, _x, _r, _nb, ALU.mult, ALU.add), inc=cDV)
            if tt + 2 < 9:
                b.wait(ph, "sync", cDV, t_xn)
                x_tok[tt + 2] = b.dma(ph, "sync", f"x1{par}", xi,
                                      x_ext[(tt + 2) * 128:(tt + 3) * 128, :])
            b.wait(ph, "tensor", cDV, t_xn)
            last_pe = 0
            for k in range(KD):
                bk = tbanks[k % 2]
                pguard(bk)
                t_pe = b.op(ph, "tensor",
                            lambda e, _i=xo[:, k * 128:(k + 1) * 128], _bk=bk:
                            e.transpose(_bk[:, 0:128], _i, identb[:, :]), inc=cPE)
                last_pe = t_pe
                b.wait(ph, "vector", cPE, t_pe)
                if tt == 0:
                    dst = xn_halo[:, k * 4:k * 4 + 3]
                    src = bk[:, 125:128]
                else:
                    dst = xn_fm[:, k * TMAIN + (tt - 1) * 128:k * TMAIN + tt * 128]
                    src = bk[:, 0:128]
                t_ev = b.op(ph, "vector", lambda e, _o=dst, _s=src:
                            e.tensor_copy(_o, _s), inc=cDV)
                prel(bk, cDV, t_ev)
            b.release(("xnt", par), (cPE, last_pe))

        _es2 = contextlib.ExitStack()
        with _es2:
            wt1 = _es2.enter_context(nc.sbuf_tensor("wt1", [128, 2 * KD * 128], BF16))
            pre = _es2.enter_context(nc.sbuf_tensor("pre", [128, 2 * 1027], F32))
            cacc = _es2.enter_context(nc.sbuf_tensor("cacc", [128, 4 * TMAIN], F32))
            cvo = _es2.enter_context(nc.sbuf_tensor("cvo", [128, 2 * TMAIN], BF16))
            xstg = _es2.enter_context(nc.sbuf_tensor("xstg", [128, 2 * TMAIN], BF16))
            cw = _es2.enter_context(nc.sbuf_tensor("cw", [128, 2 * 4], F32))
            cb = _es2.enter_context(nc.sbuf_tensor("cb", [128, 2], F32))
            dtraw = _es2.enter_context(nc.sbuf_tensor("dtraw", [HH, TMAIN], F32))
            dtscr = _es2.enter_context(nc.sbuf_tensor("dtscr", [HH, TMAIN], F32))

            w_in_r = w_in.rearrange("(k p) c -> p k c", p=128)

            def wt_dma(ct):
                ncols = 128 if ct < 34 else 64
                par = ct % 2
                return b.dma(ph, "gpsimd", f"wm{par}",
                             wt1[:, par * KD * 128:par * KD * 128 + KD * ncols]
                             .rearrange("p (k c) -> p k c", c=ncols),
                             w_in_r[:, :, INTER + ct * 128:INTER + ct * 128 + ncols])

            wt_tok = [0] * NCT
            wt_tok[0] = wt_dma(0)
            wt_tok[1] = wt_dma(1)
            pe_mm_tok = [0] * NCT
            conv_tok = [0] * NCT
            cv_tok = [0] * NCT
            stg_dma = [0] * NCT
            cw_tok = [0] * NCT
            cb_tok = [0] * NCT
            cw_tok[0] = b.dma(ph, "scalar", "cw0", cw[:, 0:4], conv_wt[0:128, :])
            cb_tok[0] = b.dma(ph, "scalar", "cb0", cb[:, 0:1], conv_bt[0:128, :])
            cw_tok[1] = b.dma(ph, "scalar", "cw1", cw[:, 4:8], conv_wt[128:256, :])
            cb_tok[1] = b.dma(ph, "scalar", "cb1", cb[:, 1:2], conv_bt[128:256, :])

            for ct in range(NCT):
                par = ct % 2
                ncols = 128 if ct < 34 else 64
                bank0 = [ps_a, ps_c][par]
                bank1 = [ps_b, ps_d][par]
                b.wait(ph, "tensor", b.q(f"wm{par}"), wt_tok[ct])
                pguard(bank0)
                pguard(bank1)
                pguard(ps_e)
                t_mm = 0
                for k in range(KD):
                    w_ap = wt1[:, par * KD * 128 + k * ncols:par * KD * 128 + (k + 1) * ncols]
                    st = (k == 0); sp = (k == KD - 1)
                    b.op(ph, "tensor", lambda e, _w=w_ap, _o=bank0[0:ncols, :],
                         _r=xn_fm[:, k * TMAIN:k * TMAIN + 512], _st=st, _sp=sp:
                         e.matmul(_o, _w, _r, start=_st, stop=_sp))
                    b.op(ph, "tensor", lambda e, _w=w_ap, _o=bank1[0:ncols, :],
                         _r=xn_fm[:, k * TMAIN + 512:k * TMAIN + 1024], _st=st, _sp=sp:
                         e.matmul(_o, _w, _r, start=_st, stop=_sp))
                    t_mm = b.op(ph, "tensor", lambda e, _w=w_ap, _o=ps_e[0:ncols, 0:3],
                                _r=xn_halo[:, k * 4:k * 4 + 3], _st=st, _sp=sp:
                                e.matmul(_o, _w, _r, start=_st, stop=_sp),
                                inc=(cPE if sp else None))
                pe_mm_tok[ct] = t_mm
                if ct + 2 < NCT:
                    b.wait(ph, "gpsimd", cPE, t_mm)
                    wt_tok[ct + 2] = wt_dma(ct + 2)
                b.wait(ph, "scalar", cPE, t_mm)
                if ct < 34:
                    prb = pre[:, par * 1027:(par + 1) * 1027]
                    b.guard(ph, "scalar", ("pre", par))
                    t_evh = b.op(ph, "scalar", lambda e, _o=prb[:, 0:3]:
                                 e.copy(_o, ps_e[:, 0:3]), inc=cAC)
                    prel(ps_e, cAC, t_evh)
                    b.op(ph, "scalar", lambda e, _o=prb[:, 3:515], _s=bank0[:, :]:
                         e.copy(_o, _s))
                    t_evp = b.op(ph, "scalar", lambda e, _o=prb[:, 515:1027], _s=bank1[:, :]:
                                 e.copy(_o, _s), inc=cAC)
                    prel(bank0, cAC, t_evp)
                    prel(bank1, cAC, t_evp)
                    a0 = cacc[:, (par * 2) * TMAIN:(par * 2 + 1) * TMAIN]
                    a1 = cacc[:, (par * 2 + 1) * TMAIN:(par * 2 + 2) * TMAIN]
                    b.wait(ph, "vector", cAC, t_evp)
                    b.wait(ph, "vector", b.q(f"cw{par}"), cw_tok[ct])
                    if ct >= 2:
                        b.wait(ph, "vector", cAC, conv_tok[ct - 2])
                    wc = cw[:, par * 4:par * 4 + 4]
                    b.op(ph, "vector", lambda e, _o=a0, _i=prb[:, 0:1024], _w=wc[:, 0:1]:
                         e.tensor_scalar(_o, _i, _w, None, ALU.mult))
                    b.op(ph, "vector", lambda e, _o=a1, _i=prb[:, 1:1025], _w=wc[:, 1:2],
                         _a=a0: e.scalar_tensor_tensor(_o, _i, _w, _a, ALU.mult, ALU.add))
                    b.op(ph, "vector", lambda e, _o=a0, _i=prb[:, 2:1026], _w=wc[:, 2:3],
                         _a=a1: e.scalar_tensor_tensor(_o, _i, _w, _a, ALU.mult, ALU.add))
                    cv_tok[ct] = b.op(ph, "vector",
                                      lambda e, _o=a1, _i=prb[:, 3:1027], _w=wc[:, 3:4],
                                      _a=a0:
                                      e.scalar_tensor_tensor(_o, _i, _w, _a, ALU.mult,
                                                             ALU.add), inc=cDV)
                    b.release(("pre", par), (cDV, cv_tok[ct]))
                    if ct < 32:
                        cdst = cvo[:, par * TMAIN:(par + 1) * TMAIN]
                        b.guard(ph, "scalar", ("cvo", par))
                    elif ct == 32:
                        cdst = bfm[:, :]
                    else:
                        cdst = cfm[:, :]
                    b.wait(ph, "scalar", cDV, cv_tok[ct])
                    b.wait(ph, "scalar", b.q(f"cb{par}"), cb_tok[ct])
                    conv_tok[ct] = b.op(ph, "scalar",
                                        lambda e, _o=cdst, _i=a1, _b=cb[:, par:par + 1]:
                                        e.activation(_o, _i, AF.Silu, bias=_b), inc=cAC)
                    if ct + 2 < 34:
                        b.wait(ph, "scalar", cDV, cv_tok[ct])
                        cw_tok[ct + 2] = b.dma(ph, "scalar", f"cw{par}", cw[:, par * 4:par * 4 + 4],
                                               conv_wt[(ct + 2) * 128:(ct + 3) * 128, :])
                        b.wait(ph, "scalar", cAC, conv_tok[ct])
                        cb_tok[ct + 2] = b.dma(ph, "scalar", f"cb{par}", cb[:, par:par + 1],
                                               conv_bt[(ct + 2) * 128:(ct + 3) * 128, :])
                else:
                    b.op(ph, "scalar", lambda e: e.copy(dtraw[:, 0:512], bank0[0:64, :]))
                    t_evp = b.op(ph, "scalar",
                                 lambda e: e.copy(dtraw[:, 512:1024], bank1[0:64, :]),
                                 inc=cAC)
                    prel(bank0, cAC, t_evp)
                    prel(bank1, cAC, t_evp)
                    t_evh = b.op(ph, "scalar", lambda e: e.copy(lnscr[:, 0:3], ps_e[:, 0:3]),
                                 inc=cAC)
                    prel(ps_e, cAC, t_evh)
                    conv_tok[ct] = t_evp

                tct = ct - 1
                if 0 <= tct < 32:
                    tpar = tct % 2
                    b.wait(ph, "tensor", cAC, conv_tok[tct])
                    b.guard(ph, "vector", ("xstg", tpar))
                    last_pe = 0
                    ev_last = 0
                    for tt in range(8):
                        bk = tbanks[tt % 2]
                        pguard(bk)
                        t_pe = b.op(ph, "tensor",
                                    lambda e, _i=cvo[:, tpar * TMAIN + tt * 128:
                                                     tpar * TMAIN + (tt + 1) * 128], _bk=bk:
                                    e.transpose(_bk[:, 0:128], _i, identb[:, :]), inc=cPE)
                        last_pe = t_pe
                        b.wait(ph, "vector", cPE, t_pe)
                        ev_last = b.op(ph, "vector",
                                       lambda e, _o=xstg[:, tpar * TMAIN + tt * 128:
                                                         tpar * TMAIN + (tt + 1) * 128],
                                       _bk=bk: e.tensor_copy(_o, _bk[:, 0:128]), inc=cDV)
                        prel(bk, cDV, ev_last)
                    b.release(("cvo", tpar), (cPE, last_pe))
                    b.wait(ph, "sync", cDV, ev_last)
                    stg_dma[tct] = b.dma(ph, "sync", f"sxw{tpar}",
                                         x_tm_d.rearrange("(t p) (c q) -> p t q c",
                                                          p=128, q=128)[:, :, :, tct],
                                         xstg[:, tpar * TMAIN:(tpar + 1) * TMAIN]
                                         .rearrange("p (t q) -> p t q", q=128))
                    b.release(("xstg", tpar), (b.q(f"sxw{tpar}"), stg_dma[tct]))
            x_tm_fin = (b.q("sxw0").n, b.q("sxw1").n)

            b.wait(ph, "scalar", cAC, conv_tok[34])
            b.op(ph, "scalar", lambda e: e.activation(dtscr[:, :], dtraw[:, :], AF.Abs,
                                                      bias=dtb_c[:, :]))
            b.op(ph, "scalar", lambda e: e.activation(dtscr[:, :], dtscr[:, :], AF.Exp,
                                                      scale=-1.0))
            t_l = b.op(ph, "scalar", lambda e: e.activation(dtscr[:, :], dtscr[:, :],
                                                            AF.Ln, bias=1.0), inc=cAC)
            b.wait(ph, "vector", cAC, t_l)
            b.op(ph, "vector", lambda e: e.tensor_scalar(dt_fm[:, :], dtraw[:, :],
                                                         dtb_c[:, :], 0.0, ALU.add, ALU.max))
            b.op(ph, "vector", lambda e: e.tensor_tensor(dt_fm[:, :], dt_fm[:, :],
                                                         dtscr[:, :], ALU.add))
            dt_done = b.op(ph, "vector", lambda e: e.tensor_scalar(dt_fm[:, :], dt_fm[:, :],
                                                                   100.0, None, ALU.min),
                           inc=cDV)
            ph.emit(block)

        _es1.close()
        b.fence(ph, [cPE, cDV, cAC, cPO], ["sxw0", "sxw1"])
        _es3 = contextlib.ExitStack()
        with _es3:
            qtl = _es3.enter_context(nc.sbuf_tensor("qtl", [128, 2 * D], F32))
            qto = _es3.enter_context(nc.sbuf_tensor("qto", [128, 2 * D], BF16))
            qtn = _es3.enter_context(nc.sbuf_tensor("qtn", [128, D], BF16))
            qac = _es3.enter_context(nc.sbuf_tensor("qac", [128, 26], F32))
            qsc = _es3.enter_context(nc.sbuf_tensor("qsc", [128, 4], F32))
            cc1s = _es3.enter_context(nc.sbuf_tensor("cc1s", [4, 1], F32))

            tiles = []
            for (wsrc, tdst, npc, ncc, wi) in (
                    (wu_s, tau_u_s, 2, 4, 0), (wg_s, tau_g_s, 2, 4, 1),
                    (wd_s, tau_d_s, 8, 1, 2)):
                for pc in range(npc):
                    for cc in range(ncc):
                        tiles.append((wsrc, tdst, pc, cc, wi))

            def qld(i):
                wsrc, tdst, pc, cc, wi = tiles[i]
                par = i % 2
                return b.dma(ph, "scalar", f"ql{par}", qtl[:, par * D:(par + 1) * D],
                             wsrc[pc * 128:(pc + 1) * 128, cc * D:(cc + 1) * D])

            ld_tok = [0] * len(tiles)
            abs_tok = [0] * len(tiles)
            ld_tok[0] = qld(0)
            ld_tok[1] = qld(1)
            for i in range(len(tiles)):
                par = i % 2
                b.wait(ph, "scalar", b.q(f"ql{par}"), ld_tok[i])
                abs_tok[i] = b.op(ph, "scalar",
                                  lambda e, _i=qtl[:, par * D:(par + 1) * D],
                                  _a=qac[:, i:i + 1]:
                                  e.activation(_i, _i, AF.Abs, accum_out=_a), inc=cAC)
                if i + 2 < len(tiles):
                    b.wait(ph, "scalar", cAC, abs_tok[i])
                    ld_tok[i + 2] = qld(i + 2)
            b.wait(ph, "vector", cAC, abs_tok[-1])
            for wi in range(3):
                b.op(ph, "vector", lambda e, _o=qsc[:, wi:wi + 1],
                     _i=qac[:, wi * 8:wi * 8 + 8]: e.reduce_sum(_o, _i, axis=AX.X))
            t_q = b.op(ph, "vector", lambda e: e.memset(qsc[:, 3:4], 0.0), inc=cDV)
            b.wait(ph, "tensor", cDV, t_q)
            pguard(ps_bc)
            t_pr = b.op(ph, "tensor", lambda e: e.matmul(ps_bc[0:4, 0:1], qsc[:, :],
                                                         onesf[:, 0:1], start=True,
                                                         stop=True), inc=cPE)
            b.wait(ph, "vector", cPE, t_pr)
            t_ev = b.op(ph, "vector", lambda e: e.tensor_copy(cc1s[:, :], ps_bc[0:4, 0:1]),
                        inc=cDV)
            prel(ps_bc, cDV, t_ev)
            b.wait(ph, "gpsimd", cDV, t_ev)
            t_st = b.dma(ph, "gpsimd", "ccs", cc1_i.ap()[:, :], cc1s[:, :])
            b.wait(ph, "gpsimd", b.q("ccs"), t_st)
            b.op_cc(ph, lambda e: e.collective_compute(
                "AllReduce", ALU.add, replica_groups=[list(range(NCORE))],
                ins=[cc1_i.ap().opt()], outs=[cc1_o.ap().opt()]), cc1)
            b.wait(ph, "gpsimd", cc1, 1)
            t_rb = b.dma(ph, "gpsimd", "ccs", sc4[:, :], cc1_o.ap()[:, :])
            b.wait(ph, "vector", b.q("ccs"), t_rb)
            t_sc = b.op(ph, "vector", lambda e: e.tensor_scalar(sc4[:, :], sc4[:, :],
                                                                1.0 / WSZ, EPS_Q, ALU.mult,
                                                                ALU.add), inc=cDV)
            b.wait(ph, "scalar", cDV, t_sc)
            t_sw = b.dma(ph, "scalar", "scr", scr_d.ap()[:, :], sc4[:, :])
            b.wait(ph, "scalar", b.q("scr"), t_sw)
            t_scr = b.dma(ph, "scalar", "scr", scrow[:, :],
                          scr_d.ap().rearrange("a o -> o a"))
            b.wait(ph, "tensor", b.q("scr"), t_scr)
            pguard(ps_bc)
            t_bc2 = b.op(ph, "tensor", lambda e: e.matmul(ps_bc[:, 0:4], onesf[0:1, :],
                                                          scrow[0:1, :], start=True,
                                                          stop=True), inc=cPE)
            b.wait(ph, "vector", cPE, t_bc2)
            t_ev = b.op(ph, "vector", lambda e: e.tensor_copy(scbc[:, :], ps_bc[:, 0:4]),
                        inc=cDV)
            prel(ps_bc, cDV, t_ev)
            b.op(ph, "vector", lambda e:
                 e.tensor_tensor(sgsd_c[:, :], scbc[:, 1:2], scbc[:, 2:3], ALU.mult))
            scales_done = b.op(ph, "vector", lambda e:
                               e.reciprocal(qsc[:, 0:3], scbc[:, 0:3]), inc=cDV)
            st_tok2 = [0] * len(tiles)
            ld_tok[0] = qld(0)
            ld_tok[1] = qld(1)
            for i, (wsrc, tdst, pc, cc, wi) in enumerate(tiles):
                par = i % 2
                scol = qsc[:, wi:wi + 1]
                qo = qto[:, par * D:(par + 1) * D]
                b.wait(ph, "vector", b.q(f"ql{par}"), ld_tok[i])
                b.wait(ph, "vector", cDV, scales_done)
                if i >= 2:
                    b.wait(ph, "vector", b.q(f"ts{par}"), st_tok2[i - 2])
                b.op(ph, "vector", lambda e, _i=qtl[:, par * D:(par + 1) * D], _s=scol,
                     _o=qo: e.tensor_scalar(_o, _i, _s, 0.5, ALU.mult, ALU.is_gt))
                b.op(ph, "vector", lambda e, _i=qtl[:, par * D:(par + 1) * D], _s=scol:
                     e.tensor_scalar(qtn[:, :], _i, _s, -0.5, ALU.mult, ALU.is_lt))
                tau_tok = b.op(ph, "vector", lambda e, _o=qo:
                               e.tensor_tensor(_o, _o, qtn[:, :], ALU.subtract), inc=cDV)
                if i + 2 < len(tiles):
                    b.wait(ph, "scalar", cDV, tau_tok)
                    ld_tok[i + 2] = qld(i + 2)
                b.wait(ph, "sync", cDV, tau_tok)
                st_tok2[i] = b.dma(ph, "sync", f"ts{par}",
                                   tdst.ap()[pc * 128:(pc + 1) * 128, cc * D:(cc + 1) * D],
                                   qo)
            b.wait(ph, "gpsimd", b.q("ts0"), b.q("ts0").n)
            b.wait(ph, "gpsimd", b.q("ts1"), b.q("ts1").n)
            b.op_cc(ph, lambda e: e.collective_compute(
                "AllGather", ALU.bypass, replica_groups=[list(range(NCORE))],
                ins=[tau_u_s.ap().opt()], outs=[tau_u.ap().opt()]), cc2)
            b.op_cc(ph, lambda e: e.collective_compute(
                "AllGather", ALU.bypass, replica_groups=[list(range(NCORE))],
                ins=[tau_g_s.ap().opt()], outs=[tau_g.ap().opt()]), cc2b)
            b.op_cc(ph, lambda e: e.collective_compute(
                "AllGather", ALU.bypass, replica_groups=[list(range(NCORE))],
                ins=[tau_d_s.ap().opt()], outs=[tau_d.ap().opt()]), cc2c)
            ph.emit(block)

        b.fence(ph, [cPE, cDV, cAC, cPO], ["ts0", "ts1"])
        _es4 = contextlib.ExitStack()
        with _es4:
            nda = _es4.enter_context(nc.sbuf_tensor("nda", [HH, CHK], F32))
            ndat = _es4.enter_context(nc.sbuf_tensor("ndat", [128, 2 * HH], F32))
            decf = _es4.enter_context(nc.sbuf_tensor("decf", [HH, CHK], F32))
            ddf = _es4.enter_context(nc.sbuf_tensor("ddf", [HH, CHK], F32))
            ncol = _es4.enter_context(nc.sbuf_tensor("ncol", [HH, 8], F32))
            xch = _es4.enter_context(nc.sbuf_tensor("xch", [128, 2 * INTER], BF16))
            xdd = _es4.enter_context(nc.sbuf_tensor("xdd", [128, 2 * INTER], BF16))
            stile = _es4.enter_context(nc.sbuf_tensor("stile", [128, INTER], BF16))
            run = _es4.enter_context(nc.sbuf_tensor("run", [128, INTER], F32))
            zrow = _es4.enter_context(nc.sbuf_tensor("zrow", [1, INTER], F32))

            b.wait(ph, "vector", cDV, t_consts)
            run_tok = b.op(ph, "vector", lambda e: e.memset(run[:, :], 0.0), inc=cDV)
            b.op(ph, "vector", lambda e: e.memset(lcrow[:, :], 0.0))
            t_zr = b.op(ph, "vector", lambda e: e.memset(zrow[:, :], 0.0), inc=cDV)
            b.wait(ph, "sync", cDV, t_zr)
            b.dma(ph, "sync", "sti", st_i.ap()[128:129, HH:INTER], zrow[0:1, HH:INTER])
            s_dma_toks = []
            xdd_tok = 0

            def fp32_T(src, dst_psum_region, inc_sem=None):
                kdim = src.shape[0]
                return b.op(ph, "tensor", lambda e, _s=src, _o=dst_psum_region, _k=kdim:
                            e.transpose(_o, _s, identf[0:_k, 0:_k]), inc=inc_sem)

            for c in range(NCHU):
                b.wait(ph, "scalar", b.q("sxw0"), x_tm_fin[0])
                b.wait(ph, "scalar", b.q("sxw1"), x_tm_fin[1])
                b.wait(ph, "scalar", cDV, xdd_tok)
                xch_tok = [0, 0]
                for t in range(2):
                    xch_tok[t] = b.dma(ph, "scalar", "xr3", xch[:, t * INTER:(t + 1) * INTER],
                                       x_tm_d[(c * 2 + t) * 128:(c * 2 + t + 1) * 128, :])
                b.wait(ph, "vector", cDV, dt_done)
                b.wait(ph, "vector", cAC, t_expa)
                t_nda = b.op(ph, "vector", lambda e, _c=c:
                             e.tensor_scalar(nda[:, :], dt_fm[:, _c * CHK:(_c + 1) * CHK],
                                             expa_c[:, :], None, ALU.mult), inc=cDV)
                b.wait(ph, "tensor", cDV, t_nda)
                for t in range(2):
                    pguard(ps_bc)
                    t1 = fp32_T(nda[:, t * 128:(t + 1) * 128], ps_bc[:, 0:64], cPE)
                    pguard(ps_e)
                    t2 = fp32_T(dt_fm[:, c * CHK + t * 128:c * CHK + (t + 1) * 128],
                                ps_e[:, 0:64], cPE)
                    b.wait(ph, "vector", cPE, t1)
                    t_e1 = b.op(ph, "vector", lambda e, _t=t:
                                e.tensor_copy(ndat[:, _t * 64:(_t + 1) * 64], ps_bc[:, 0:64]),
                                inc=cDV)
                    prel(ps_bc, cDV, t_e1)
                    b.wait(ph, "vector", cPE, t2)
                    t_e2 = b.op(ph, "vector", lambda e, _t=t, _c=c:
                                e.tensor_copy(dt_tm[:, (_c * 2 + _t) * HH:(_c * 2 + _t + 1) * HH],
                                              ps_e[:, 0:64]), inc=cDV)
                    prel(ps_e, cDV, t_e2)
                b.wait(ph, "tensor", cDV, t_e1)
                pguard(ps_e)
                b.op(ph, "tensor", lambda e: e.matmul(ps_e[0:HH, 0:CHK], ndat[:, 0:HH],
                                                      ta[:, :], start=True, stop=False))
                b.op(ph, "tensor", lambda e: e.matmul(ps_e[0:HH, 0:CHK], ndat[:, HH:128],
                                                      tb_c[:, :], start=False, stop=True))
                b.op(ph, "tensor", lambda e: e.matmul(ps_e[:, 256:320], triu[:, :],
                                                      ndat[:, 0:HH], start=True, stop=True))
                b.op(ph, "tensor", lambda e: e.matmul(ps_e[:, 320:384], onesf[:, :],
                                                      ndat[:, 0:HH], start=True, stop=False))
                t_cs = b.op(ph, "tensor", lambda e: e.matmul(ps_e[:, 320:384], triu[:, :],
                                                             ndat[:, HH:128], start=False,
                                                             stop=True), inc=cPE)
                b.wait(ph, "vector", cPE, t_cs)
                t_csfm = b.op(ph, "vector", lambda e, _c=c:
                              e.tensor_copy(csneg_fm[:, _c * CHK:(_c + 1) * CHK],
                                            ps_e[0:HH, 0:CHK]), inc=cDV)
                b.wait(ph, "sync", cDV, t_csfm)
                b.dma(ph, "sync", "csf", csf_d.ap()[c * HH:(c + 1) * HH, :],
                      csneg_fm[:, c * CHK:(c + 1) * CHK])
                t_cst = b.op(ph, "vector", lambda e, _c=c:
                             e.tensor_copy(csneg_tm[:, _c * 2 * HH:(_c + 1) * 2 * HH],
                                           ps_e[:, 256:384]), inc=cDV)
                prel(ps_e, cDV, t_cst)
                t_nc = b.op(ph, "vector", lambda e, _c=c:
                            e.tensor_scalar(ncol[:, _c:_c + 1],
                                            csneg_fm[:, _c * CHK + 255:_c * CHK + 256],
                                            -1.0, None, ALU.mult), inc=cDV)
                b.wait(ph, "sync", cDV, t_cst)
                b.guard(ph, "sync", ("ngr",))
                t_nr = b.dma(ph, "sync", "ngr", negrow[:, :],
                             csneg_tm[127:128, (c * 2 + 1) * HH:(c * 2 + 2) * HH])
                b.wait(ph, "scalar", b.q("ngr"), t_nr)
                t_dr = b.op(ph, "scalar", lambda e: e.activation(decrow[:, :], negrow[:, :],
                                                                 AF.Exp, scale=-1.0), inc=cAC)
                b.wait(ph, "vector", b.q("ngr"), t_nr)
                t_lcr = b.op(ph, "vector", lambda e:
                             e.scalar_tensor_tensor(lcrow[:, :], negrow[:, :], -1.0, lcrow[:, :],
                                                    ALU.mult, ALU.add), inc=cDV)
                b.release(("ngr",), (cAC, t_dr), (cDV, t_lcr))
                b.wait(ph, "tensor", cAC, t_dr)
                pguard(ps_bc)
                t_db = b.op(ph, "tensor", lambda e: e.matmul(ps_bc[:, 0:HH], onesf[0:1, :],
                                                             decrow[0:1, :], start=True,
                                                             stop=True), inc=cPE)
                b.wait(ph, "vector", cPE, t_db)
                t_dbe = b.op(ph, "vector", lambda e, _c=c:
                             e.tensor_copy(dec_bc[:, _c * HH:(_c + 1) * HH], ps_bc[:, 0:HH]),
                             inc=cDV)
                prel(ps_bc, cDV, t_dbe)
                b.wait(ph, "scalar", cDV, t_nc)
                t_df = b.op(ph, "scalar", lambda e, _c=c:
                            e.activation(decf[:, :], csneg_fm[:, _c * CHK:(_c + 1) * CHK],
                                         AF.Exp, bias=ncol[:, _c:_c + 1]), inc=cAC)
                b.wait(ph, "vector", cAC, t_df)
                t_dd = b.op(ph, "vector", lambda e, _c=c:
                            e.tensor_tensor(ddf[:, :], dt_fm[:, _c * CHK:(_c + 1) * CHK],
                                            decf[:, :], ALU.mult), inc=cDV)
                b.wait(ph, "tensor", cDV, t_dd)
                for t in range(2):
                    pguard(ps_bc)
                    t3 = fp32_T(ddf[:, t * 128:(t + 1) * 128], ps_bc[:, 0:64], cPE)
                    b.wait(ph, "vector", cPE, t3)
                    t_dde = b.op(ph, "vector", lambda e, _t=t, _c=c:
                                 e.tensor_copy(dtdec_tm[:, (_c * 2 + _t) * HH:
                                                        (_c * 2 + _t + 1) * HH],
                                               ps_bc[:, 0:64]), inc=cDV)
                    prel(ps_bc, cDV, t_dde)
                b.wait(ph, "tensor", cAC, conv_tok[32])
                for t in range(2):
                    bk = tbanks[t]
                    pguard(bk)
                    t_bt = b.op(ph, "tensor", lambda e, _c=c, _t=t, _bk=bk:
                                e.transpose(_bk[:, 0:128],
                                            bfm[:, _c * CHK + _t * 128:_c * CHK + (_t + 1) * 128],
                                            identb[:, :]), inc=cPE)
                    b.wait(ph, "vector", cPE, t_bt)
                    t_bte = b.op(ph, "vector", lambda e, _c=c, _t=t, _bk=bk:
                                 e.tensor_copy(btm[:, _c * 256 + _t * 128:_c * 256 + (_t + 1) * 128],
                                               _bk[:, 0:128]), inc=cDV)
                    prel(bk, cDV, t_bte)
                b.wait(ph, "vector", b.q("xr3"), xch_tok[1])
                b.wait(ph, "vector", cDV, t_dde)
                b.guard(ph, "vector", ("xdd",))
                for t in range(2):
                    for h in range(HH):
                        xdd_tok = b.op(ph, "vector", lambda e, _t=t, _h=h, _c=c:
                                       e.tensor_scalar(
                                           xdd[:, _t * INTER + _h * PH:_t * INTER + (_h + 1) * PH],
                                           xch[:, _t * INTER + _h * PH:_t * INTER + (_h + 1) * PH],
                                           dtdec_tm[:, (_c * 2 + _t) * HH + _h:
                                                    (_c * 2 + _t) * HH + _h + 1],
                                           None, ALU.mult),
                                       inc=(cDV if (t == 1 and h == HH - 1) else None))
                b.wait(ph, "tensor", cDV, xdd_tok)
                b.wait(ph, "tensor", cDV, t_bte)
                st_ev_last = 0
                for hg in range(8):
                    bank = [ps_c, ps_d][hg % 2]
                    pguard(bank)
                    t_smm = 0
                    for hh in range(8):
                        h = hg * 8 + hh
                        for t in range(2):
                            t_smm = b.op(ph, "tensor",
                                         lambda e, _h=h, _t=t, _c=c, _bk=bank, _hh=hh:
                                         e.matmul(_bk[:, _hh * PH:(_hh + 1) * PH],
                                                  btm[:, (_c * 2 + _t) * 128:(_c * 2 + _t + 1) * 128],
                                                  xdd[:, _t * INTER + _h * PH:_t * INTER + (_h + 1) * PH],
                                                  start=(_t == 0), stop=(_t == 1)),
                                         inc=(cPE if (hh == 7 and t == 1) else None))
                    b.wait(ph, "scalar", cPE, t_smm)
                    b.guard(ph, "scalar", ("stile", hg))
                    t_se = b.op(ph, "scalar", lambda e, _hg=hg, _bk=bank:
                                e.copy(stile[:, _hg * 512:(_hg + 1) * 512], _bk[:, :]),
                                inc=cAC)
                    prel(bank, cAC, t_se)
                    st_ev_last = t_se
                b.release(("xdd",), (cPE, t_smm))
                b.wait(ph, "vector", cAC, st_ev_last)
                b.wait(ph, "vector", cDV, t_dbe)
                for h in range(HH):
                    run_tok = b.op(ph, "vector", lambda e, _h=h, _c=c:
                                   e.scalar_tensor_tensor(run[:, _h * PH:(_h + 1) * PH],
                                                          run[:, _h * PH:(_h + 1) * PH],
                                                          dec_bc[:, _c * HH + _h:_c * HH + _h + 1],
                                                          stile[:, _h * PH:(_h + 1) * PH],
                                                          ALU.mult, ALU.add),
                                   inc=(cDV if h == HH - 1 else None))
                b.wait(ph, "sync", cAC, st_ev_last)
                t_sd = b.dma(ph, "sync", "ssw", s_st_d[c * 128:(c + 1) * 128, :], stile[:, :])
                s_dma_toks.append(t_sd)
                for hg in range(8):
                    b.release(("stile", hg), (b.q("ssw"), t_sd), (cDV, run_tok))
            b.wait(ph, "sync", cDV, run_tok)
            t_acc = b.dma(ph, "sync", "sti", st_i.ap()[0:128, :], run[:, :])
            b.wait(ph, "sync", cDV, cDV.n)
            t_lc2 = b.dma(ph, "sync", "sti", st_i.ap()[128:129, 0:HH], lcrow[:, :])
            b.wait(ph, "gpsimd", b.q("sti"), t_lc2)
            b.op_cc(ph, lambda e: e.collective_compute(
                "AllGather", ALU.bypass, replica_groups=[list(range(NCORE))],
                ins=[st_i.ap().opt()], outs=[st_o.ap().opt()]), cc3)
            ph.emit(block)

        b.fence(ph, [cPE, cDV, cAC, cPO], ["sti", "ssw", "csf"])
        _es5 = contextlib.ExitStack()
        with _es5:
            la8 = _es5.enter_context(nc.sbuf_tensor("la8", [8, HH], F32))
            w8 = _es5.enter_context(nc.sbuf_tensor("w8", [8, HH], F32))
            w8r = _es5.enter_context(nc.sbuf_tensor("w8r", [1, 8 * HH], F32))
            wbc = _es5.enter_context(nc.sbuf_tensor("wbc", [128, 8 * HH], F32))
            dbc = _es5.enter_context(nc.sbuf_tensor("dbc", [128, INTER], BF16))
            gac = _es5.enter_context(nc.sbuf_tensor("gac", [128, INTER], F32))
            st = _es5.enter_context(nc.sbuf_tensor("st", [128, INTER], F32))
            stb = _es5.enter_context(nc.sbuf_tensor("stb", [128, INTER], BF16))
            xch4 = _es5.enter_context(nc.sbuf_tensor("xch4", [128, 2 * INTER], BF16))
            xdt4 = _es5.enter_context(nc.sbuf_tensor("xdt4", [128, 2 * INTER], BF16))
            sch4 = _es5.enter_context(nc.sbuf_tensor("sch4", [128, INTER], BF16))
            ytl = _es5.enter_context(nc.sbuf_tensor("ytl", [128, 2 * INTER], BF16))
            gtm = _es5.enter_context(nc.sbuf_tensor("gtm", [128, 384], BF16))
            csr = _es5.enter_context(nc.sbuf_tensor("csr", [1, 2 * 2048], F32))
            seg = _es5.enter_context(nc.sbuf_tensor("seg", [128, 2 * 384], F32))
            etl = _es5.enter_context(nc.sbuf_tensor("etl", [128, 2 * 384], BF16))
            evt = _es5.enter_context(nc.sbuf_tensor("evt", [128, 2 * 256], BF16))
            gl = _es5.enter_context(nc.sbuf_tensor("gl", [128, 2 * 384], BF16))
            glv = _es5.enter_context(nc.sbuf_tensor("glv", [128, 2 * 256], BF16))

            bcast_bank = [ps_bc, ps_e]

            t_dr2 = b.dma(ph, "scalar", "d4", csr[0:1, 0:INTER], d_row)
            b.wait(ph, "tensor", b.q("d4"), t_dr2)
            dbc_last = 0
            for j in range(8):
                bk = bcast_bank[j % 2]
                pguard(bk)
                t_dj = b.op(ph, "tensor", lambda e, _j=j, _bk=bk:
                            e.matmul(_bk[:, :], onesf[0:1, :],
                                     csr[0:1, _j * 512:(_j + 1) * 512],
                                     start=True, stop=True), inc=cPE)
                b.wait(ph, "vector", cPE, t_dj)
                dbc_last = b.op(ph, "vector", lambda e, _j=j, _bk=bk:
                                e.tensor_copy(dbc[:, _j * 512:(_j + 1) * 512], _bk[:, :]),
                                inc=cDV)
                prel(bk, cDV, dbc_last)
            b.wait(ph, "scalar", cc3, 1)
            t_la = b.dma(ph, "scalar", "la", la8[:, :],
                         st_o.ap().rearrange("(r q) f -> r q f", q=129)[:, 128, 0:HH])
            b.wait(ph, "tensor", b.q("la"), t_la)
            pguard(ps_bc)
            t_ld8 = b.op(ph, "tensor", lambda e: e.matmul(ps_bc[0:8, 0:HH], selt[:, :],
                                                          la8[:, :], start=True, stop=True),
                         inc=cPE)
            b.wait(ph, "scalar", cPE, t_ld8)
            t_w8 = b.op(ph, "scalar", lambda e: e.activation(w8[:, :], ps_bc[0:8, 0:HH],
                                                             AF.Exp), inc=cAC)
            prel(ps_bc, cAC, t_w8)
            b.wait(ph, "vector", cAC, t_w8)
            t_w8m = b.op(ph, "vector", lambda e: e.tensor_scalar(w8[:, :], w8[:, :],
                                                                 pmask[:, :], None, ALU.mult),
                         inc=cDV)
            b.wait(ph, "sync", cDV, t_w8m)
            t_w8w = b.dma(ph, "sync", "w8", w8_d.ap()[:, :], w8[:, :])
            b.wait(ph, "sync", b.q("w8"), t_w8w)
            t_w8r = b.dma(ph, "sync", "w8", w8r[0:1, :],
                          w8_d.ap().rearrange("r h -> (r h)").rearrange("(o f) -> o f", o=1))
            b.wait(ph, "tensor", b.q("w8"), t_w8r)
            wbc_last = 0
            for j in range(4):
                bk = bcast_bank[j % 2]
                pguard(bk)
                t_wj = b.op(ph, "tensor", lambda e, _j=j, _bk=bk:
                            e.matmul(_bk[:, 0:128], onesf[0:1, :],
                                     w8r[0:1, _j * 128:(_j + 1) * 128], start=True,
                                     stop=True), inc=cPE)
                b.wait(ph, "vector", cPE, t_wj)
                wbc_last = b.op(ph, "vector", lambda e, _j=j, _bk=bk:
                                e.tensor_copy(wbc[:, _j * 128:(_j + 1) * 128], _bk[:, 0:128]),
                                inc=cDV)
                prel(bk, cDV, wbc_last)
            st_tok = b.op(ph, "vector", lambda e: e.memset(st[:, :], 0.0), inc=cDV)
            for i in range(NCORE):
                t_gr = b.dma(ph, "scalar", "gac", gac[:, :], st_o.ap()[i * 129:i * 129 + 128, :])
                b.wait(ph, "vector", b.q("gac"), t_gr)
                b.wait(ph, "vector", cDV, wbc_last)
                for h in range(HH):
                    st_tok = b.op(ph, "vector", lambda e, _i=i, _h=h:
                                  e.scalar_tensor_tensor(st[:, _h * PH:(_h + 1) * PH],
                                                         gac[:, _h * PH:(_h + 1) * PH],
                                                         wbc[:, _i * HH + _h:_i * HH + _h + 1],
                                                         st[:, _h * PH:(_h + 1) * PH],
                                                         ALU.mult, ALU.add),
                                  inc=(cDV if h == HH - 1 else None))
                b.wait(ph, "scalar", cDV, st_tok)

            y_dma_toks = []
            for c in range(NCHU):
                b.guard(ph, "vector", ("stb",))
                b.wait(ph, "vector", cDV, st_tok)
                t_stb = b.op(ph, "vector", lambda e: e.tensor_copy(stb[:, :], st[:, :]),
                             inc=cDV)
                b.guard(ph, "scalar", ("xch4",))
                xc_tok = [0, 0]
                for t in range(2):
                    xc_tok[t] = b.dma(ph, "scalar", "xr4", xch4[:, t * INTER:(t + 1) * INTER],
                                      x_tm_d[(c * 2 + t) * 128:(c * 2 + t + 1) * 128, :])
                b.wait(ph, "scalar", b.q("ssw"), b.q("ssw").n)
                b.guard(ph, "scalar", ("sch4",))
                sc_tok = b.dma(ph, "scalar", "ssr", sch4[:, :], s_st_d[c * 128:(c + 1) * 128, :])
                b.wait(ph, "vector", b.q("xr4"), xc_tok[1])
                b.guard(ph, "vector", ("xdt4",))
                xdt_tok = 0
                for t in range(2):
                    for h in range(HH):
                        xdt_tok = b.op(ph, "vector", lambda e, _t=t, _h=h, _c=c:
                                       e.tensor_scalar(
                                           xdt4[:, _t * INTER + _h * PH:_t * INTER + (_h + 1) * PH],
                                           xch4[:, _t * INTER + _h * PH:_t * INTER + (_h + 1) * PH],
                                           dt_tm[:, (_c * 2 + _t) * HH + _h:
                                                 (_c * 2 + _t) * HH + _h + 1],
                                           None, ALU.mult),
                                       inc=(cDV if (t == 1 and h == HH - 1) else None))
                b.wait(ph, "vector", cDV, dbc_last)
                b.guard(ph, "vector", ("ytl",))
                yi_tok = 0
                for t in range(2):
                    yi_tok = b.op(ph, "vector", lambda e, _t=t:
                                  e.tensor_tensor(ytl[:, _t * INTER:(_t + 1) * INTER],
                                                  xch4[:, _t * INTER:(_t + 1) * INTER],
                                                  dbc[:, :], ALU.mult), inc=cDV)
                b.release(("xch4",), (cDV, yi_tok))
                pguard(ps_e)
                b.wait(ph, "tensor", cAC, conv_tok[33])
                t_gmm = 0
                for sh in range(2):
                    t_gmm = b.op(ph, "tensor", lambda e, _c=c, _sh=sh:
                                 e.matmul(ps_e[:, _sh * 256:(_sh + 1) * 256],
                                          bfm[:, _c * CHK + _sh * 128:_c * CHK + (_sh + 1) * 128],
                                          cfm[:, _c * CHK:(_c + 1) * CHK],
                                          start=True, stop=True),
                                 inc=(cPE if sh == 1 else None))
                b.wait(ph, "vector", cPE, t_gmm)
                b.guard(ph, "vector", ("gtm",))
                b.op(ph, "vector", lambda e: e.tensor_tensor(gtm[:, 0:128], ps_e[:, 0:128],
                                                             triu[:, :], ALU.mult))
                b.op(ph, "vector", lambda e: e.tensor_copy(gtm[:, 128:256], ps_e[:, 128:256]))
                t_gt = b.op(ph, "vector", lambda e:
                            e.tensor_tensor(gtm[:, 256:384], ps_e[:, 384:512], triu[:, :],
                                            ALU.mult), inc=cDV)
                prel(ps_e, cDV, t_gt)
                cs_tok = [0, 0]
                cs_last_bcm = [0, 0]
                t_ymm_l = [0] * (HH + 2)
                t_glv_l = [0] * (HH + 2)

                def csr_load(hg):
                    par2 = hg % 2
                    b.guard(ph, "sync", ("csr", par2))
                    b.wait(ph, "sync", b.q("csf"), b.q("csf").n)
                    cs_tok[par2] = b.dma(ph, "sync", f"cs{par2}",
                                         csr[0:1, par2 * 2048:(par2 + 1) * 2048],
                                         csf_d.ap()[c * HH + hg * 8:c * HH + (hg + 1) * 8, :]
                                         .rearrange("h l -> (h l)")
                                         .rearrange("(o f) -> o f", o=1))

                def front(h):
                    hg, hh = divmod(h, 8)
                    par2 = hg % 2
                    par = h % 2
                    bcb = bcast_bank[par]
                    b.wait(ph, "tensor", b.q(f"cs{par2}"), cs_tok[par2])
                    pguard(bcb)
                    t_bcm = b.op(ph, "tensor", lambda e, _hh=hh, _par2=par2, _bk=bcb:
                                 e.matmul(_bk[:, 0:256], onesf[0:1, :],
                                          csr[0:1, _par2 * 2048 + _hh * 256:
                                              _par2 * 2048 + (_hh + 1) * 256],
                                          start=True, stop=True), inc=cPE)
                    cs_last_bcm[par2] = t_bcm
                    b.wait(ph, "vector", cPE, t_bcm)
                    b.guard(ph, "vector", ("seg", par))
                    b.op(ph, "vector", lambda e, _par=par, _c=c, _h=h, _bk=bcb:
                         e.tensor_scalar(seg[:, _par * 384:_par * 384 + 256],
                                         _bk[:, 0:256],
                                         csneg_tm[:, (_c * 2) * HH + _h:(_c * 2) * HH + _h + 1],
                                         0.0, ALU.subtract, ALU.max))
                    t_seg = b.op(ph, "vector", lambda e, _par=par, _c=c, _h=h, _bk=bcb:
                                 e.tensor_scalar(seg[:, _par * 384 + 256:_par * 384 + 384],
                                                 _bk[:, 128:256],
                                                 csneg_tm[:, (_c * 2 + 1) * HH + _h:
                                                          (_c * 2 + 1) * HH + _h + 1],
                                                 0.0, ALU.subtract, ALU.max), inc=cDV)
                    b.wait(ph, "scalar", cDV, t_seg)
                    b.guard(ph, "scalar", ("etl", par))
                    b.op(ph, "scalar", lambda e, _par=par:
                         e.activation(etl[:, _par * 384:(_par + 1) * 384],
                                      seg[:, _par * 384:(_par + 1) * 384], AF.Exp,
                                      scale=-1.0))
                    b.guard(ph, "scalar", ("evt", par))
                    t_ev2 = b.op(ph, "scalar", lambda e, _par=par, _bk=bcb:
                                 e.activation(evt[:, _par * 256:(_par + 1) * 256],
                                              _bk[:, 0:256], AF.Exp, scale=-1.0),
                                 inc=cAC)
                    b.release(("seg", par), (cAC, t_ev2))
                    prel(bcb, cAC, t_ev2)
                    b.wait(ph, "gpsimd", cAC, t_ev2)
                    b.wait(ph, "gpsimd", cDV, t_gt)
                    b.guard(ph, "gpsimd", ("gl", par))
                    b.op(ph, "gpsimd", lambda e, _par=par:
                         e.tensor_tensor(gl[:, _par * 384:(_par + 1) * 384],
                                         etl[:, _par * 384:(_par + 1) * 384],
                                         gtm[:, :], ALU.mult))
                    t_glv = b.op(ph, "gpsimd", lambda e, _par=par, _c=c:
                                 e.tensor_tensor(glv[:, _par * 256:(_par + 1) * 256],
                                                 evt[:, _par * 256:(_par + 1) * 256],
                                                 cfm[:, _c * CHK:(_c + 1) * CHK],
                                                 ALU.mult), inc=cPO)
                    b.release(("etl", par), (cPO, t_glv))
                    b.release(("evt", par), (cPO, t_glv))
                    t_glv_l[h] = t_glv

                def ymms(h, bank0, bank1):
                    hg, hh = divmod(h, 8)
                    par = h % 2
                    b.wait(ph, "tensor", cPO, t_glv_l[h])
                    b.wait(ph, "tensor", cDV, xdt_tok)
                    b.wait(ph, "tensor", cDV, t_stb)
                    b.op(ph, "tensor", lambda e, _par=par, _h=h, _bk=bank0, _hh=hh:
                         e.matmul(_bk[:, _hh * PH:(_hh + 1) * PH],
                                  gl[:, _par * 384:_par * 384 + 128],
                                  xdt4[:, _h * PH:(_h + 1) * PH],
                                  start=True, stop=False))
                    b.op(ph, "tensor", lambda e, _par=par, _h=h, _bk=bank0, _hh=hh:
                         e.matmul(_bk[:, _hh * PH:(_hh + 1) * PH],
                                  glv[:, _par * 256:_par * 256 + 128],
                                  stb[:, _h * PH:(_h + 1) * PH], start=False,
                                  stop=True))
                    b.op(ph, "tensor", lambda e, _par=par, _h=h, _bk=bank1, _hh=hh:
                         e.matmul(_bk[:, _hh * PH:(_hh + 1) * PH],
                                  gl[:, _par * 384 + 128:_par * 384 + 256],
                                  xdt4[:, _h * PH:(_h + 1) * PH],
                                  start=True, stop=False))
                    b.op(ph, "tensor", lambda e, _par=par, _h=h, _bk=bank1, _hh=hh:
                         e.matmul(_bk[:, _hh * PH:(_hh + 1) * PH],
                                  gl[:, _par * 384 + 256:_par * 384 + 384],
                                  xdt4[:, INTER + _h * PH:INTER + (_h + 1) * PH],
                                  start=False, stop=False))
                    t = b.op(ph, "tensor", lambda e, _par=par, _h=h, _bk=bank1, _hh=hh:
                             e.matmul(_bk[:, _hh * PH:(_hh + 1) * PH],
                                      glv[:, _par * 256 + 128:_par * 256 + 256],
                                      stb[:, _h * PH:(_h + 1) * PH], start=False,
                                      stop=True), inc=cPE)
                    t_ymm_l[h] = t
                    b.release(("gl", par), (cPE, t))

                csr_load(0)
                csr_load(1)
                front(0)
                t_ymm = 0
                for h in range(HH):
                    hg, hh = divmod(h, 8)
                    par2 = hg % 2
                    bank0 = [ps_a, ps_c][par2]
                    bank1 = [ps_b, ps_d][par2]
                    if hh == 0:
                        pguard(bank0)
                        pguard(bank1)
                    if h + 1 < HH:
                        front(h + 1)
                    ymms(h, bank0, bank1)
                    t_ymm = t_ymm_l[h]
                    if hh == 7:
                        b.release(("csr", par2), (cPE, cs_last_bcm[par2]))
                        if hg + 2 < 8:
                            csr_load(hg + 2)
                        b.wait(ph, "vector", cPE, t_ymm)
                        b.wait(ph, "vector", cDV, yi_tok)
                        b.op(ph, "vector", lambda e, _hg=hg, _bk=bank0:
                             e.tensor_tensor(ytl[:, _hg * 512:(_hg + 1) * 512],
                                             ytl[:, _hg * 512:(_hg + 1) * 512], _bk[:, :],
                                             ALU.add))
                        t_ye = b.op(ph, "vector", lambda e, _hg=hg, _bk=bank1:
                                    e.tensor_tensor(ytl[:, INTER + _hg * 512:INTER + (_hg + 1) * 512],
                                                    ytl[:, INTER + _hg * 512:INTER + (_hg + 1) * 512],
                                                    _bk[:, :], ALU.add), inc=cDV)
                        prel(bank0, cDV, t_ye)
                        prel(bank1, cDV, t_ye)
                b.release(("stb",), (cPE, t_ymm))
                b.release(("xdt4",), (cPE, t_ymm))
                b.release(("gtm",), (cPO, t_glv_l[HH - 1]))
                b.wait(ph, "gpsimd", cDV, cDV.n)
                ty0 = b.dma(ph, "gpsimd", "syw", y_d[(c * 2) * 128:(c * 2 + 1) * 128, :],
                            ytl[:, 0:INTER])
                ty1 = b.dma(ph, "gpsimd", "syw", y_d[(c * 2 + 1) * 128:(c * 2 + 2) * 128, :],
                            ytl[:, INTER:2 * INTER])
                y_dma_toks += [ty0, ty1]
                b.release(("ytl",), (b.q("syw"), ty1))
                if c < NCHU - 1:
                    b.wait(ph, "vector", b.q("ssr"), sc_tok)
                    b.wait(ph, "vector", cPE, t_ymm)
                    for h in range(HH):
                        st_tok = b.op(ph, "vector", lambda e, _h=h, _c=c:
                                      e.scalar_tensor_tensor(st[:, _h * PH:(_h + 1) * PH],
                                                             st[:, _h * PH:(_h + 1) * PH],
                                                             dec_bc[:, _c * HH + _h:
                                                                    _c * HH + _h + 1],
                                                             sch4[:, _h * PH:(_h + 1) * PH],
                                                             ALU.mult, ALU.add),
                                      inc=(cDV if h == HH - 1 else None))
                    b.release(("sch4",), (cDV, st_tok))
            y_done = y_dma_toks[-1]
            ph.emit(block)

        b.fence(ph, [cPE, cDV, cAC, cPO], ["syw"])
        _es6 = contextlib.ExitStack()
        with _es6:
            wz = _es6.enter_context(nc.sbuf_tensor("wz", [128, 2 * KD * 512], BF16))
            zst = _es6.enter_context(nc.sbuf_tensor("zst", [128, 2 * 512], BF16))

            def wz_dma(zc):
                par = zc % 2
                return b.dma(ph, "gpsimd", f"wz{par}",
                             wz[:, par * KD * 512:(par + 1) * KD * 512]
                             .rearrange("p (k c) -> p k c", k=KD),
                             w_in_r[:, :, zc * 512:(zc + 1) * 512])

            wz_tok = [0] * 8
            wz_tok[0] = wz_dma(0)
            wz_tok[1] = wz_dma(1)
            z_dma = []
            for zc in range(8):
                par = zc % 2
                b.wait(ph, "tensor", b.q(f"wz{par}"), wz_tok[zc])
                t_zl = 0
                for tt in range(8):
                    bank = [ps_a, ps_b][tt % 2]
                    pguard(bank)
                    for k in range(KD):
                        t_zl = b.op(ph, "tensor", lambda e, _k=k, _tt=tt, _par=par, _bk=bank:
                                    e.matmul(_bk[:, :],
                                             xn_fm[:, _k * TMAIN + _tt * 128:
                                                   _k * TMAIN + (_tt + 1) * 128],
                                             wz[:, _par * KD * 512 + _k * 512:
                                                _par * KD * 512 + (_k + 1) * 512],
                                             start=(_k == 0), stop=(_k == KD - 1)),
                                    inc=(cPE if k == KD - 1 else None))
                    b.wait(ph, "vector", cPE, t_zl)
                    b.guard(ph, "vector", ("zst", tt % 2))
                    t_ze = b.op(ph, "vector", lambda e, _tt=tt, _bk=bank:
                                e.tensor_copy(zst[:, (_tt % 2) * 512:(_tt % 2 + 1) * 512],
                                              _bk[:, :]), inc=cDV)
                    prel(bank, cDV, t_ze)
                    b.wait(ph, "sync", cDV, t_ze)
                    t_zd = b.dma(ph, "sync", f"szw{tt % 2}",
                                 z_d[tt * 128:(tt + 1) * 128, zc * 512:(zc + 1) * 512],
                                 zst[:, (tt % 2) * 512:(tt % 2 + 1) * 512])
                    z_dma.append(t_zd)
                    b.release(("zst", tt % 2), (b.q(f"szw{tt % 2}"), t_zd))
                if zc + 2 < 8:
                    b.wait(ph, "gpsimd", cPE, t_zl)
                    wz_tok[zc + 2] = wz_dma(zc + 2)
            z_fin = (b.q("szw0").n, b.q("szw1").n)
            ph.emit(block)

        b.fence(ph, [cPE, cDV, cAC, cPO], ["szw0", "szw1"])
        _es7 = contextlib.ExitStack()
        with _es7:
            ztl = _es7.enter_context(nc.sbuf_tensor("ztl", [128, INTER], F32))
            ytl6 = _es7.enter_context(nc.sbuf_tensor("ytl6", [128, INTER], F32))
            sil = _es7.enter_context(nc.sbuf_tensor("sil", [128, INTER], F32))
            nwb = _es7.enter_context(nc.sbuf_tensor("nwb", [128, INTER], BF16))
            nc6 = _es7.enter_context(nc.sbuf_tensor("nc6", [128, 8], F32))
            yn = _es7.enter_context(nc.sbuf_tensor("yn", [128, INTER], BF16))
            ynf = _es7.enter_context(nc.sbuf_tensor("ynf", [128, 32 * 512], BF16))
            wo = _es7.enter_context(nc.sbuf_tensor("wo", [128, 2 * 8 * 512], BF16))
            stg6 = _es7.enter_context(nc.sbuf_tensor("stg6", [128, 2048], BF16))

            t_ms = b.op(ph, "vector", lambda e: e.memset(sil[:, :], 0.0), inc=cDV)
            b.wait(ph, "scalar", cDV, t_ms)
            t_nw = b.dma(ph, "scalar", "nw", sil[0:1, 0:INTER], nw_row)
            b.wait(ph, "tensor", b.q("nw"), t_nw)
            nwb_last = 0
            nwb_pe = 0
            for j in range(8):
                bk = [ps_bc, ps_e][j % 2]
                pguard(bk)
                t_nj = b.op(ph, "tensor", lambda e, _j=j, _bk=bk:
                            e.matmul(_bk[:, :], onesf[0:1, :],
                                     sil[0:1, _j * 512:(_j + 1) * 512], start=True,
                                     stop=True), inc=cPE)
                nwb_pe = t_nj
                b.wait(ph, "vector", cPE, t_nj)
                nwb_last = b.op(ph, "vector", lambda e, _j=j, _bk=bk:
                                e.tensor_copy(nwb[:, _j * 512:(_j + 1) * 512], _bk[:, :]),
                                inc=cDV)
                prel(bk, cDV, nwb_last)

            w_out_r = w_out.rearrange("(k p) c -> p k c", p=128)
            s_dma6 = []
            for grp in range(2):
                tr_done = 0
                for ti in range(4):
                    tt = grp * 4 + ti
                    b.wait(ph, "gpsimd", b.q("szw0"), z_fin[0])
                    b.wait(ph, "gpsimd", b.q("szw1"), z_fin[1])
                    b.wait(ph, "gpsimd", b.q("syw"), y_done)
                    b.guard(ph, "gpsimd", ("ztl",))
                    t_z = b.dma(ph, "gpsimd", "zr6", ztl[:, :], z_d[tt * 128:(tt + 1) * 128, :])
                    b.guard(ph, "gpsimd", ("ytl6",))
                    t_y = b.dma(ph, "gpsimd", "yr6", ytl6[:, :], y_d[tt * 128:(tt + 1) * 128, :])
                    b.wait(ph, "scalar", b.q("zr6"), t_z)
                    b.wait(ph, "scalar", cPE, nwb_pe)
                    b.guard(ph, "scalar", ("sil",))
                    t_sil = b.op(ph, "scalar", lambda e: e.activation(sil[:, :], ztl[:, :],
                                                                      AF.Silu), inc=cAC)
                    b.release(("ztl",), (cAC, t_sil))
                    b.wait(ph, "vector", cAC, t_sil)
                    b.wait(ph, "vector", b.q("yr6"), t_y)
                    t_yf = b.op(ph, "vector", lambda e: e.tensor_tensor(ytl6[:, :],
                                                                        ytl6[:, :],
                                                                        sil[:, :], ALU.mult),
                                inc=cDV)
                    b.wait(ph, "scalar", cDV, t_yf)
                    t_sq6 = b.op(ph, "scalar", lambda e:
                                 e.activation(sil[:, :], ytl6[:, :], AF.Square,
                                              accum_out=nc6[:, 0:1]), inc=cAC)
                    b.release(("sil",), (cAC, t_sq6))
                    b.wait(ph, "vector", cAC, t_sq6)
                    t_v6 = b.op(ph, "vector", lambda e:
                                e.tensor_scalar(nc6[:, 1:2], nc6[:, 0:1], 1.0 / INTER,
                                                None, ALU.mult), inc=cDV)
                    b.wait(ph, "scalar", cDV, t_v6)
                    b.op(ph, "scalar", lambda e: e.activation(nc6[:, 2:3], nc6[:, 1:2],
                                                              AF.Ln, bias=EPS_RMS))
                    t_rs6 = b.op(ph, "scalar", lambda e: e.activation(nc6[:, 3:4],
                                                                      nc6[:, 2:3], AF.Exp,
                                                                      scale=-0.5), inc=cAC)
                    b.wait(ph, "vector", cAC, t_rs6)
                    b.wait(ph, "vector", cDV, nwb_last)
                    b.guard(ph, "vector", ("yn",))
                    t_yn = b.op(ph, "vector", lambda e:
                                e.scalar_tensor_tensor(yn[:, :], ytl6[:, :], nc6[:, 3:4],
                                                       nwb[:, :], ALU.mult, ALU.mult),
                                inc=cDV)
                    b.release(("ytl6",), (cDV, t_yn))
                    b.wait(ph, "tensor", cDV, t_yn)
                    b.guard(ph, "tensor", ("ynf",))
                    last_pe = 0
                    for k in range(32):
                        bk = tbanks[k % 2]
                        pguard(bk)
                        t_pe6 = b.op(ph, "tensor", lambda e, _k=k, _bk=bk:
                                     e.transpose(_bk[:, 0:128], yn[:, _k * 128:(_k + 1) * 128],
                                                 identb[:, :]), inc=cPE)
                        last_pe = t_pe6
                        b.wait(ph, "vector", cPE, t_pe6)
                        t_ev6 = b.op(ph, "vector", lambda e, _k=k, _ti=ti, _bk=bk:
                                     e.tensor_copy(ynf[:, _k * 512 + _ti * 128:
                                                       _k * 512 + (_ti + 1) * 128],
                                                   _bk[:, 0:128]), inc=cDV)
                        prel(bk, cDV, t_ev6)
                        tr_done = t_ev6
                    b.release(("yn",), (cPE, last_pe))
                b.wait(ph, "tensor", cDV, tr_done)
                banks6 = [ps_a, ps_b, ps_c, ps_d]
                order = [(dtile, kg) for dtile in range(4) for kg in range(4)]

                def wo_dma(idx):
                    dtile, kg = order[idx]
                    par = idx % 2
                    return b.dma(ph, "gpsimd", f"wo{par}",
                                 wo[:, par * 8 * 512:(par + 1) * 8 * 512]
                                 .rearrange("p (k c) -> p k c", k=8),
                                 w_out_r[:, kg * 8:(kg + 1) * 8,
                                         dtile * 512:(dtile + 1) * 512])

                wo_tok = {0: wo_dma(0), 1: wo_dma(1)}
                grp_tok = {}
                m2_last = 0
                for idx, (dtile, kg) in enumerate(order):
                    par = idx % 2
                    if kg == 0:
                        for p in banks6:
                            pguard(p)
                    b.wait(ph, "tensor", b.q(f"wo{par}"), wo_tok[idx])
                    for kk in range(8):
                        k = kg * 8 + kk
                        for ti in range(4):
                            m2_last = b.op(ph, "tensor",
                                           lambda e, _k=k, _kk=kk, _ti=ti, _par=par,
                                           _bk=banks6[ti], _st=(k == 0), _sp=(k == 31):
                                           e.matmul(_bk[:, :],
                                                    ynf[:, _k * 512 + _ti * 128:
                                                        _k * 512 + (_ti + 1) * 128],
                                                    wo[:, _par * 8 * 512 + _kk * 512:
                                                       _par * 8 * 512 + (_kk + 1) * 512],
                                                    start=_st, stop=_sp),
                                           inc=(cPE if (kk == 7 and ti == 3) else None))
                    grp_tok[idx] = m2_last
                    if idx + 2 < len(order):
                        b.wait(ph, "gpsimd", cPE, m2_last)
                        wo_tok[idx + 2] = wo_dma(idx + 2)
                    if kg == 3:
                        b.wait(ph, "vector", cPE, m2_last)
                        b.guard(ph, "vector", ("stg6",))
                        t_se6 = 0
                        for ti in range(4):
                            t_se6 = b.op(ph, "vector", lambda e, _ti=ti, _bk=banks6[ti]:
                                         e.tensor_copy(stg6[:, _ti * 512:(_ti + 1) * 512],
                                                       _bk[:, :]), inc=cDV)
                            prel(banks6[ti], cDV, t_se6)
                        b.wait(ph, "sync", cDV, t_se6)
                        t_s6 = b.dma(ph, "sync", "ssd",
                                     s_d.rearrange("(u p) q -> p u q", p=128)
                                     [:, grp * 4:(grp + 1) * 4, dtile * 512:(dtile + 1) * 512],
                                     stg6[:, :].rearrange("p (u q) -> p u q", u=4))
                        s_dma6.append(t_s6)
                        b.release(("stg6",), (b.q("ssd"), t_s6))
                b.release(("ynf",), (cPE, m2_last))
            s_done6 = s_dma6[-1]
            ph.emit(block)

        b.fence(ph, [cPE, cDV, cAC, cPO], ["ssd"])
        _es8 = contextlib.ExitStack()
        with _es8:
            tut = _es8.enter_context(nc.sbuf_tensor("tut", [128, 2 * KD * 128], BF16))
            tgt = _es8.enter_context(nc.sbuf_tensor("tgt", [128, 2 * KD * 128], BF16))
            hid = _es8.enter_context(nc.sbuf_tensor("hid", [128, 64 * 512], BF16))
            ups = _es8.enter_context(nc.sbuf_tensor("ups", [128, 2 * 512], F32))
            tdt = _es8.enter_context(nc.sbuf_tensor("tdt", [128, 2 * 8 * 512], BF16))
            gt7 = _es8.enter_context(nc.sbuf_tensor("gt7", [128, 512], BF16))
            st7 = _es8.enter_context(nc.sbuf_tensor("st7", [128, 2 * 512], BF16))
            xr7 = _es8.enter_context(nc.sbuf_tensor("xr7", [128, 2 * 512], F32))
            ot7 = _es8.enter_context(nc.sbuf_tensor("ot7", [128, 512], F32))

            tau_u_r = tau_u.ap().rearrange("(k p) c -> p k c", p=128)
            tau_g_r = tau_g.ap().rearrange("(k p) c -> p k c", p=128)
            tau_d_r = tau_d.ap().rearrange("(g p) c -> p g c", p=128)
            b.wait(ph, "sync", cc2, 1)
            b.wait(ph, "sync", cc2b, 1)
            b.wait(ph, "sync", cc2c, 1)

            def tu_dma(eg):
                par = eg % 2
                return (b.dma(ph, "sync", f"tu{par}",
                              tut[:, par * KD * 128:(par + 1) * KD * 128]
                              .rearrange("p (k c) -> p k c", k=KD),
                              tau_u_r[:, :, eg * 128:(eg + 1) * 128]),
                        b.dma(ph, "sync", f"tg{par}",
                              tgt[:, par * KD * 128:(par + 1) * KD * 128]
                              .rearrange("p (k c) -> p k c", k=KD),
                              tau_g_r[:, :, eg * 128:(eg + 1) * 128]))

            m4_last = 0
            for slab in range(2):
                tu_tok = [None] * 64
                tu_tok[0] = tu_dma(0)
                tu_tok[1] = tu_dma(1)
                if slab == 1:
                    b.wait(ph, "vector", cPE, m4_last)
                hid_done = 0
                for eg in range(64):
                    par = eg % 2
                    ubank = [ps_a, ps_c][par]
                    gbank = [ps_b, ps_d][par]
                    b.wait(ph, "tensor", b.q(f"tu{par}"), tu_tok[eg][0])
                    b.wait(ph, "tensor", b.q(f"tg{par}"), tu_tok[eg][1])
                    pguard(ubank)
                    pguard(gbank)
                    t_ug = 0
                    for k in range(KD):
                        rhs = xn_fm[:, k * TMAIN + slab * 512:k * TMAIN + slab * 512 + 512]
                        b.op(ph, "tensor", lambda e, _k=k, _par=par, _bk=ubank, _r=rhs:
                             e.matmul(_bk[:, :],
                                      tut[:, _par * KD * 128 + _k * 128:
                                          _par * KD * 128 + (_k + 1) * 128],
                                      _r, start=(_k == 0), stop=(_k == KD - 1)))
                        t_ug = b.op(ph, "tensor", lambda e, _k=k, _par=par, _bk=gbank, _r=rhs:
                                    e.matmul(_bk[:, :],
                                             tgt[:, _par * KD * 128 + _k * 128:
                                                 _par * KD * 128 + (_k + 1) * 128],
                                             _r, start=(_k == 0), stop=(_k == KD - 1)),
                                    inc=(cPE if k == KD - 1 else None))
                    if eg + 2 < 64:
                        b.wait(ph, "sync", cPE, t_ug)
                        tu_tok[eg + 2] = tu_dma(eg + 2)
                    b.wait(ph, "scalar", cPE, t_ug)
                    b.wait(ph, "scalar", cDV, scales_done)
                    b.guard(ph, "scalar", ("ups", par))
                    t_us = b.op(ph, "scalar", lambda e, _par=par, _bk=ubank:
                                e.activation(ups[:, _par * 512:(_par + 1) * 512], _bk[:, :],
                                             AF.Silu, scale=scbc[:, 0:1]), inc=cAC)
                    prel(ubank, cAC, t_us)
                    b.wait(ph, "vector", cAC, t_us)
                    hid_done = b.op(ph, "vector", lambda e, _par=par, _eg=eg, _bk=gbank:
                                    e.tensor_tensor(hid[:, _eg * 512:(_eg + 1) * 512],
                                                    ups[:, _par * 512:(_par + 1) * 512],
                                                    _bk[:, :], ALU.mult), inc=cDV)
                    prel(gbank, cDV, hid_done)
                    b.release(("ups", par), (cDV, hid_done))
                banks7 = [ps_a, ps_b, ps_c, ps_d]
                b.wait(ph, "tensor", cDV, hid_done)

                def td_dma(dtile, egr):
                    par = egr % 2
                    return b.dma(ph, "sync", f"td{par}",
                                 tdt[:, par * 8 * 512:(par + 1) * 8 * 512]
                                 .rearrange("p (k c) -> p k c", k=8),
                                 tau_d_r[:, egr * 8:(egr + 1) * 8,
                                         dtile * 512:(dtile + 1) * 512])

                for dtile in range(4):
                    td_tok = [0] * 8
                    td_tok[0] = td_dma(dtile, 0)
                    td_tok[1] = td_dma(dtile, 1)
                    for p in banks7:
                        pguard(p)
                    for egr in range(8):
                        par = egr % 2
                        b.wait(ph, "tensor", b.q(f"td{par}"), td_tok[egr])
                        for ee_ in range(8):
                            e_idx = egr * 8 + ee_
                            for ts in range(4):
                                m4_last = b.op(ph, "tensor",
                                               lambda e, _e=e_idx, _ee=ee_, _ts=ts, _par=par,
                                               _bk=banks7[ts], _st=(e_idx == 0),
                                               _sp=(e_idx == 63):
                                               e.matmul(_bk[:, :],
                                                        hid[:, _e * 512 + _ts * 128:
                                                            _e * 512 + (_ts + 1) * 128],
                                                        tdt[:, _par * 8 * 512 + _ee * 512:
                                                            _par * 8 * 512 + (_ee + 1) * 512],
                                                        start=_st, stop=_sp),
                                               inc=(cPE if (ee_ == 7 and ts == 3) else None))
                        if egr + 2 < 8:
                            b.wait(ph, "sync", cPE, m4_last)
                            td_tok[egr + 2] = td_dma(dtile, egr + 2)
                    for ts in range(4):
                        tt_g = slab * 4 + ts
                        b.wait(ph, "scalar", b.q("ssd"), s_done6)
                        b.guard(ph, "scalar", ("st7", ts % 2))
                        t_s7 = b.dma(ph, "scalar", f"s7{ts % 2}",
                                     st7[:, (ts % 2) * 512:(ts % 2 + 1) * 512],
                                     s_d[tt_g * 128:(tt_g + 1) * 128,
                                         dtile * 512:(dtile + 1) * 512])
                        b.guard(ph, "scalar", ("xr7", ts % 2))
                        t_x7 = b.dma(ph, "scalar", f"x7{ts % 2}",
                                     xr7[:, (ts % 2) * 512:(ts % 2 + 1) * 512],
                                     x_ext[THALO + tt_g * 128:THALO + (tt_g + 1) * 128,
                                           dtile * 512:(dtile + 1) * 512])
                        b.wait(ph, "scalar", cPE, m4_last)
                        b.guard(ph, "scalar", ("gt7",))
                        t_g7 = b.op(ph, "scalar", lambda e, _bk=banks7[ts]:
                                    e.activation(gt7[:, :], _bk[:, :], AF.Silu,
                                                 scale=sgsd_c[:, 0:1]), inc=cAC)
                        prel(banks7[ts], cAC, t_g7)
                        b.wait(ph, "vector", cAC, t_g7)
                        b.wait(ph, "vector", b.q(f"s7{ts % 2}"), t_s7)
                        b.wait(ph, "vector", b.q(f"x7{ts % 2}"), t_x7)
                        b.guard(ph, "vector", ("ot7",))
                        t_m1 = b.op(ph, "vector", lambda e, _ts=ts:
                                    e.tensor_tensor(ot7[:, :], gt7[:, :],
                                                    st7[:, (_ts % 2) * 512:(_ts % 2 + 1) * 512],
                                                    ALU.mult), inc=cDV)
                        b.release(("gt7",), (cDV, t_m1))
                        b.release(("st7", ts % 2), (cDV, t_m1))
                        t_o7 = b.op(ph, "vector", lambda e, _ts=ts:
                                    e.tensor_tensor(ot7[:, :], ot7[:, :],
                                                    xr7[:, (_ts % 2) * 512:(_ts % 2 + 1) * 512],
                                                    ALU.add), inc=cDV)
                        b.release(("xr7", ts % 2), (cDV, t_o7))
                        b.wait(ph, "sync", cDV, t_o7)
                        t_od = b.dma(ph, "sync", qOUT,
                                     out_d[tt_g * 128:(tt_g + 1) * 128,
                                           dtile * 512:(dtile + 1) * 512], ot7[:, :])
                        b.release(("ot7",), (qOUT, t_od))
            b.wait(ph, "sync", qOUT, qOUT.n)
            ph.emit(block)

    stack.close()
    return nc


_NC_CACHE = None


def kernel(L0_attn_hidden_states, L0_attn_W_in, L0_attn_conv_w, L0_attn_conv_b,
           L0_attn_dt_bias, L0_attn_A_log, L0_attn_D, L0_attn_norm_weight,
           L0_attn_W_out, L0_ffn_W_up, L0_ffn_W_gate, L0_ffn_W_down):
    global _NC_CACHE
    x = np.ascontiguousarray(np.asarray(L0_attn_hidden_states, np.float32))
    w_in = np.ascontiguousarray(np.asarray(L0_attn_W_in, np.float32))
    w_out = np.ascontiguousarray(np.asarray(L0_attn_W_out, np.float32))
    wu_t = np.ascontiguousarray(np.asarray(L0_ffn_W_up, np.float32).T)
    wg_t = np.ascontiguousarray(np.asarray(L0_ffn_W_gate, np.float32).T)
    wd_t = np.ascontiguousarray(np.asarray(L0_ffn_W_down, np.float32).T)
    conv_wt = np.ascontiguousarray(np.asarray(L0_attn_conv_w, np.float32).T)
    conv_bt = np.ascontiguousarray(np.asarray(L0_attn_conv_b, np.float32).reshape(CONVD, 1))
    dt_b = np.ascontiguousarray(np.asarray(L0_attn_dt_bias, np.float32).reshape(HH, 1))
    a_log = np.ascontiguousarray(np.asarray(L0_attn_A_log, np.float32).reshape(HH, 1))
    d_row = np.ascontiguousarray(
        np.repeat(np.asarray(L0_attn_D, np.float32), PH).reshape(1, INTER))
    nw_row = np.ascontiguousarray(np.asarray(L0_attn_norm_weight, np.float32)
                                  .reshape(1, INTER))
    ident_b = np.eye(128, dtype=ml_dtypes.bfloat16)
    ident_f = np.eye(128, dtype=np.float32)
    triu = np.ascontiguousarray(np.triu(np.ones((128, 128), np.float32)))

    if _NC_CACHE is None:
        _NC_CACHE = build_nc()
    nc = _NC_CACHE

    in_maps = []
    for c in range(NCORE):
        bb, j = divmod(c, 4)
        xs = x[bb, j * TMAIN:(j + 1) * TMAIN]
        if j > 0:
            halo = x[bb, j * TMAIN - THALO:j * TMAIN]
        else:
            halo = np.zeros((THALO, D), np.float32)
        x_e = np.ascontiguousarray(np.concatenate([halo, xs], axis=0))
        selt = np.zeros((8, 8), np.float32)
        pmask = np.zeros((8, 1), np.float32)
        for i in range(8):
            ib, ij = divmod(i, 4)
            if ib == bb and ij < j:
                pmask[i, 0] = 1.0
                for m in range(8):
                    mb, mj = divmod(m, 4)
                    if mb == bb and ij < mj < j:
                        selt[m, i] = 1.0
        in_maps.append({
            "x_ext": x_e, "w_in": w_in, "w_out": w_out,
            "wu_s": np.ascontiguousarray(wu_t[c * (D // NCORE):(c + 1) * (D // NCORE)]),
            "wg_s": np.ascontiguousarray(wg_t[c * (D // NCORE):(c + 1) * (D // NCORE)]),
            "wd_s": np.ascontiguousarray(wd_t[c * (EE // NCORE):(c + 1) * (EE // NCORE)]),
            "conv_wt": conv_wt, "conv_bt": conv_bt, "dt_bias": dt_b, "a_log": a_log,
            "d_row": d_row, "nw_row": nw_row,
            "ident_b": ident_b, "ident_f": ident_f, "triu_in": triu,
            "selt_in": selt, "pmask_in": pmask,
        })

    res = run_bass_kernel_spmd(nc, in_maps, core_ids=list(range(NCORE)))
    outs = [res.results[c]["out_d"] for c in range(NCORE)]
    full = np.stack([np.concatenate(outs[0:4], axis=0),
                     np.concatenate(outs[4:8], axis=0)], axis=0)
    return full.astype(np.float32)


# version 07
# speedup vs baseline: 1.0693x; 1.0188x over previous
import contextlib
import numpy as np
import ml_dtypes
import concourse.bass as bass
import concourse.mybir as mybir
from concourse.bass_utils import run_bass_kernel_spmd

F32 = mybir.dt.float32
BF16 = mybir.dt.bfloat16
AF = mybir.ActivationFunctionType
ALU = mybir.AluOpType
AX = mybir.AxisListType

NCORE = 8
TMAIN = 1024
THALO = 128
D = 2048
KD = 16
OUT_IN = 8512
INTER = 4096
CONVD = 4352
XBCD = 4416
NCT = 35
NN = 128
HH = 64
PH = 64
CHK = 256
NCHU = 4
EE = 8192
WSZ = float(D * EE)
EPS_LN = 1e-5
EPS_RMS = 1e-8
EPS_Q = 1e-8

ENGS = ("tensor", "vector", "scalar", "gpsimd", "sync")


class Sem:
    __slots__ = ("h", "name", "n")

    def __init__(self, h, name):
        self.h = h
        self.name = name
        self.n = 0


class Ph:
    def __init__(self):
        self.q = {e: [] for e in ENGS}

    def add(self, eng, fn):
        self.q[eng].append(fn)

    def emit(self, block):
        for e in ENGS:
            if self.q[e]:
                getattr(block, e)(lambda eng, fl=tuple(self.q[e]): [f(eng) for f in fl])
        self.q = {e: [] for e in ENGS}


class B:
    def __init__(self, mk):
        self.mk = mk
        self.streams = {}
        self.lastwait = {}
        self.guards = {}

    def q(self, name):
        s = self.streams.get(name)
        if s is None:
            s = Sem(self.mk(name), name)
            self.streams[name] = s
        return s

    def wait(self, ph, eng, sem, tok):
        if not tok:
            return
        key = (eng, sem.name)
        if self.lastwait.get(key, 0) >= tok:
            return
        self.lastwait[key] = tok
        ph.add(eng, lambda e, _h=sem.h, _t=tok: e.wait_ge(_h, _t))

    SELFSEM = {"vector": "selfDV", "scalar": "selfAC", "gpsimd": "selfPO"}

    def op(self, ph, eng, fn, inc=None, k=1):
        ss = self.SELFSEM.get(eng)
        if ss is not None:
            last = getattr(self, "_lastdone", None)
            if last is None:
                last = self._lastdone = {}
            lp = last.get(eng)
            if lp is not None:
                ph.add(eng, lambda e, _h=lp[0], _p=lp[1]: e.wait_ge(_h, _p))
            if inc is not None:
                inc.n += k
                tok = inc.n
                last[eng] = (inc.h, tok)
                ph.add(eng, lambda e, _f=fn, _h=inc.h, _k=k: _f(e).then_inc(_h, _k))
                return tok
            sem = self.q(ss)
            sem.n += 1
            last[eng] = (sem.h, sem.n)
            ph.add(eng, lambda e, _f=fn, _sh=sem.h: _f(e).then_inc(_sh, 1))
            return None
        if inc is not None:
            inc.n += k
            tok = inc.n
            ph.add(eng, lambda e, _f=fn, _h=inc.h, _k=k: _f(e).then_inc(_h, _k))
            return tok
        ph.add(eng, fn)
        return None

    def fence(self, ph, comp_sems, dma_streams):
        pairs = [(s, s.n) for s in comp_sems]
        pairs += [(self.q(n), self.q(n).n) for n in dma_streams if n in self.streams]
        for eng in ENGS:
            for s, t in pairs:
                self.wait(ph, eng, s, t)

    def op_cc(self, ph, fn, sem):
        sem.n += 1
        ph.add("gpsimd", lambda e, _f=fn, _h=sem.h: _f(e).then_inc(_h))
        return sem.n

    def dma(self, ph, eng, stream, dst, src, **kw):
        sem = self.q(stream) if isinstance(stream, str) else stream
        sem.n += 16
        tok = sem.n
        ph.add(eng, lambda e, _d=dst, _s=src, _h=sem.h, _kw=kw:
               e.dma_start(out=_d, in_=_s, **_kw).then_inc(_h, 16))
        return tok

    def guard(self, ph, eng, key):
        for s, t in self.guards.get(key, ()):
            self.wait(ph, eng, s, t)

    def release(self, key, *pairs):
        self.guards[key] = [p for p in pairs if p[1]]


def build_nc():
    nc = bass.Bass("TRN2", target_bir_lowering=False)
    ph = Ph()

    ap = lambda t: t.ap()
    x_ext = ap(nc.dram_tensor("x_ext", [THALO + TMAIN, D], F32, kind="ExternalInput"))
    w_in = ap(nc.dram_tensor("w_in", [D, OUT_IN], F32, kind="ExternalInput"))
    w_out = ap(nc.dram_tensor("w_out", [INTER, D], F32, kind="ExternalInput"))
    wu_s = ap(nc.dram_tensor("wu_s", [D // NCORE, EE], F32, kind="ExternalInput"))
    wg_s = ap(nc.dram_tensor("wg_s", [D // NCORE, EE], F32, kind="ExternalInput"))
    wd_s = ap(nc.dram_tensor("wd_s", [EE // NCORE, D], F32, kind="ExternalInput"))
    conv_wt = ap(nc.dram_tensor("conv_wt", [CONVD, 4], F32, kind="ExternalInput"))
    conv_bt = ap(nc.dram_tensor("conv_bt", [CONVD, 1], F32, kind="ExternalInput"))
    dt_bias = ap(nc.dram_tensor("dt_bias", [HH, 1], F32, kind="ExternalInput"))
    a_log = ap(nc.dram_tensor("a_log", [HH, 1], F32, kind="ExternalInput"))
    d_row = ap(nc.dram_tensor("d_row", [1, INTER], F32, kind="ExternalInput"))
    nw_row = ap(nc.dram_tensor("nw_row", [1, INTER], F32, kind="ExternalInput"))
    ident_b = ap(nc.dram_tensor("ident_b", [128, 128], BF16, kind="ExternalInput"))
    ident_f = ap(nc.dram_tensor("ident_f", [128, 128], F32, kind="ExternalInput"))
    triu_in = ap(nc.dram_tensor("triu_in", [128, 128], F32, kind="ExternalInput"))
    selt_in = ap(nc.dram_tensor("selt_in", [8, 8], F32, kind="ExternalInput"))
    pmask_in = ap(nc.dram_tensor("pmask_in", [8, 1], F32, kind="ExternalInput"))
    out_d = ap(nc.dram_tensor("out_d", [TMAIN, D], F32, kind="ExternalOutput"))

    x_tm_d = ap(nc.dram_tensor("x_tm_d", [TMAIN, INTER], BF16))
    s_st_d = ap(nc.dram_tensor("s_st_d", [NCHU * 128, INTER], BF16))
    y_d = ap(nc.dram_tensor("y_d", [TMAIN, INTER], BF16))
    z_d = ap(nc.dram_tensor("z_d", [TMAIN, INTER], BF16))
    s_d = ap(nc.dram_tensor("s_d", [TMAIN, D], BF16))
    cc1_i = nc.dram_tensor("cc1_i", [4, 1], F32)
    cc1_o = nc.dram_tensor("cc1_o", [4, 1], F32)
    tau_u_s = nc.dram_tensor("tau_u_s", [D // NCORE, EE], BF16)
    tau_g_s = nc.dram_tensor("tau_g_s", [D // NCORE, EE], BF16)
    tau_d_s = nc.dram_tensor("tau_d_s", [EE // NCORE, D], BF16)
    tau_u = nc.dram_tensor("tau_u", [D, EE], BF16)
    tau_g = nc.dram_tensor("tau_g", [D, EE], BF16)
    tau_d = nc.dram_tensor("tau_d", [EE, D], BF16)
    scr_d = nc.dram_tensor("scr_d", [4, 1], F32)
    w8_d = nc.dram_tensor("w8_d", [8, HH], F32)
    csf_d = nc.dram_tensor("csf_d", [NCHU * HH, CHK], F32)
    st_i = nc.dram_tensor("st_i", [129, INTER], F32)
    st_o = nc.dram_tensor("st_o", [NCORE * 129, INTER], F32)

    stack = contextlib.ExitStack()
    b = B(lambda name: stack.enter_context(nc.semaphore(name)))
    sem = {}
    for name in ("cPE", "cDV", "cAC", "cPO",
                 "cc1", "cc2", "cc2b", "cc2c", "cc3", "qOUT"):
        sem[name] = Sem(stack.enter_context(nc.semaphore(name)), name)
    cPE, cDV, cAC, cPO = sem["cPE"], sem["cDV"], sem["cAC"], sem["cPO"]
    cc1, cc2, cc3, qOUT = sem["cc1"], sem["cc2"], sem["cc3"], sem["qOUT"]
    cc2b, cc2c = sem["cc2b"], sem["cc2c"]

    sb = lambda n, s, d: stack.enter_context(nc.sbuf_tensor(n, s, d))
    pst = lambda n, s, d: stack.enter_context(nc.psum_tensor(n, s, d))

    xn_fm = sb("xn_fm", [128, KD * TMAIN], BF16)
    xn_halo = sb("xn_halo", [128, KD * 4], BF16)
    bfm = sb("bfm", [128, TMAIN], BF16)
    cfm = sb("cfm", [128, TMAIN], BF16)
    btm = sb("btm", [128, NCHU * 2 * 128], BF16)
    dt_fm = sb("dt_fm", [HH, TMAIN], F32)
    csneg_fm = sb("csneg_fm", [HH, TMAIN], F32)
    csneg_tm = sb("csneg_tm", [128, NCHU * 2 * HH], F32)
    dt_tm = sb("dt_tm", [128, NCHU * 2 * HH], F32)
    dtdec_tm = sb("dtdec_tm", [128, NCHU * 2 * HH], F32)
    dec_bc = sb("dec_bc", [128, NCHU * HH], F32)
    identb = sb("identb", [128, 128], BF16)
    identf = sb("identf", [128, 128], F32)
    triu = sb("triu", [128, 128], F32)
    ta = sb("ta", [128, 256], F32)
    tb_c = sb("tb_c", [128, 256], F32)
    onesf = sb("onesf", [128, 128], F32)
    alog_c = sb("alog_c", [HH, 1], F32)
    expa_c = sb("expa_c", [HH, 1], F32)
    dtb_c = sb("dtb_c", [HH, 1], F32)
    selt = sb("selt", [8, 8], F32)
    pmask = sb("pmask", [8, 1], F32)
    sc4 = sb("sc4", [4, 1], F32)
    scrow = sb("scrow", [1, 4], F32)
    scbc = sb("scbc", [128, 4], F32)
    sgsd_c = sb("sgsd_c", [128, 1], F32)
    negrow = sb("negrow", [1, HH], F32)
    decrow = sb("decrow", [1, HH], F32)
    lcrow = sb("lcrow", [1, HH], F32)
    cpool = sb("cpool", [128, 4], F32)
    qac = sb("qac", [128, 26], F32)
    qsc = sb("qsc", [128, 4], F32)
    cc1s = sb("cc1s", [4, 1], F32)
    qtiles = []
    for (_wn, _tn, _npc, _ncc, _wi) in (("wu_s", "tau_u_s", 2, 4, 0),
                                        ("wg_s", "tau_g_s", 2, 4, 1),
                                        ("wd_s", "tau_d_s", 8, 1, 2)):
        for _pc in range(_npc):
            for _cc in range(_ncc):
                qtiles.append((_wn, _tn, _pc, _cc, _wi))
    _cvals = [EPS_LN, 1.0, EPS_RMS]
    for _j, _v in enumerate(_cvals):
        nc.const_aps.register(F32, _v, cpool[:, _j:_j + 1])

    ps_a = pst("ps_a", [128, 512], F32)
    ps_b = pst("ps_b", [128, 512], F32)
    ps_c = pst("ps_c", [128, 512], F32)
    ps_d = pst("ps_d", [128, 512], F32)
    ps_e = pst("ps_e", [128, 512], F32)
    ps_bc = pst("ps_bc", [128, 512], F32)
    ps_tb = pst("ps_tb", [128, 1024], BF16)
    ps_tb2 = pst("ps_tb2", [128, 1024], BF16)

    block = stack.enter_context(nc.Block())

    pname = {id(ps_a): "ps_a", id(ps_b): "ps_b", id(ps_c): "ps_c", id(ps_d): "ps_d",
             id(ps_e): "ps_e", id(ps_bc): "ps_bc", id(ps_tb): "ps_tb",
             id(ps_tb2): "ps_tb2"}
    tbanks = [ps_tb, ps_tb2]

    def pguard(p, eng="tensor"):
        b.guard(ph, eng, pname[id(p)])

    def prel(p, s, t):
        b.release(pname[id(p)], (s, t))

    b.dma(ph, "sync", "cn", identb[:, :], ident_b)
    b.dma(ph, "sync", "cn", identf[:, :], ident_f)
    b.dma(ph, "sync", "cn", triu[:, :], triu_in)
    b.dma(ph, "sync", "cn", alog_c[:, :], a_log)
    b.dma(ph, "sync", "cn", dtb_c[:, :], dt_bias)
    b.dma(ph, "sync", "cn", selt[:, :], selt_in)
    t0 = b.dma(ph, "sync", "cn", pmask[:, :], pmask_in)
    b.wait(ph, "vector", b.q("cn"), t0)
    for _j, _v in enumerate(_cvals):
        b.op(ph, "vector", lambda e, _jj=_j, _vv=_v: e.memset(cpool[:, _jj:_jj + 1], _vv))
    b.op(ph, "vector", lambda e: e.memset(onesf[:, :], 1.0))
    b.op(ph, "vector", lambda e: e.tensor_copy(ta[:, 0:128], triu[:, :]))
    b.op(ph, "vector", lambda e: e.memset(ta[:, 128:256], 1.0))
    b.op(ph, "vector", lambda e: e.memset(tb_c[:, 0:128], 0.0))
    t_consts = b.op(ph, "vector", lambda e: e.tensor_copy(tb_c[:, 128:256], triu[:, :]),
                    inc=cDV)
    b.wait(ph, "scalar", b.q("cn"), t0)
    t_expa = b.op(ph, "scalar", lambda e: e.activation(expa_c[:, :], alog_c[:, :], AF.Exp),
                  inc=cAC)

    _es1 = contextlib.ExitStack()
    with _es1:
        xin = _es1.enter_context(nc.sbuf_tensor("xin", [128, 2 * D], F32))
        lnscr = _es1.enter_context(nc.sbuf_tensor("lnscr", [128, D], F32))
        lncols = _es1.enter_context(nc.sbuf_tensor("lncols", [128, 16], F32))
        xnt = _es1.enter_context(nc.sbuf_tensor("xnt", [128, 2 * D], BF16))

        x_tok = [0] * 9
        x_tok[0] = b.dma(ph, "sync", "x10", xin[:, 0:D], x_ext[0:128, :])
        x_tok[1] = b.dma(ph, "sync", "x11", xin[:, D:2 * D], x_ext[128:256, :])
        for tt in range(9):
            par = tt % 2
            xi = xin[:, par * D:(par + 1) * D]
            xo = xnt[:, par * D:(par + 1) * D]
            sm = lncols[:, 0:1]; mcol = lncols[:, 1:2]; sq = lncols[:, 2:3]
            var = lncols[:, 3:4]; msq = lncols[:, 4:5]; rstd = lncols[:, 5:6]
            nbias = lncols[:, 6:7]
            b.wait(ph, "vector", b.q(f"x1{par}"), x_tok[tt])
            b.op(ph, "vector", lambda e, _x=xi, _o=sm: e.reduce_sum(_o, _x, axis=AX.X))
            b.op(ph, "vector", lambda e, _o=mcol, _i=sm:
                 e.tensor_scalar(_o, _i, 1.0 / D, None, ALU.mult))
            b.wait(ph, "scalar", b.q(f"x1{par}"), x_tok[tt])
            t_sq = b.op(ph, "scalar", lambda e, _x=xi, _a=sq:
                        e.activation(lnscr[:, :], _x, AF.Square, accum_out=_a), inc=cAC)
            b.wait(ph, "vector", cAC, t_sq)
            b.op(ph, "vector", lambda e, _o=var, _i=sq:
                 e.tensor_scalar(_o, _i, 1.0 / D, None, ALU.mult))
            b.op(ph, "vector", lambda e, _o=msq, _m=mcol:
                 e.tensor_tensor(_o, _m, _m, ALU.mult))
            t_var = b.op(ph, "vector", lambda e, _o=var, _m=msq:
                         e.tensor_tensor(_o, _o, _m, ALU.subtract), inc=cDV)
            b.wait(ph, "scalar", cDV, t_var)
            b.op(ph, "scalar", lambda e, _o=var: e.activation(_o, _o, AF.Ln, bias=EPS_LN))
            t_rs = b.op(ph, "scalar", lambda e, _o=rstd, _i=var:
                        e.activation(_o, _i, AF.Exp, scale=-0.5), inc=cAC)
            b.wait(ph, "vector", cAC, t_rs)
            t_nb = b.op(ph, "vector", lambda e, _o=nbias, _m=mcol, _r=rstd:
                        e.scalar_tensor_tensor(_o, _m, -1.0, _r, ALU.mult, ALU.mult),
                        inc=cDV)
            b.guard(ph, "vector", ("xnt", par))
            t_xn = b.op(ph, "vector", lambda e, _x=xi, _o=xo, _r=rstd, _nb=nbias:
                        e.tensor_scalar(_o, _x, _r, _nb, ALU.mult, ALU.add), inc=cDV)
            if tt + 2 < 9:
                b.wait(ph, "sync", cDV, t_xn)
                x_tok[tt + 2] = b.dma(ph, "sync", f"x1{par}", xi,
                                      x_ext[(tt + 2) * 128:(tt + 3) * 128, :])
            b.wait(ph, "tensor", cDV, t_xn)
            last_pe = 0
            for k in range(KD):
                bk = tbanks[k % 2]
                pguard(bk)
                t_pe = b.op(ph, "tensor",
                            lambda e, _i=xo[:, k * 128:(k + 1) * 128], _bk=bk:
                            e.transpose(_bk[:, 0:128], _i, identb[:, :]), inc=cPE)
                last_pe = t_pe
                b.wait(ph, "vector", cPE, t_pe)
                if tt == 0:
                    dst = xn_halo[:, k * 4:k * 4 + 3]
                    src = bk[:, 125:128]
                else:
                    dst = xn_fm[:, k * TMAIN + (tt - 1) * 128:k * TMAIN + tt * 128]
                    src = bk[:, 0:128]
                t_ev = b.op(ph, "vector", lambda e, _o=dst, _s=src:
                            e.tensor_copy(_o, _s), inc=cDV)
                prel(bk, cDV, t_ev)
            b.release(("xnt", par), (cPE, last_pe))

        _es2 = contextlib.ExitStack()
        with _es2:
            wt1 = _es2.enter_context(nc.sbuf_tensor("wt1", [128, 2 * KD * 128], BF16))
            pre = _es2.enter_context(nc.sbuf_tensor("pre", [128, 2 * 1027], F32))
            cacc = _es2.enter_context(nc.sbuf_tensor("cacc", [128, 4 * TMAIN], F32))
            cvo = _es2.enter_context(nc.sbuf_tensor("cvo", [128, 2 * TMAIN], BF16))
            xstg = _es2.enter_context(nc.sbuf_tensor("xstg", [128, 2 * TMAIN], BF16))
            cw = _es2.enter_context(nc.sbuf_tensor("cw", [128, 2 * 4], F32))
            cb = _es2.enter_context(nc.sbuf_tensor("cb", [128, 2], F32))
            dtraw = _es2.enter_context(nc.sbuf_tensor("dtraw", [HH, TMAIN], F32))
            dtscr = _es2.enter_context(nc.sbuf_tensor("dtscr", [HH, TMAIN], F32))
            qtl1 = _es2.enter_context(nc.sbuf_tensor("qtl1", [128, 2 * D], F32))

            w_in_r = w_in.rearrange("(k p) c -> p k c", p=128)
            wsrcs = {"wu_s": wu_s, "wg_s": wg_s, "wd_s": wd_s}

            def qld1(i):
                wn, tn, pc, cc, wi = qtiles[i]
                return b.dma(ph, "scalar", f"ql{i % 2}", qtl1[:, (i % 2) * D:(i % 2 + 1) * D],
                             wsrcs[wn][pc * 128:(pc + 1) * 128, cc * D:(cc + 1) * D])

            q1_tok = [0] * len(qtiles)
            q1_tok[0] = qld1(0)
            q1_tok[1] = qld1(1)
            abs_tok = [0] * len(qtiles)

            def wt_dma(ct):
                ncols = 128 if ct < 34 else 64
                par = ct % 2
                return b.dma(ph, "gpsimd", f"wm{par}",
                             wt1[:, par * KD * 128:par * KD * 128 + KD * ncols]
                             .rearrange("p (k c) -> p k c", c=ncols),
                             w_in_r[:, :, INTER + ct * 128:INTER + ct * 128 + ncols])

            wt_tok = [0] * NCT
            wt_tok[0] = wt_dma(0)
            wt_tok[1] = wt_dma(1)
            pe_mm_tok = [0] * NCT
            conv_tok = [0] * NCT
            cv_tok = [0] * NCT
            stg_dma = [0] * NCT
            cw_tok = [0] * NCT
            cb_tok = [0] * NCT
            cw_tok[0] = b.dma(ph, "scalar", "cw0", cw[:, 0:4], conv_wt[0:128, :])
            cb_tok[0] = b.dma(ph, "scalar", "cb0", cb[:, 0:1], conv_bt[0:128, :])
            cw_tok[1] = b.dma(ph, "scalar", "cw1", cw[:, 4:8], conv_wt[128:256, :])
            cb_tok[1] = b.dma(ph, "scalar", "cb1", cb[:, 1:2], conv_bt[128:256, :])

            for ct in range(NCT):
                par = ct % 2
                ncols = 128 if ct < 34 else 64
                bank0 = [ps_a, ps_c][par]
                bank1 = [ps_b, ps_d][par]
                b.wait(ph, "tensor", b.q(f"wm{par}"), wt_tok[ct])
                pguard(bank0)
                pguard(bank1)
                pguard(ps_e)
                t_mm = 0
                for k in range(KD):
                    w_ap = wt1[:, par * KD * 128 + k * ncols:par * KD * 128 + (k + 1) * ncols]
                    st = (k == 0); sp = (k == KD - 1)
                    b.op(ph, "tensor", lambda e, _w=w_ap, _o=bank0[0:ncols, :],
                         _r=xn_fm[:, k * TMAIN:k * TMAIN + 512], _st=st, _sp=sp:
                         e.matmul(_o, _w, _r, start=_st, stop=_sp))
                    b.op(ph, "tensor", lambda e, _w=w_ap, _o=bank1[0:ncols, :],
                         _r=xn_fm[:, k * TMAIN + 512:k * TMAIN + 1024], _st=st, _sp=sp:
                         e.matmul(_o, _w, _r, start=_st, stop=_sp))
                    t_mm = b.op(ph, "tensor", lambda e, _w=w_ap, _o=ps_e[0:ncols, 0:3],
                                _r=xn_halo[:, k * 4:k * 4 + 3], _st=st, _sp=sp:
                                e.matmul(_o, _w, _r, start=_st, stop=_sp),
                                inc=(cPE if sp else None))
                pe_mm_tok[ct] = t_mm
                if ct + 2 < NCT:
                    b.wait(ph, "gpsimd", cPE, t_mm)
                    wt_tok[ct + 2] = wt_dma(ct + 2)
                if ct < len(qtiles):
                    i = ct
                    b.wait(ph, "scalar", b.q(f"ql{i % 2}"), q1_tok[i])
                    abs_tok[i] = b.op(ph, "scalar",
                                      lambda e, _i=qtl1[:, (i % 2) * D:(i % 2 + 1) * D],
                                      _a=qac[:, i:i + 1]:
                                      e.activation(_i, _i, AF.Abs, accum_out=_a), inc=cAC)
                    if i + 2 < len(qtiles):
                        b.wait(ph, "scalar", cAC, abs_tok[i])
                        q1_tok[i + 2] = qld1(i + 2)
                b.wait(ph, "scalar", cPE, t_mm)
                if ct < 34:
                    prb = pre[:, par * 1027:(par + 1) * 1027]
                    b.guard(ph, "scalar", ("pre", par))
                    t_evh = b.op(ph, "scalar", lambda e, _o=prb[:, 0:3]:
                                 e.copy(_o, ps_e[:, 0:3]), inc=cAC)
                    prel(ps_e, cAC, t_evh)
                    b.op(ph, "scalar", lambda e, _o=prb[:, 3:515], _s=bank0[:, :]:
                         e.copy(_o, _s))
                    t_evp = b.op(ph, "scalar", lambda e, _o=prb[:, 515:1027], _s=bank1[:, :]:
                                 e.copy(_o, _s), inc=cAC)
                    prel(bank0, cAC, t_evp)
                    prel(bank1, cAC, t_evp)
                    a0 = cacc[:, (par * 2) * TMAIN:(par * 2 + 1) * TMAIN]
                    a1 = cacc[:, (par * 2 + 1) * TMAIN:(par * 2 + 2) * TMAIN]
                    b.wait(ph, "vector", cAC, t_evp)
                    b.wait(ph, "vector", b.q(f"cw{par}"), cw_tok[ct])
                    if ct >= 2:
                        b.wait(ph, "vector", cAC, conv_tok[ct - 2])
                    wc = cw[:, par * 4:par * 4 + 4]
                    b.op(ph, "vector", lambda e, _o=a0, _i=prb[:, 0:1024], _w=wc[:, 0:1]:
                         e.tensor_scalar(_o, _i, _w, None, ALU.mult))
                    b.op(ph, "vector", lambda e, _o=a1, _i=prb[:, 1:1025], _w=wc[:, 1:2],
                         _a=a0: e.scalar_tensor_tensor(_o, _i, _w, _a, ALU.mult, ALU.add))
                    b.op(ph, "vector", lambda e, _o=a0, _i=prb[:, 2:1026], _w=wc[:, 2:3],
                         _a=a1: e.scalar_tensor_tensor(_o, _i, _w, _a, ALU.mult, ALU.add))
                    cv_tok[ct] = b.op(ph, "vector",
                                      lambda e, _o=a1, _i=prb[:, 3:1027], _w=wc[:, 3:4],
                                      _a=a0:
                                      e.scalar_tensor_tensor(_o, _i, _w, _a, ALU.mult,
                                                             ALU.add), inc=cDV)
                    b.release(("pre", par), (cDV, cv_tok[ct]))
                    if ct < 32:
                        cdst = cvo[:, par * TMAIN:(par + 1) * TMAIN]
                        b.guard(ph, "scalar", ("cvo", par))
                    elif ct == 32:
                        cdst = bfm[:, :]
                    else:
                        cdst = cfm[:, :]
                    b.wait(ph, "scalar", cDV, cv_tok[ct])
                    b.wait(ph, "scalar", b.q(f"cb{par}"), cb_tok[ct])
                    conv_tok[ct] = b.op(ph, "scalar",
                                        lambda e, _o=cdst, _i=a1, _b=cb[:, par:par + 1]:
                                        e.activation(_o, _i, AF.Silu, bias=_b), inc=cAC)
                    if ct + 2 < 34:
                        b.wait(ph, "scalar", cDV, cv_tok[ct])
                        cw_tok[ct + 2] = b.dma(ph, "scalar", f"cw{par}", cw[:, par * 4:par * 4 + 4],
                                               conv_wt[(ct + 2) * 128:(ct + 3) * 128, :])
                        b.wait(ph, "scalar", cAC, conv_tok[ct])
                        cb_tok[ct + 2] = b.dma(ph, "scalar", f"cb{par}", cb[:, par:par + 1],
                                               conv_bt[(ct + 2) * 128:(ct + 3) * 128, :])
                else:
                    b.op(ph, "scalar", lambda e: e.copy(dtraw[:, 0:512], bank0[0:64, :]))
                    t_evp = b.op(ph, "scalar",
                                 lambda e: e.copy(dtraw[:, 512:1024], bank1[0:64, :]),
                                 inc=cAC)
                    prel(bank0, cAC, t_evp)
                    prel(bank1, cAC, t_evp)
                    t_evh = b.op(ph, "scalar", lambda e: e.copy(lnscr[:, 0:3], ps_e[:, 0:3]),
                                 inc=cAC)
                    prel(ps_e, cAC, t_evh)
                    conv_tok[ct] = t_evp

                tct = ct - 1
                if 0 <= tct < 32:
                    tpar = tct % 2
                    b.wait(ph, "tensor", cAC, conv_tok[tct])
                    b.guard(ph, "vector", ("xstg", tpar))
                    last_pe = 0
                    ev_last = 0
                    for tt in range(8):
                        bk = tbanks[tt % 2]
                        pguard(bk)
                        t_pe = b.op(ph, "tensor",
                                    lambda e, _i=cvo[:, tpar * TMAIN + tt * 128:
                                                     tpar * TMAIN + (tt + 1) * 128], _bk=bk:
                                    e.transpose(_bk[:, 0:128], _i, identb[:, :]), inc=cPE)
                        last_pe = t_pe
                        b.wait(ph, "vector", cPE, t_pe)
                        ev_last = b.op(ph, "vector",
                                       lambda e, _o=xstg[:, tpar * TMAIN + tt * 128:
                                                         tpar * TMAIN + (tt + 1) * 128],
                                       _bk=bk: e.tensor_copy(_o, _bk[:, 0:128]), inc=cDV)
                        prel(bk, cDV, ev_last)
                    b.release(("cvo", tpar), (cPE, last_pe))
                    b.wait(ph, "sync", cDV, ev_last)
                    stg_dma[tct] = b.dma(ph, "sync", f"sxw{tpar}",
                                         x_tm_d.rearrange("(t p) (c q) -> p t q c",
                                                          p=128, q=128)[:, :, :, tct],
                                         xstg[:, tpar * TMAIN:(tpar + 1) * TMAIN]
                                         .rearrange("p (t q) -> p t q", q=128))
                    b.release(("xstg", tpar), (b.q(f"sxw{tpar}"), stg_dma[tct]))
            x_tm_fin = (b.q("sxw0").n, b.q("sxw1").n)

            b.wait(ph, "scalar", cAC, conv_tok[34])
            b.op(ph, "scalar", lambda e: e.activation(dtscr[:, :], dtraw[:, :], AF.Abs,
                                                      bias=dtb_c[:, :]))
            b.op(ph, "scalar", lambda e: e.activation(dtscr[:, :], dtscr[:, :], AF.Exp,
                                                      scale=-1.0))
            t_l = b.op(ph, "scalar", lambda e: e.activation(dtscr[:, :], dtscr[:, :],
                                                            AF.Ln, bias=1.0), inc=cAC)
            b.wait(ph, "vector", cAC, t_l)
            b.op(ph, "vector", lambda e: e.tensor_scalar(dt_fm[:, :], dtraw[:, :],
                                                         dtb_c[:, :], 0.0, ALU.add, ALU.max))
            b.op(ph, "vector", lambda e: e.tensor_tensor(dt_fm[:, :], dt_fm[:, :],
                                                         dtscr[:, :], ALU.add))
            dt_done = b.op(ph, "vector", lambda e: e.tensor_scalar(dt_fm[:, :], dt_fm[:, :],
                                                                   100.0, None, ALU.min),
                           inc=cDV)
            b.wait(ph, "vector", cAC, abs_tok[-1])
            for wi in range(3):
                b.op(ph, "vector", lambda e, _o=qsc[:, wi:wi + 1],
                     _i=qac[:, wi * 8:wi * 8 + 8]: e.reduce_sum(_o, _i, axis=AX.X))
            t_q = b.op(ph, "vector", lambda e: e.memset(qsc[:, 3:4], 0.0), inc=cDV)
            b.wait(ph, "tensor", cDV, t_q)
            pguard(ps_bc)
            t_pr = b.op(ph, "tensor", lambda e: e.matmul(ps_bc[0:4, 0:1], qsc[:, :],
                                                         onesf[:, 0:1], start=True,
                                                         stop=True), inc=cPE)
            b.wait(ph, "vector", cPE, t_pr)
            t_ev = b.op(ph, "vector", lambda e: e.tensor_copy(cc1s[:, :], ps_bc[0:4, 0:1]),
                        inc=cDV)
            prel(ps_bc, cDV, t_ev)
            b.wait(ph, "gpsimd", cDV, t_ev)
            t_st = b.dma(ph, "gpsimd", "ccs", cc1_i.ap()[:, :], cc1s[:, :])
            b.wait(ph, "gpsimd", b.q("ccs"), t_st)
            b.op_cc(ph, lambda e: e.collective_compute(
                "AllReduce", ALU.add, replica_groups=[list(range(NCORE))],
                ins=[cc1_i.ap().opt()], outs=[cc1_o.ap().opt()]), cc1)
            b.wait(ph, "gpsimd", cc1, 1)
            t_rb = b.dma(ph, "gpsimd", "ccs", sc4[:, :], cc1_o.ap()[:, :])
            b.wait(ph, "vector", b.q("ccs"), t_rb)
            t_sc = b.op(ph, "vector", lambda e: e.tensor_scalar(sc4[:, :], sc4[:, :],
                                                                1.0 / WSZ, EPS_Q, ALU.mult,
                                                                ALU.add), inc=cDV)
            b.wait(ph, "scalar", cDV, t_sc)
            t_sw = b.dma(ph, "scalar", "scr", scr_d.ap()[:, :], sc4[:, :])
            b.wait(ph, "scalar", b.q("scr"), t_sw)
            t_scr = b.dma(ph, "scalar", "scr", scrow[:, :],
                          scr_d.ap().rearrange("a o -> o a"))
            b.wait(ph, "tensor", b.q("scr"), t_scr)
            pguard(ps_bc)
            t_bc2 = b.op(ph, "tensor", lambda e: e.matmul(ps_bc[:, 0:4], onesf[0:1, :],
                                                          scrow[0:1, :], start=True,
                                                          stop=True), inc=cPE)
            b.wait(ph, "vector", cPE, t_bc2)
            t_ev = b.op(ph, "vector", lambda e: e.tensor_copy(scbc[:, :], ps_bc[:, 0:4]),
                        inc=cDV)
            prel(ps_bc, cDV, t_ev)
            b.op(ph, "vector", lambda e:
                 e.tensor_tensor(sgsd_c[:, :], scbc[:, 1:2], scbc[:, 2:3], ALU.mult))
            scales_done = b.op(ph, "vector", lambda e:
                               e.reciprocal(qsc[:, 0:3], scbc[:, 0:3]), inc=cDV)
            ph.emit(block)

        _es1.close()
        b.fence(ph, [cPE, cDV, cAC, cPO], ["sxw0", "sxw1", "ql0", "ql1"])
        _es4 = contextlib.ExitStack()
        with _es4:
            nda = _es4.enter_context(nc.sbuf_tensor("nda", [HH, CHK], F32))
            ndat = _es4.enter_context(nc.sbuf_tensor("ndat", [128, 2 * HH], F32))
            decf = _es4.enter_context(nc.sbuf_tensor("decf", [HH, CHK], F32))
            ddf = _es4.enter_context(nc.sbuf_tensor("ddf", [HH, CHK], F32))
            ncol = _es4.enter_context(nc.sbuf_tensor("ncol", [HH, 8], F32))
            xch = _es4.enter_context(nc.sbuf_tensor("xch", [128, 2 * INTER], BF16))
            xdd = _es4.enter_context(nc.sbuf_tensor("xdd", [128, 2 * INTER], BF16))
            stile = _es4.enter_context(nc.sbuf_tensor("stile", [128, INTER], BF16))
            run = _es4.enter_context(nc.sbuf_tensor("run", [128, INTER], F32))
            zrow = _es4.enter_context(nc.sbuf_tensor("zrow", [1, INTER], F32))
            qtl2 = _es4.enter_context(nc.sbuf_tensor("qtl2", [128, 2 * D], F32))
            qto = _es4.enter_context(nc.sbuf_tensor("qto", [128, 2 * D], BF16))
            qtn = _es4.enter_context(nc.sbuf_tensor("qtn", [128, D], BF16))

            tdsts = {"tau_u_s": tau_u_s, "tau_g_s": tau_g_s, "tau_d_s": tau_d_s}
            wsrcs2 = {"wu_s": wu_s, "wg_s": wg_s, "wd_s": wd_s}

            def qld2(i):
                wn, tn, pc, cc, wi = qtiles[i]
                return b.dma(ph, "scalar", f"ql{i % 2}", qtl2[:, (i % 2) * D:(i % 2 + 1) * D],
                             wsrcs2[wn][pc * 128:(pc + 1) * 128, cc * D:(cc + 1) * D])

            q2_tok = [0] * len(qtiles)
            st_tok2 = [0] * len(qtiles)
            q2_tok[0] = qld2(0)
            q2_tok[1] = qld2(1)

            def quant2(i):
                wn, tn, pc, cc, wi = qtiles[i]
                par = i % 2
                scol = qsc[:, wi:wi + 1]
                qo = qto[:, par * D:(par + 1) * D]
                b.wait(ph, "vector", b.q(f"ql{par}"), q2_tok[i])
                b.wait(ph, "vector", cDV, scales_done)
                if i >= 2:
                    b.wait(ph, "vector", b.q(f"ts{par}"), st_tok2[i - 2])
                b.op(ph, "vector", lambda e, _i=qtl2[:, par * D:(par + 1) * D], _s=scol,
                     _o=qo: e.tensor_scalar(_o, _i, _s, 0.5, ALU.mult, ALU.is_gt))
                b.op(ph, "vector", lambda e, _i=qtl2[:, par * D:(par + 1) * D], _s=scol:
                     e.tensor_scalar(qtn[:, :], _i, _s, -0.5, ALU.mult, ALU.is_lt))
                tau_tok = b.op(ph, "vector", lambda e, _o=qo:
                               e.tensor_tensor(_o, _o, qtn[:, :], ALU.subtract), inc=cDV)
                if i + 2 < len(qtiles):
                    b.wait(ph, "scalar", cDV, tau_tok)
                    q2_tok[i + 2] = qld2(i + 2)
                b.wait(ph, "sync", cDV, tau_tok)
                st_tok2[i] = b.dma(ph, "sync", f"ts{par}",
                                   tdsts[tn].ap()[pc * 128:(pc + 1) * 128,
                                                  cc * D:(cc + 1) * D], qo)

            b.wait(ph, "vector", cDV, t_consts)
            run_tok = b.op(ph, "vector", lambda e: e.memset(run[:, :], 0.0), inc=cDV)
            b.op(ph, "vector", lambda e: e.memset(lcrow[:, :], 0.0))
            t_zr = b.op(ph, "vector", lambda e: e.memset(zrow[:, :], 0.0), inc=cDV)
            b.wait(ph, "sync", cDV, t_zr)
            b.dma(ph, "sync", "sti", st_i.ap()[128:129, HH:INTER], zrow[0:1, HH:INTER])
            s_dma_toks = []
            xdd_tok = 0

            def fp32_T(src, dst_psum_region, inc_sem=None):
                kdim = src.shape[0]
                return b.op(ph, "tensor", lambda e, _s=src, _o=dst_psum_region, _k=kdim:
                            e.transpose(_o, _s, identf[0:_k, 0:_k]), inc=inc_sem)

            for c in range(NCHU):
                for i in range(c * 6, min((c + 1) * 6, len(qtiles))):
                    quant2(i)
                b.wait(ph, "scalar", b.q("sxw0"), x_tm_fin[0])
                b.wait(ph, "scalar", b.q("sxw1"), x_tm_fin[1])
                b.wait(ph, "scalar", cDV, xdd_tok)
                xch_tok = [0, 0]
                for t in range(2):
                    xch_tok[t] = b.dma(ph, "scalar", "xr3", xch[:, t * INTER:(t + 1) * INTER],
                                       x_tm_d[(c * 2 + t) * 128:(c * 2 + t + 1) * 128, :])
                b.wait(ph, "vector", cDV, dt_done)
                b.wait(ph, "vector", cAC, t_expa)
                t_nda = b.op(ph, "vector", lambda e, _c=c:
                             e.tensor_scalar(nda[:, :], dt_fm[:, _c * CHK:(_c + 1) * CHK],
                                             expa_c[:, :], None, ALU.mult), inc=cDV)
                b.wait(ph, "tensor", cDV, t_nda)
                for t in range(2):
                    pguard(ps_bc)
                    t1 = fp32_T(nda[:, t * 128:(t + 1) * 128], ps_bc[:, 0:64], cPE)
                    pguard(ps_e)
                    t2 = fp32_T(dt_fm[:, c * CHK + t * 128:c * CHK + (t + 1) * 128],
                                ps_e[:, 0:64], cPE)
                    b.wait(ph, "vector", cPE, t1)
                    t_e1 = b.op(ph, "vector", lambda e, _t=t:
                                e.tensor_copy(ndat[:, _t * 64:(_t + 1) * 64], ps_bc[:, 0:64]),
                                inc=cDV)
                    prel(ps_bc, cDV, t_e1)
                    b.wait(ph, "vector", cPE, t2)
                    t_e2 = b.op(ph, "vector", lambda e, _t=t, _c=c:
                                e.tensor_copy(dt_tm[:, (_c * 2 + _t) * HH:(_c * 2 + _t + 1) * HH],
                                              ps_e[:, 0:64]), inc=cDV)
                    prel(ps_e, cDV, t_e2)
                b.wait(ph, "tensor", cDV, t_e1)
                pguard(ps_e)
                b.op(ph, "tensor", lambda e: e.matmul(ps_e[0:HH, 0:CHK], ndat[:, 0:HH],
                                                      ta[:, :], start=True, stop=False))
                b.op(ph, "tensor", lambda e: e.matmul(ps_e[0:HH, 0:CHK], ndat[:, HH:128],
                                                      tb_c[:, :], start=False, stop=True))
                b.op(ph, "tensor", lambda e: e.matmul(ps_e[:, 256:320], triu[:, :],
                                                      ndat[:, 0:HH], start=True, stop=True))
                b.op(ph, "tensor", lambda e: e.matmul(ps_e[:, 320:384], onesf[:, :],
                                                      ndat[:, 0:HH], start=True, stop=False))
                t_cs = b.op(ph, "tensor", lambda e: e.matmul(ps_e[:, 320:384], triu[:, :],
                                                             ndat[:, HH:128], start=False,
                                                             stop=True), inc=cPE)
                b.wait(ph, "vector", cPE, t_cs)
                t_csfm = b.op(ph, "vector", lambda e, _c=c:
                              e.tensor_copy(csneg_fm[:, _c * CHK:(_c + 1) * CHK],
                                            ps_e[0:HH, 0:CHK]), inc=cDV)
                b.wait(ph, "sync", cDV, t_csfm)
                b.dma(ph, "sync", "csf", csf_d.ap()[c * HH:(c + 1) * HH, :],
                      csneg_fm[:, c * CHK:(c + 1) * CHK])
                t_cst = b.op(ph, "vector", lambda e, _c=c:
                             e.tensor_copy(csneg_tm[:, _c * 2 * HH:(_c + 1) * 2 * HH],
                                           ps_e[:, 256:384]), inc=cDV)
                prel(ps_e, cDV, t_cst)
                t_nc = b.op(ph, "vector", lambda e, _c=c:
                            e.tensor_scalar(ncol[:, _c:_c + 1],
                                            csneg_fm[:, _c * CHK + 255:_c * CHK + 256],
                                            -1.0, None, ALU.mult), inc=cDV)
                b.wait(ph, "sync", cDV, t_cst)
                b.guard(ph, "sync", ("ngr",))
                t_nr = b.dma(ph, "sync", "ngr", negrow[:, :],
                             csneg_tm[127:128, (c * 2 + 1) * HH:(c * 2 + 2) * HH])
                b.wait(ph, "scalar", b.q("ngr"), t_nr)
                t_dr = b.op(ph, "scalar", lambda e: e.activation(decrow[:, :], negrow[:, :],
                                                                 AF.Exp, scale=-1.0), inc=cAC)
                b.wait(ph, "vector", b.q("ngr"), t_nr)
                t_lcr = b.op(ph, "vector", lambda e:
                             e.scalar_tensor_tensor(lcrow[:, :], negrow[:, :], -1.0, lcrow[:, :],
                                                    ALU.mult, ALU.add), inc=cDV)
                b.release(("ngr",), (cAC, t_dr), (cDV, t_lcr))
                b.wait(ph, "tensor", cAC, t_dr)
                pguard(ps_bc)
                t_db = b.op(ph, "tensor", lambda e: e.matmul(ps_bc[:, 0:HH], onesf[0:1, :],
                                                             decrow[0:1, :], start=True,
                                                             stop=True), inc=cPE)
                b.wait(ph, "vector", cPE, t_db)
                t_dbe = b.op(ph, "vector", lambda e, _c=c:
                             e.tensor_copy(dec_bc[:, _c * HH:(_c + 1) * HH], ps_bc[:, 0:HH]),
                             inc=cDV)
                prel(ps_bc, cDV, t_dbe)
                b.wait(ph, "scalar", cDV, t_nc)
                t_df = b.op(ph, "scalar", lambda e, _c=c:
                            e.activation(decf[:, :], csneg_fm[:, _c * CHK:(_c + 1) * CHK],
                                         AF.Exp, bias=ncol[:, _c:_c + 1]), inc=cAC)
                b.wait(ph, "vector", cAC, t_df)
                t_dd = b.op(ph, "vector", lambda e, _c=c:
                            e.tensor_tensor(ddf[:, :], dt_fm[:, _c * CHK:(_c + 1) * CHK],
                                            decf[:, :], ALU.mult), inc=cDV)
                b.wait(ph, "tensor", cDV, t_dd)
                for t in range(2):
                    pguard(ps_bc)
                    t3 = fp32_T(ddf[:, t * 128:(t + 1) * 128], ps_bc[:, 0:64], cPE)
                    b.wait(ph, "vector", cPE, t3)
                    t_dde = b.op(ph, "vector", lambda e, _t=t, _c=c:
                                 e.tensor_copy(dtdec_tm[:, (_c * 2 + _t) * HH:
                                                        (_c * 2 + _t + 1) * HH],
                                               ps_bc[:, 0:64]), inc=cDV)
                    prel(ps_bc, cDV, t_dde)
                b.wait(ph, "tensor", cAC, conv_tok[32])
                for t in range(2):
                    bk = tbanks[t]
                    pguard(bk)
                    t_bt = b.op(ph, "tensor", lambda e, _c=c, _t=t, _bk=bk:
                                e.transpose(_bk[:, 0:128],
                                            bfm[:, _c * CHK + _t * 128:_c * CHK + (_t + 1) * 128],
                                            identb[:, :]), inc=cPE)
                    b.wait(ph, "vector", cPE, t_bt)
                    t_bte = b.op(ph, "vector", lambda e, _c=c, _t=t, _bk=bk:
                                 e.tensor_copy(btm[:, _c * 256 + _t * 128:_c * 256 + (_t + 1) * 128],
                                               _bk[:, 0:128]), inc=cDV)
                    prel(bk, cDV, t_bte)
                b.wait(ph, "vector", b.q("xr3"), xch_tok[1])
                b.wait(ph, "vector", cDV, t_dde)
                b.guard(ph, "vector", ("xdd",))
                b.guard(ph, "gpsimd", ("xdd",))
                b.wait(ph, "gpsimd", b.q("xr3"), xch_tok[1])
                b.wait(ph, "gpsimd", cDV, t_dde)
                xdd_tok_po = 0
                for t in range(2):
                    for h in range(HH):
                        eng7 = "vector" if h < HH // 2 else "gpsimd"
                        tk = b.op(ph, eng7, lambda e, _t=t, _h=h, _c=c:
                                  e.tensor_scalar(
                                      xdd[:, _t * INTER + _h * PH:_t * INTER + (_h + 1) * PH],
                                      xch[:, _t * INTER + _h * PH:_t * INTER + (_h + 1) * PH],
                                      dtdec_tm[:, (_c * 2 + _t) * HH + _h:
                                               (_c * 2 + _t) * HH + _h + 1],
                                      None, ALU.mult),
                                  inc=(cDV if (t == 1 and h == HH // 2 - 1) else
                                       (cPO if (t == 1 and h == HH - 1) else None)))
                        if t == 1 and h == HH // 2 - 1:
                            xdd_tok = tk
                        if t == 1 and h == HH - 1:
                            xdd_tok_po = tk
                b.wait(ph, "tensor", cDV, xdd_tok)
                b.wait(ph, "tensor", cPO, xdd_tok_po)
                b.wait(ph, "tensor", cDV, t_bte)
                st_ev_last = 0
                for hg in range(8):
                    bank = [ps_c, ps_d][hg % 2]
                    pguard(bank)
                    t_smm = 0
                    for hh in range(8):
                        h = hg * 8 + hh
                        for t in range(2):
                            t_smm = b.op(ph, "tensor",
                                         lambda e, _h=h, _t=t, _c=c, _bk=bank, _hh=hh:
                                         e.matmul(_bk[:, _hh * PH:(_hh + 1) * PH],
                                                  btm[:, (_c * 2 + _t) * 128:(_c * 2 + _t + 1) * 128],
                                                  xdd[:, _t * INTER + _h * PH:_t * INTER + (_h + 1) * PH],
                                                  start=(_t == 0), stop=(_t == 1)),
                                         inc=(cPE if (hh == 7 and t == 1) else None))
                    b.wait(ph, "scalar", cPE, t_smm)
                    b.guard(ph, "scalar", ("stile", hg))
                    t_se = b.op(ph, "scalar", lambda e, _hg=hg, _bk=bank:
                                e.copy(stile[:, _hg * 512:(_hg + 1) * 512], _bk[:, :]),
                                inc=cAC)
                    prel(bank, cAC, t_se)
                    st_ev_last = t_se
                b.release(("xdd",), (cPE, t_smm))
                b.wait(ph, "vector", cAC, st_ev_last)
                b.wait(ph, "vector", cDV, t_dbe)
                for h in range(HH):
                    run_tok = b.op(ph, "vector", lambda e, _h=h, _c=c:
                                   e.scalar_tensor_tensor(run[:, _h * PH:(_h + 1) * PH],
                                                          run[:, _h * PH:(_h + 1) * PH],
                                                          dec_bc[:, _c * HH + _h:_c * HH + _h + 1],
                                                          stile[:, _h * PH:(_h + 1) * PH],
                                                          ALU.mult, ALU.add),
                                   inc=(cDV if h == HH - 1 else None))
                b.wait(ph, "sync", cAC, st_ev_last)
                t_sd = b.dma(ph, "sync", "ssw", s_st_d[c * 128:(c + 1) * 128, :], stile[:, :])
                s_dma_toks.append(t_sd)
                for hg in range(8):
                    b.release(("stile", hg), (b.q("ssw"), t_sd), (cDV, run_tok))
            b.wait(ph, "gpsimd", b.q("ts0"), b.q("ts0").n)
            b.wait(ph, "gpsimd", b.q("ts1"), b.q("ts1").n)
            b.op_cc(ph, lambda e: e.collective_compute(
                "AllGather", ALU.bypass, replica_groups=[list(range(NCORE))],
                ins=[tau_u_s.ap().opt()], outs=[tau_u.ap().opt()]), cc2)
            b.op_cc(ph, lambda e: e.collective_compute(
                "AllGather", ALU.bypass, replica_groups=[list(range(NCORE))],
                ins=[tau_g_s.ap().opt()], outs=[tau_g.ap().opt()]), cc2b)
            b.op_cc(ph, lambda e: e.collective_compute(
                "AllGather", ALU.bypass, replica_groups=[list(range(NCORE))],
                ins=[tau_d_s.ap().opt()], outs=[tau_d.ap().opt()]), cc2c)
            b.wait(ph, "sync", cDV, run_tok)
            t_acc = b.dma(ph, "sync", "sti", st_i.ap()[0:128, :], run[:, :])
            b.wait(ph, "sync", cDV, cDV.n)
            t_lc2 = b.dma(ph, "sync", "sti", st_i.ap()[128:129, 0:HH], lcrow[:, :])
            b.wait(ph, "gpsimd", b.q("sti"), t_lc2)
            b.op_cc(ph, lambda e: e.collective_compute(
                "AllGather", ALU.bypass, replica_groups=[list(range(NCORE))],
                ins=[st_i.ap().opt()], outs=[st_o.ap().opt()]), cc3)
            ph.emit(block)

        b.fence(ph, [cPE, cDV, cAC, cPO], ["sti", "ssw", "csf", "ts0", "ts1"])
        _es5 = contextlib.ExitStack()
        with _es5:
            la8 = _es5.enter_context(nc.sbuf_tensor("la8", [8, HH], F32))
            w8 = _es5.enter_context(nc.sbuf_tensor("w8", [8, HH], F32))
            w8r = _es5.enter_context(nc.sbuf_tensor("w8r", [1, 8 * HH], F32))
            wbc = _es5.enter_context(nc.sbuf_tensor("wbc", [128, 8 * HH], F32))
            dbc = _es5.enter_context(nc.sbuf_tensor("dbc", [128, INTER], BF16))
            gac = _es5.enter_context(nc.sbuf_tensor("gac", [128, INTER], BF16))
            st = _es5.enter_context(nc.sbuf_tensor("st", [128, INTER], F32))
            stb = _es5.enter_context(nc.sbuf_tensor("stb", [128, INTER], BF16))
            xch4 = _es5.enter_context(nc.sbuf_tensor("xch4", [128, 2 * INTER], BF16))
            xdt4 = _es5.enter_context(nc.sbuf_tensor("xdt4", [128, 2 * INTER], BF16))
            sch4 = _es5.enter_context(nc.sbuf_tensor("sch4", [128, INTER], BF16))
            ytl = _es5.enter_context(nc.sbuf_tensor("ytl", [128, 2 * INTER], BF16))
            gtm = _es5.enter_context(nc.sbuf_tensor("gtm", [128, 384], BF16))
            csr = _es5.enter_context(nc.sbuf_tensor("csr", [1, 2 * 2048], F32))
            seg = _es5.enter_context(nc.sbuf_tensor("seg", [128, 2 * 384], F32))
            etl = _es5.enter_context(nc.sbuf_tensor("etl", [128, 2 * 384], BF16))
            evt = _es5.enter_context(nc.sbuf_tensor("evt", [128, 2 * 256], BF16))
            gl = _es5.enter_context(nc.sbuf_tensor("gl", [128, 2 * 384], BF16))
            glv = _es5.enter_context(nc.sbuf_tensor("glv", [128, 2 * 256], BF16))
            wz = _es5.enter_context(nc.sbuf_tensor("wz", [128, 2 * 8 * 512], BF16))
            zst = _es5.enter_context(nc.sbuf_tensor("zst", [128, 2 * 512], BF16))

            bcast_bank = [ps_bc, ps_e]

            t_dr2 = b.dma(ph, "scalar", "d4", csr[0:1, 0:INTER], d_row)
            b.wait(ph, "tensor", b.q("d4"), t_dr2)
            dbc_last = 0
            for j in range(8):
                bk = bcast_bank[j % 2]
                pguard(bk)
                t_dj = b.op(ph, "tensor", lambda e, _j=j, _bk=bk:
                            e.matmul(_bk[:, :], onesf[0:1, :],
                                     csr[0:1, _j * 512:(_j + 1) * 512],
                                     start=True, stop=True), inc=cPE)
                b.wait(ph, "vector", cPE, t_dj)
                dbc_last = b.op(ph, "vector", lambda e, _j=j, _bk=bk:
                                e.tensor_copy(dbc[:, _j * 512:(_j + 1) * 512], _bk[:, :]),
                                inc=cDV)
                prel(bk, cDV, dbc_last)
            b.wait(ph, "scalar", cc3, 1)
            t_la = b.dma(ph, "scalar", "la", la8[:, :],
                         st_o.ap().rearrange("(r q) f -> r q f", q=129)[:, 128, 0:HH])
            b.wait(ph, "tensor", b.q("la"), t_la)
            pguard(ps_bc)
            t_ld8 = b.op(ph, "tensor", lambda e: e.matmul(ps_bc[0:8, 0:HH], selt[:, :],
                                                          la8[:, :], start=True, stop=True),
                         inc=cPE)
            b.wait(ph, "scalar", cPE, t_ld8)
            t_w8 = b.op(ph, "scalar", lambda e: e.activation(w8[:, :], ps_bc[0:8, 0:HH],
                                                             AF.Exp), inc=cAC)
            prel(ps_bc, cAC, t_w8)
            b.wait(ph, "vector", cAC, t_w8)
            t_w8m = b.op(ph, "vector", lambda e: e.tensor_scalar(w8[:, :], w8[:, :],
                                                                 pmask[:, :], None, ALU.mult),
                         inc=cDV)
            b.wait(ph, "sync", cDV, t_w8m)
            t_w8w = b.dma(ph, "sync", "w8", w8_d.ap()[:, :], w8[:, :])
            b.wait(ph, "sync", b.q("w8"), t_w8w)
            t_w8r = b.dma(ph, "sync", "w8", w8r[0:1, :],
                          w8_d.ap().rearrange("r h -> (r h)").rearrange("(o f) -> o f", o=1))
            b.wait(ph, "tensor", b.q("w8"), t_w8r)
            wbc_last = 0
            for j in range(4):
                bk = bcast_bank[j % 2]
                pguard(bk)
                t_wj = b.op(ph, "tensor", lambda e, _j=j, _bk=bk:
                            e.matmul(_bk[:, 0:128], onesf[0:1, :],
                                     w8r[0:1, _j * 128:(_j + 1) * 128], start=True,
                                     stop=True), inc=cPE)
                b.wait(ph, "vector", cPE, t_wj)
                wbc_last = b.op(ph, "vector", lambda e, _j=j, _bk=bk:
                                e.tensor_copy(wbc[:, _j * 128:(_j + 1) * 128], _bk[:, 0:128]),
                                inc=cDV)
                prel(bk, cDV, wbc_last)
            def wz_dma(zc, half):
                par = half
                return b.dma(ph, "gpsimd", f"wz{par}",
                             wz[:, par * 8 * 512:(par + 1) * 8 * 512]
                             .rearrange("p (k c) -> p k c", k=8),
                             w_in_r[:, half * 8:(half + 1) * 8, zc * 512:(zc + 1) * 512])

            wz_tok = {}
            wz_tok[(0, 0)] = wz_dma(0, 0)
            wz_tok[(0, 1)] = wz_dma(0, 1)
            z_dma = []
            for zc in range(8):
                zl_half0 = 0
                t_zl = 0
                for tt in range(8):
                    bank = [ps_a, ps_b][tt % 2]
                    pguard(bank)
                    for k in range(KD):
                        half = k // 8
                        if tt == 0:
                            b.wait(ph, "tensor", b.q(f"wz{half}"), wz_tok[(zc, half)])
                        t_zl = b.op(ph, "tensor",
                                    lambda e, _k=k, _tt=tt, _bk=bank, _h2=half:
                                    e.matmul(_bk[:, :],
                                             xn_fm[:, _k * TMAIN + _tt * 128:
                                                   _k * TMAIN + (_tt + 1) * 128],
                                             wz[:, _h2 * 8 * 512 + (_k % 8) * 512:
                                                _h2 * 8 * 512 + (_k % 8 + 1) * 512],
                                             start=(_k == 0), stop=(_k == KD - 1)),
                                    inc=(cPE if k in (7, KD - 1) else None))
                        if k == 7 and tt == 7:
                            zl_half0 = cPE.n
                    b.wait(ph, "vector", cPE, t_zl)
                    b.guard(ph, "vector", ("zst", tt % 2))
                    t_ze = b.op(ph, "vector", lambda e, _tt=tt, _bk=bank:
                                e.tensor_copy(zst[:, (_tt % 2) * 512:(_tt % 2 + 1) * 512],
                                              _bk[:, :]), inc=cDV)
                    prel(bank, cDV, t_ze)
                    b.wait(ph, "sync", cDV, t_ze)
                    t_zd = b.dma(ph, "sync", f"szw{tt % 2}",
                                 z_d[tt * 128:(tt + 1) * 128, zc * 512:(zc + 1) * 512],
                                 zst[:, (tt % 2) * 512:(tt % 2 + 1) * 512])
                    z_dma.append(t_zd)
                    b.release(("zst", tt % 2), (b.q(f"szw{tt % 2}"), t_zd))
                if zc + 1 < 8:
                    b.wait(ph, "gpsimd", cPE, zl_half0)
                    wz_tok[(zc + 1, 0)] = wz_dma(zc + 1, 0)
                    b.wait(ph, "gpsimd", cPE, t_zl)
                    wz_tok[(zc + 1, 1)] = wz_dma(zc + 1, 1)
            z_fin = (b.q("szw0").n, b.q("szw1").n)

            st_tok = b.op(ph, "vector", lambda e: e.memset(st[:, 0:INTER // 2], 0.0),
                          inc=cDV)
            st_tok_po = b.op(ph, "gpsimd", lambda e: e.memset(st[:, INTER // 2:INTER], 0.0),
                             inc=cPO)
            for i in range(NCORE):
                t_gr = b.dma(ph, "gpsimd", "gac", gac[:, :], st_o.ap()[i * 129:i * 129 + 128, :])
                b.wait(ph, "vector", b.q("gac"), t_gr)
                b.wait(ph, "vector", cDV, wbc_last)
                b.wait(ph, "gpsimd", b.q("gac"), t_gr)
                b.wait(ph, "gpsimd", cDV, wbc_last)
                for h in range(HH):
                    tk = b.op(ph, "vector", lambda e, _i=i, _h=h:
                              e.scalar_tensor_tensor(st[:, _h * PH:(_h + 1) * PH],
                                                     gac[:, _h * PH:(_h + 1) * PH],
                                                     wbc[:, _i * HH + _h:_i * HH + _h + 1],
                                                     st[:, _h * PH:(_h + 1) * PH],
                                                     ALU.mult, ALU.add),
                              inc=(cDV if h == HH - 1 else None))
                    if h == HH - 1:
                        st_tok = tk
                st_tok_po = 0
                b.wait(ph, "scalar", cDV, st_tok)

            y_dma_toks = []
            for c in range(NCHU):
                b.guard(ph, "vector", ("stb",))
                b.wait(ph, "vector", cDV, st_tok)
                if st_tok_po:
                    b.wait(ph, "vector", cPO, st_tok_po)
                t_stb = b.op(ph, "vector", lambda e: e.tensor_copy(stb[:, :], st[:, :]),
                             inc=cDV)
                b.guard(ph, "scalar", ("xch4",))
                xc_tok = [0, 0]
                for t in range(2):
                    xc_tok[t] = b.dma(ph, "scalar", "xr4", xch4[:, t * INTER:(t + 1) * INTER],
                                      x_tm_d[(c * 2 + t) * 128:(c * 2 + t + 1) * 128, :])
                b.wait(ph, "scalar", b.q("ssw"), b.q("ssw").n)
                b.guard(ph, "scalar", ("sch4",))
                sc_tok = b.dma(ph, "scalar", "ssr", sch4[:, :], s_st_d[c * 128:(c + 1) * 128, :])
                b.wait(ph, "vector", b.q("xr4"), xc_tok[1])
                b.wait(ph, "gpsimd", b.q("xr4"), xc_tok[1])
                b.guard(ph, "vector", ("xdt4",))
                b.guard(ph, "gpsimd", ("xdt4",))
                xdt_tok = 0
                xdt_tok_po = 0
                for t in range(2):
                    for h in range(HH):
                        eng8 = "vector" if h < HH // 2 else "gpsimd"
                        tk = b.op(ph, eng8, lambda e, _t=t, _h=h, _c=c:
                                  e.tensor_scalar(
                                      xdt4[:, _t * INTER + _h * PH:_t * INTER + (_h + 1) * PH],
                                      xch4[:, _t * INTER + _h * PH:_t * INTER + (_h + 1) * PH],
                                      dt_tm[:, (_c * 2 + _t) * HH + _h:
                                            (_c * 2 + _t) * HH + _h + 1],
                                      None, ALU.mult),
                                  inc=(cDV if (t == 1 and h == HH // 2 - 1) else
                                       (cPO if (t == 1 and h == HH - 1) else None)))
                        if t == 1 and h == HH // 2 - 1:
                            xdt_tok = tk
                        if t == 1 and h == HH - 1:
                            xdt_tok_po = tk
                b.wait(ph, "vector", cDV, dbc_last)
                b.guard(ph, "vector", ("ytl",))
                yi_tok = 0
                for t in range(2):
                    yi_tok = b.op(ph, "vector", lambda e, _t=t:
                                  e.tensor_tensor(ytl[:, _t * INTER:(_t + 1) * INTER],
                                                  xch4[:, _t * INTER:(_t + 1) * INTER],
                                                  dbc[:, :], ALU.mult), inc=cDV)
                b.release(("xch4",), (cDV, yi_tok))
                pguard(ps_e)
                b.wait(ph, "tensor", cAC, conv_tok[33])
                t_gmm = 0
                for sh in range(2):
                    t_gmm = b.op(ph, "tensor", lambda e, _c=c, _sh=sh:
                                 e.matmul(ps_e[:, _sh * 256:(_sh + 1) * 256],
                                          bfm[:, _c * CHK + _sh * 128:_c * CHK + (_sh + 1) * 128],
                                          cfm[:, _c * CHK:(_c + 1) * CHK],
                                          start=True, stop=True),
                                 inc=(cPE if sh == 1 else None))
                b.wait(ph, "vector", cPE, t_gmm)
                b.guard(ph, "vector", ("gtm",))
                b.op(ph, "vector", lambda e: e.tensor_tensor(gtm[:, 0:128], ps_e[:, 0:128],
                                                             triu[:, :], ALU.mult))
                b.op(ph, "vector", lambda e: e.tensor_copy(gtm[:, 128:256], ps_e[:, 128:256]))
                t_gt = b.op(ph, "vector", lambda e:
                            e.tensor_tensor(gtm[:, 256:384], ps_e[:, 384:512], triu[:, :],
                                            ALU.mult), inc=cDV)
                prel(ps_e, cDV, t_gt)
                cs_tok = [0, 0]
                cs_last_bcm = [0, 0]
                t_ymm_l = [0] * (HH + 2)
                t_glv_l = [0] * (HH + 2)

                def csr_load(hg):
                    par2 = hg % 2
                    b.guard(ph, "sync", ("csr", par2))
                    b.wait(ph, "sync", b.q("csf"), b.q("csf").n)
                    cs_tok[par2] = b.dma(ph, "sync", f"cs{par2}",
                                         csr[0:1, par2 * 2048:(par2 + 1) * 2048],
                                         csf_d.ap()[c * HH + hg * 8:c * HH + (hg + 1) * 8, :]
                                         .rearrange("h l -> (h l)")
                                         .rearrange("(o f) -> o f", o=1))

                def front(h):
                    hg, hh = divmod(h, 8)
                    par2 = hg % 2
                    par = h % 2
                    bcb = bcast_bank[par]
                    b.wait(ph, "tensor", b.q(f"cs{par2}"), cs_tok[par2])
                    pguard(bcb)
                    t_bcm = b.op(ph, "tensor", lambda e, _hh=hh, _par2=par2, _bk=bcb:
                                 e.matmul(_bk[:, 0:256], onesf[0:1, :],
                                          csr[0:1, _par2 * 2048 + _hh * 256:
                                              _par2 * 2048 + (_hh + 1) * 256],
                                          start=True, stop=True), inc=cPE)
                    cs_last_bcm[par2] = t_bcm
                    b.wait(ph, "vector", cPE, t_bcm)
                    b.guard(ph, "vector", ("seg", par))
                    b.op(ph, "vector", lambda e, _par=par, _c=c, _h=h, _bk=bcb:
                         e.tensor_scalar(seg[:, _par * 384:_par * 384 + 256],
                                         _bk[:, 0:256],
                                         csneg_tm[:, (_c * 2) * HH + _h:(_c * 2) * HH + _h + 1],
                                         0.0, ALU.subtract, ALU.max))
                    t_seg = b.op(ph, "vector", lambda e, _par=par, _c=c, _h=h, _bk=bcb:
                                 e.tensor_scalar(seg[:, _par * 384 + 256:_par * 384 + 384],
                                                 _bk[:, 128:256],
                                                 csneg_tm[:, (_c * 2 + 1) * HH + _h:
                                                          (_c * 2 + 1) * HH + _h + 1],
                                                 0.0, ALU.subtract, ALU.max), inc=cDV)
                    b.wait(ph, "scalar", cDV, t_seg)
                    b.guard(ph, "scalar", ("etl", par))
                    b.op(ph, "scalar", lambda e, _par=par:
                         e.activation(etl[:, _par * 384:(_par + 1) * 384],
                                      seg[:, _par * 384:(_par + 1) * 384], AF.Exp,
                                      scale=-1.0))
                    b.guard(ph, "scalar", ("evt", par))
                    t_ev2 = b.op(ph, "scalar", lambda e, _par=par, _bk=bcb:
                                 e.activation(evt[:, _par * 256:(_par + 1) * 256],
                                              _bk[:, 0:256], AF.Exp, scale=-1.0),
                                 inc=cAC)
                    b.release(("seg", par), (cAC, t_ev2))
                    prel(bcb, cAC, t_ev2)
                    b.wait(ph, "vector", cAC, t_ev2)
                    b.wait(ph, "vector", cDV, t_gt)
                    b.guard(ph, "vector", ("gl", par))
                    b.op(ph, "vector", lambda e, _par=par:
                         e.tensor_tensor(gl[:, _par * 384:(_par + 1) * 384],
                                         etl[:, _par * 384:(_par + 1) * 384],
                                         gtm[:, :], ALU.mult))
                    t_glv = b.op(ph, "vector", lambda e, _par=par, _c=c:
                                 e.tensor_tensor(glv[:, _par * 256:(_par + 1) * 256],
                                                 evt[:, _par * 256:(_par + 1) * 256],
                                                 cfm[:, _c * CHK:(_c + 1) * CHK],
                                                 ALU.mult), inc=cDV)
                    b.release(("etl", par), (cDV, t_glv))
                    b.release(("evt", par), (cDV, t_glv))
                    t_glv_l[h] = t_glv

                def ymms(h, bank0, bank1):
                    hg, hh = divmod(h, 8)
                    par = h % 2
                    b.wait(ph, "tensor", cDV, t_glv_l[h])
                    b.wait(ph, "tensor", cDV, xdt_tok)
                    b.wait(ph, "tensor", cDV, t_stb)
                    b.op(ph, "tensor", lambda e, _par=par, _h=h, _bk=bank0, _hh=hh:
                         e.matmul(_bk[:, _hh * PH:(_hh + 1) * PH],
                                  gl[:, _par * 384:_par * 384 + 128],
                                  xdt4[:, _h * PH:(_h + 1) * PH],
                                  start=True, stop=False))
                    b.op(ph, "tensor", lambda e, _par=par, _h=h, _bk=bank0, _hh=hh:
                         e.matmul(_bk[:, _hh * PH:(_hh + 1) * PH],
                                  glv[:, _par * 256:_par * 256 + 128],
                                  stb[:, _h * PH:(_h + 1) * PH], start=False,
                                  stop=True))
                    b.op(ph, "tensor", lambda e, _par=par, _h=h, _bk=bank1, _hh=hh:
                         e.matmul(_bk[:, _hh * PH:(_hh + 1) * PH],
                                  gl[:, _par * 384 + 128:_par * 384 + 256],
                                  xdt4[:, _h * PH:(_h + 1) * PH],
                                  start=True, stop=False))
                    b.op(ph, "tensor", lambda e, _par=par, _h=h, _bk=bank1, _hh=hh:
                         e.matmul(_bk[:, _hh * PH:(_hh + 1) * PH],
                                  gl[:, _par * 384 + 256:_par * 384 + 384],
                                  xdt4[:, INTER + _h * PH:INTER + (_h + 1) * PH],
                                  start=False, stop=False))
                    t = b.op(ph, "tensor", lambda e, _par=par, _h=h, _bk=bank1, _hh=hh:
                             e.matmul(_bk[:, _hh * PH:(_hh + 1) * PH],
                                      glv[:, _par * 256 + 128:_par * 256 + 256],
                                      stb[:, _h * PH:(_h + 1) * PH], start=False,
                                      stop=True), inc=cPE)
                    t_ymm_l[h] = t
                    b.release(("gl", par), (cPE, t))

                csr_load(0)
                csr_load(1)
                front(0)
                t_ymm = 0
                for h in range(HH):
                    hg, hh = divmod(h, 8)
                    par2 = hg % 2
                    bank0 = [ps_a, ps_c][par2]
                    bank1 = [ps_b, ps_d][par2]
                    if hh == 0:
                        pguard(bank0)
                        pguard(bank1)
                    if h + 1 < HH:
                        front(h + 1)
                    ymms(h, bank0, bank1)
                    t_ymm = t_ymm_l[h]
                    if hh == 7:
                        b.release(("csr", par2), (cPE, cs_last_bcm[par2]))
                        if hg + 2 < 8:
                            csr_load(hg + 2)
                        b.wait(ph, "vector", cPE, t_ymm)
                        b.wait(ph, "vector", cDV, yi_tok)
                        b.op(ph, "vector", lambda e, _hg=hg, _bk=bank0:
                             e.tensor_tensor(ytl[:, _hg * 512:(_hg + 1) * 512],
                                             ytl[:, _hg * 512:(_hg + 1) * 512], _bk[:, :],
                                             ALU.add))
                        t_ye = b.op(ph, "vector", lambda e, _hg=hg, _bk=bank1:
                                    e.tensor_tensor(ytl[:, INTER + _hg * 512:INTER + (_hg + 1) * 512],
                                                    ytl[:, INTER + _hg * 512:INTER + (_hg + 1) * 512],
                                                    _bk[:, :], ALU.add), inc=cDV)
                        prel(bank0, cDV, t_ye)
                        prel(bank1, cDV, t_ye)
                b.release(("stb",), (cPE, t_ymm))
                b.release(("xdt4",), (cPE, t_ymm))
                b.wait(ph, "gpsimd", cPE, t_ymm)
                b.release(("gtm",), (cDV, t_glv_l[HH - 1]))
                b.wait(ph, "gpsimd", cDV, cDV.n)
                ty0 = b.dma(ph, "gpsimd", "syw", y_d[(c * 2) * 128:(c * 2 + 1) * 128, :],
                            ytl[:, 0:INTER])
                ty1 = b.dma(ph, "gpsimd", "syw", y_d[(c * 2 + 1) * 128:(c * 2 + 2) * 128, :],
                            ytl[:, INTER:2 * INTER])
                y_dma_toks += [ty0, ty1]
                b.release(("ytl",), (b.q("syw"), ty1))
                if c < NCHU - 1:
                    b.wait(ph, "vector", b.q("ssr"), sc_tok)
                    b.wait(ph, "vector", cPE, t_ymm)
                    for h in range(HH):
                        tk = b.op(ph, "vector", lambda e, _h=h, _c=c:
                                  e.scalar_tensor_tensor(st[:, _h * PH:(_h + 1) * PH],
                                                         st[:, _h * PH:(_h + 1) * PH],
                                                         dec_bc[:, _c * HH + _h:
                                                                _c * HH + _h + 1],
                                                         sch4[:, _h * PH:(_h + 1) * PH],
                                                         ALU.mult, ALU.add),
                                  inc=(cDV if h == HH - 1 else None))
                        if h == HH - 1:
                            st_tok = tk
                    st_tok_po = 0
                    b.release(("sch4",), (cDV, st_tok))
            y_done = y_dma_toks[-1]
            ph.emit(block)

        b.fence(ph, [cPE, cDV, cAC, cPO], ["szw0", "szw1", "syw"])
        _es7 = contextlib.ExitStack()
        with _es7:
            ztl = _es7.enter_context(nc.sbuf_tensor("ztl", [128, INTER], F32))
            ytl6 = _es7.enter_context(nc.sbuf_tensor("ytl6", [128, INTER], F32))
            sil = _es7.enter_context(nc.sbuf_tensor("sil", [128, INTER], F32))
            nwb = _es7.enter_context(nc.sbuf_tensor("nwb", [128, INTER], BF16))
            nc6 = _es7.enter_context(nc.sbuf_tensor("nc6", [128, 8], F32))
            yn = _es7.enter_context(nc.sbuf_tensor("yn", [128, INTER], BF16))
            ynf = _es7.enter_context(nc.sbuf_tensor("ynf", [128, 32 * 512], BF16))
            wo = _es7.enter_context(nc.sbuf_tensor("wo", [128, 2 * 8 * 512], BF16))
            stg6 = _es7.enter_context(nc.sbuf_tensor("stg6", [128, 2048], BF16))

            t_ms = b.op(ph, "vector", lambda e: e.memset(sil[:, :], 0.0), inc=cDV)
            b.wait(ph, "scalar", cDV, t_ms)
            t_nw = b.dma(ph, "scalar", "nw", sil[0:1, 0:INTER], nw_row)
            b.wait(ph, "tensor", b.q("nw"), t_nw)
            nwb_last = 0
            nwb_pe = 0
            for j in range(8):
                bk = [ps_bc, ps_e][j % 2]
                pguard(bk)
                t_nj = b.op(ph, "tensor", lambda e, _j=j, _bk=bk:
                            e.matmul(_bk[:, :], onesf[0:1, :],
                                     sil[0:1, _j * 512:(_j + 1) * 512], start=True,
                                     stop=True), inc=cPE)
                nwb_pe = t_nj
                b.wait(ph, "vector", cPE, t_nj)
                nwb_last = b.op(ph, "vector", lambda e, _j=j, _bk=bk:
                                e.tensor_copy(nwb[:, _j * 512:(_j + 1) * 512], _bk[:, :]),
                                inc=cDV)
                prel(bk, cDV, nwb_last)

            w_out_r = w_out.rearrange("(k p) c -> p k c", p=128)
            s_dma6 = []
            for grp in range(2):
                tr_done = 0
                for ti in range(4):
                    tt = grp * 4 + ti
                    b.wait(ph, "gpsimd", b.q("szw0"), z_fin[0])
                    b.wait(ph, "gpsimd", b.q("szw1"), z_fin[1])
                    b.wait(ph, "gpsimd", b.q("syw"), y_done)
                    b.guard(ph, "gpsimd", ("ztl",))
                    t_z = b.dma(ph, "gpsimd", "zr6", ztl[:, :], z_d[tt * 128:(tt + 1) * 128, :])
                    b.guard(ph, "gpsimd", ("ytl6",))
                    t_y = b.dma(ph, "gpsimd", "yr6", ytl6[:, :], y_d[tt * 128:(tt + 1) * 128, :])
                    b.wait(ph, "scalar", b.q("zr6"), t_z)
                    b.wait(ph, "scalar", cPE, nwb_pe)
                    b.guard(ph, "scalar", ("sil",))
                    t_sil = b.op(ph, "scalar", lambda e: e.activation(sil[:, :], ztl[:, :],
                                                                      AF.Silu), inc=cAC)
                    b.release(("ztl",), (cAC, t_sil))
                    b.wait(ph, "vector", cAC, t_sil)
                    b.wait(ph, "vector", b.q("yr6"), t_y)
                    t_yf = b.op(ph, "vector", lambda e: e.tensor_tensor(ytl6[:, :],
                                                                        ytl6[:, :],
                                                                        sil[:, :], ALU.mult),
                                inc=cDV)
                    b.wait(ph, "scalar", cDV, t_yf)
                    t_sq6 = b.op(ph, "scalar", lambda e:
                                 e.activation(sil[:, :], ytl6[:, :], AF.Square,
                                              accum_out=nc6[:, 0:1]), inc=cAC)
                    b.release(("sil",), (cAC, t_sq6))
                    b.wait(ph, "vector", cAC, t_sq6)
                    t_v6 = b.op(ph, "vector", lambda e:
                                e.tensor_scalar(nc6[:, 1:2], nc6[:, 0:1], 1.0 / INTER,
                                                None, ALU.mult), inc=cDV)
                    b.wait(ph, "scalar", cDV, t_v6)
                    b.op(ph, "scalar", lambda e: e.activation(nc6[:, 2:3], nc6[:, 1:2],
                                                              AF.Ln, bias=EPS_RMS))
                    t_rs6 = b.op(ph, "scalar", lambda e: e.activation(nc6[:, 3:4],
                                                                      nc6[:, 2:3], AF.Exp,
                                                                      scale=-0.5), inc=cAC)
                    b.wait(ph, "vector", cAC, t_rs6)
                    b.wait(ph, "vector", cDV, nwb_last)
                    b.guard(ph, "vector", ("yn",))
                    t_yn = b.op(ph, "vector", lambda e:
                                e.scalar_tensor_tensor(yn[:, :], ytl6[:, :], nc6[:, 3:4],
                                                       nwb[:, :], ALU.mult, ALU.mult),
                                inc=cDV)
                    b.release(("ytl6",), (cDV, t_yn))
                    b.wait(ph, "tensor", cDV, t_yn)
                    b.guard(ph, "tensor", ("ynf",))
                    last_pe = 0
                    for k in range(32):
                        bk = tbanks[k % 2]
                        pguard(bk)
                        t_pe6 = b.op(ph, "tensor", lambda e, _k=k, _bk=bk:
                                     e.transpose(_bk[:, 0:128], yn[:, _k * 128:(_k + 1) * 128],
                                                 identb[:, :]), inc=cPE)
                        last_pe = t_pe6
                        b.wait(ph, "vector", cPE, t_pe6)
                        t_ev6 = b.op(ph, "vector", lambda e, _k=k, _ti=ti, _bk=bk:
                                     e.tensor_copy(ynf[:, _k * 512 + _ti * 128:
                                                       _k * 512 + (_ti + 1) * 128],
                                                   _bk[:, 0:128]), inc=cDV)
                        prel(bk, cDV, t_ev6)
                        tr_done = t_ev6
                    b.release(("yn",), (cPE, last_pe))
                b.wait(ph, "tensor", cDV, tr_done)
                banks6 = [ps_a, ps_b, ps_c, ps_d]
                order = [(dtile, kg) for dtile in range(4) for kg in range(4)]

                def wo_dma(idx):
                    dtile, kg = order[idx]
                    par = idx % 2
                    return b.dma(ph, "gpsimd", f"wo{par}",
                                 wo[:, par * 8 * 512:(par + 1) * 8 * 512]
                                 .rearrange("p (k c) -> p k c", k=8),
                                 w_out_r[:, kg * 8:(kg + 1) * 8,
                                         dtile * 512:(dtile + 1) * 512])

                wo_tok = {0: wo_dma(0), 1: wo_dma(1)}
                grp_tok = {}
                m2_last = 0
                for idx, (dtile, kg) in enumerate(order):
                    par = idx % 2
                    if kg == 0:
                        for p in banks6:
                            pguard(p)
                    b.wait(ph, "tensor", b.q(f"wo{par}"), wo_tok[idx])
                    for kk in range(8):
                        k = kg * 8 + kk
                        for ti in range(4):
                            m2_last = b.op(ph, "tensor",
                                           lambda e, _k=k, _kk=kk, _ti=ti, _par=par,
                                           _bk=banks6[ti], _st=(k == 0), _sp=(k == 31):
                                           e.matmul(_bk[:, :],
                                                    ynf[:, _k * 512 + _ti * 128:
                                                        _k * 512 + (_ti + 1) * 128],
                                                    wo[:, _par * 8 * 512 + _kk * 512:
                                                       _par * 8 * 512 + (_kk + 1) * 512],
                                                    start=_st, stop=_sp),
                                           inc=(cPE if (kk == 7 and ti == 3) else None))
                    grp_tok[idx] = m2_last
                    if idx + 2 < len(order):
                        b.wait(ph, "gpsimd", cPE, m2_last)
                        wo_tok[idx + 2] = wo_dma(idx + 2)
                    if kg == 3:
                        b.wait(ph, "vector", cPE, m2_last)
                        b.guard(ph, "vector", ("stg6",))
                        t_se6 = 0
                        for ti in range(4):
                            t_se6 = b.op(ph, "vector", lambda e, _ti=ti, _bk=banks6[ti]:
                                         e.tensor_copy(stg6[:, _ti * 512:(_ti + 1) * 512],
                                                       _bk[:, :]), inc=cDV)
                            prel(banks6[ti], cDV, t_se6)
                        b.wait(ph, "sync", cDV, t_se6)
                        t_s6 = b.dma(ph, "sync", "ssd",
                                     s_d.rearrange("(u p) q -> p u q", p=128)
                                     [:, grp * 4:(grp + 1) * 4, dtile * 512:(dtile + 1) * 512],
                                     stg6[:, :].rearrange("p (u q) -> p u q", u=4))
                        s_dma6.append(t_s6)
                        b.release(("stg6",), (b.q("ssd"), t_s6))
                b.release(("ynf",), (cPE, m2_last))
            s_done6 = s_dma6[-1]
            ph.emit(block)

        b.fence(ph, [cPE, cDV, cAC, cPO], ["ssd"])
        _es8 = contextlib.ExitStack()
        with _es8:
            tut = _es8.enter_context(nc.sbuf_tensor("tut", [128, 2 * KD * 128], BF16))
            tgt = _es8.enter_context(nc.sbuf_tensor("tgt", [128, 2 * KD * 128], BF16))
            hid = _es8.enter_context(nc.sbuf_tensor("hid", [128, 64 * 512], BF16))
            ups = _es8.enter_context(nc.sbuf_tensor("ups", [128, 2 * 512], F32))
            tdt = _es8.enter_context(nc.sbuf_tensor("tdt", [128, 2 * 8 * 512], BF16))
            gt7 = _es8.enter_context(nc.sbuf_tensor("gt7", [128, 512], BF16))
            st7 = _es8.enter_context(nc.sbuf_tensor("st7", [128, 2 * 512], BF16))
            xr7 = _es8.enter_context(nc.sbuf_tensor("xr7", [128, 2 * 512], F32))
            ot7 = _es8.enter_context(nc.sbuf_tensor("ot7", [128, 512], F32))

            tau_u_r = tau_u.ap().rearrange("(k p) c -> p k c", p=128)
            tau_g_r = tau_g.ap().rearrange("(k p) c -> p k c", p=128)
            tau_d_r = tau_d.ap().rearrange("(g p) c -> p g c", p=128)
            b.wait(ph, "sync", cc2, 1)
            b.wait(ph, "sync", cc2b, 1)
            b.wait(ph, "sync", cc2c, 1)

            def tu_dma(eg):
                par = eg % 2
                return (b.dma(ph, "sync", f"tu{par}",
                              tut[:, par * KD * 128:(par + 1) * KD * 128]
                              .rearrange("p (k c) -> p k c", k=KD),
                              tau_u_r[:, :, eg * 128:(eg + 1) * 128]),
                        b.dma(ph, "sync", f"tg{par}",
                              tgt[:, par * KD * 128:(par + 1) * KD * 128]
                              .rearrange("p (k c) -> p k c", k=KD),
                              tau_g_r[:, :, eg * 128:(eg + 1) * 128]))

            m4_last = 0
            for slab in range(2):
                tu_tok = [None] * 64
                tu_tok[0] = tu_dma(0)
                tu_tok[1] = tu_dma(1)
                if slab == 1:
                    b.wait(ph, "vector", cPE, m4_last)
                hid_done = 0
                for eg in range(64):
                    par = eg % 2
                    ubank = [ps_a, ps_c][par]
                    gbank = [ps_b, ps_d][par]
                    b.wait(ph, "tensor", b.q(f"tu{par}"), tu_tok[eg][0])
                    b.wait(ph, "tensor", b.q(f"tg{par}"), tu_tok[eg][1])
                    pguard(ubank)
                    pguard(gbank)
                    t_ug = 0
                    for k in range(KD):
                        rhs = xn_fm[:, k * TMAIN + slab * 512:k * TMAIN + slab * 512 + 512]
                        b.op(ph, "tensor", lambda e, _k=k, _par=par, _bk=ubank, _r=rhs:
                             e.matmul(_bk[:, :],
                                      tut[:, _par * KD * 128 + _k * 128:
                                          _par * KD * 128 + (_k + 1) * 128],
                                      _r, start=(_k == 0), stop=(_k == KD - 1)))
                        t_ug = b.op(ph, "tensor", lambda e, _k=k, _par=par, _bk=gbank, _r=rhs:
                                    e.matmul(_bk[:, :],
                                             tgt[:, _par * KD * 128 + _k * 128:
                                                 _par * KD * 128 + (_k + 1) * 128],
                                             _r, start=(_k == 0), stop=(_k == KD - 1)),
                                    inc=(cPE if k == KD - 1 else None))
                    if eg + 2 < 64:
                        b.wait(ph, "sync", cPE, t_ug)
                        tu_tok[eg + 2] = tu_dma(eg + 2)
                    b.wait(ph, "scalar", cPE, t_ug)
                    b.wait(ph, "scalar", cDV, scales_done)
                    b.guard(ph, "scalar", ("ups", par))
                    t_us = b.op(ph, "scalar", lambda e, _par=par, _bk=ubank:
                                e.activation(ups[:, _par * 512:(_par + 1) * 512], _bk[:, :],
                                             AF.Silu, scale=scbc[:, 0:1]), inc=cAC)
                    prel(ubank, cAC, t_us)
                    b.wait(ph, "vector", cAC, t_us)
                    hid_done = b.op(ph, "vector", lambda e, _par=par, _eg=eg, _bk=gbank:
                                    e.tensor_tensor(hid[:, _eg * 512:(_eg + 1) * 512],
                                                    ups[:, _par * 512:(_par + 1) * 512],
                                                    _bk[:, :], ALU.mult), inc=cDV)
                    prel(gbank, cDV, hid_done)
                    b.release(("ups", par), (cDV, hid_done))
                banks7 = [ps_a, ps_b, ps_c, ps_d]
                b.wait(ph, "tensor", cDV, hid_done)

                def td_dma(dtile, egr):
                    par = egr % 2
                    return b.dma(ph, "sync", f"td{par}",
                                 tdt[:, par * 8 * 512:(par + 1) * 8 * 512]
                                 .rearrange("p (k c) -> p k c", k=8),
                                 tau_d_r[:, egr * 8:(egr + 1) * 8,
                                         dtile * 512:(dtile + 1) * 512])

                for dtile in range(4):
                    td_tok = [0] * 8
                    td_tok[0] = td_dma(dtile, 0)
                    td_tok[1] = td_dma(dtile, 1)
                    for p in banks7:
                        pguard(p)
                    for egr in range(8):
                        par = egr % 2
                        b.wait(ph, "tensor", b.q(f"td{par}"), td_tok[egr])
                        for ee_ in range(8):
                            e_idx = egr * 8 + ee_
                            for ts in range(4):
                                m4_last = b.op(ph, "tensor",
                                               lambda e, _e=e_idx, _ee=ee_, _ts=ts, _par=par,
                                               _bk=banks7[ts], _st=(e_idx == 0),
                                               _sp=(e_idx == 63):
                                               e.matmul(_bk[:, :],
                                                        hid[:, _e * 512 + _ts * 128:
                                                            _e * 512 + (_ts + 1) * 128],
                                                        tdt[:, _par * 8 * 512 + _ee * 512:
                                                            _par * 8 * 512 + (_ee + 1) * 512],
                                                        start=_st, stop=_sp),
                                               inc=(cPE if (ee_ == 7 and ts == 3) else None))
                        if egr + 2 < 8:
                            b.wait(ph, "sync", cPE, m4_last)
                            td_tok[egr + 2] = td_dma(dtile, egr + 2)
                    for ts in range(4):
                        tt_g = slab * 4 + ts
                        b.wait(ph, "scalar", b.q("ssd"), s_done6)
                        b.guard(ph, "scalar", ("st7", ts % 2))
                        t_s7 = b.dma(ph, "scalar", f"s7{ts % 2}",
                                     st7[:, (ts % 2) * 512:(ts % 2 + 1) * 512],
                                     s_d[tt_g * 128:(tt_g + 1) * 128,
                                         dtile * 512:(dtile + 1) * 512])
                        b.guard(ph, "scalar", ("xr7", ts % 2))
                        t_x7 = b.dma(ph, "scalar", f"x7{ts % 2}",
                                     xr7[:, (ts % 2) * 512:(ts % 2 + 1) * 512],
                                     x_ext[THALO + tt_g * 128:THALO + (tt_g + 1) * 128,
                                           dtile * 512:(dtile + 1) * 512])
                        b.wait(ph, "scalar", cPE, m4_last)
                        b.guard(ph, "scalar", ("gt7",))
                        t_g7 = b.op(ph, "scalar", lambda e, _bk=banks7[ts]:
                                    e.activation(gt7[:, :], _bk[:, :], AF.Silu,
                                                 scale=sgsd_c[:, 0:1]), inc=cAC)
                        prel(banks7[ts], cAC, t_g7)
                        b.wait(ph, "vector", cAC, t_g7)
                        b.wait(ph, "vector", b.q(f"s7{ts % 2}"), t_s7)
                        b.wait(ph, "vector", b.q(f"x7{ts % 2}"), t_x7)
                        b.guard(ph, "vector", ("ot7",))
                        t_m1 = b.op(ph, "vector", lambda e, _ts=ts:
                                    e.tensor_tensor(ot7[:, :], gt7[:, :],
                                                    st7[:, (_ts % 2) * 512:(_ts % 2 + 1) * 512],
                                                    ALU.mult), inc=cDV)
                        b.release(("gt7",), (cDV, t_m1))
                        b.release(("st7", ts % 2), (cDV, t_m1))
                        t_o7 = b.op(ph, "vector", lambda e, _ts=ts:
                                    e.tensor_tensor(ot7[:, :], ot7[:, :],
                                                    xr7[:, (_ts % 2) * 512:(_ts % 2 + 1) * 512],
                                                    ALU.add), inc=cDV)
                        b.release(("xr7", ts % 2), (cDV, t_o7))
                        b.wait(ph, "sync", cDV, t_o7)
                        t_od = b.dma(ph, "sync", qOUT,
                                     out_d[tt_g * 128:(tt_g + 1) * 128,
                                           dtile * 512:(dtile + 1) * 512], ot7[:, :])
                        b.release(("ot7",), (qOUT, t_od))
            b.wait(ph, "sync", qOUT, qOUT.n)
            ph.emit(block)

    stack.close()
    return nc


_NC_CACHE = None


def kernel(L0_attn_hidden_states, L0_attn_W_in, L0_attn_conv_w, L0_attn_conv_b,
           L0_attn_dt_bias, L0_attn_A_log, L0_attn_D, L0_attn_norm_weight,
           L0_attn_W_out, L0_ffn_W_up, L0_ffn_W_gate, L0_ffn_W_down):
    global _NC_CACHE
    x = np.ascontiguousarray(np.asarray(L0_attn_hidden_states, np.float32))
    w_in = np.ascontiguousarray(np.asarray(L0_attn_W_in, np.float32))
    w_out = np.ascontiguousarray(np.asarray(L0_attn_W_out, np.float32))
    wu_t = np.ascontiguousarray(np.asarray(L0_ffn_W_up, np.float32).T)
    wg_t = np.ascontiguousarray(np.asarray(L0_ffn_W_gate, np.float32).T)
    wd_t = np.ascontiguousarray(np.asarray(L0_ffn_W_down, np.float32).T)
    conv_wt = np.ascontiguousarray(np.asarray(L0_attn_conv_w, np.float32).T)
    conv_bt = np.ascontiguousarray(np.asarray(L0_attn_conv_b, np.float32).reshape(CONVD, 1))
    dt_b = np.ascontiguousarray(np.asarray(L0_attn_dt_bias, np.float32).reshape(HH, 1))
    a_log = np.ascontiguousarray(np.asarray(L0_attn_A_log, np.float32).reshape(HH, 1))
    d_row = np.ascontiguousarray(
        np.repeat(np.asarray(L0_attn_D, np.float32), PH).reshape(1, INTER))
    nw_row = np.ascontiguousarray(np.asarray(L0_attn_norm_weight, np.float32)
                                  .reshape(1, INTER))
    ident_b = np.eye(128, dtype=ml_dtypes.bfloat16)
    ident_f = np.eye(128, dtype=np.float32)
    triu = np.ascontiguousarray(np.triu(np.ones((128, 128), np.float32)))

    if _NC_CACHE is None:
        _NC_CACHE = build_nc()
    nc = _NC_CACHE

    in_maps = []
    for c in range(NCORE):
        bb, j = divmod(c, 4)
        xs = x[bb, j * TMAIN:(j + 1) * TMAIN]
        if j > 0:
            halo = x[bb, j * TMAIN - THALO:j * TMAIN]
        else:
            halo = np.zeros((THALO, D), np.float32)
        x_e = np.ascontiguousarray(np.concatenate([halo, xs], axis=0))
        selt = np.zeros((8, 8), np.float32)
        pmask = np.zeros((8, 1), np.float32)
        for i in range(8):
            ib, ij = divmod(i, 4)
            if ib == bb and ij < j:
                pmask[i, 0] = 1.0
                for m in range(8):
                    mb, mj = divmod(m, 4)
                    if mb == bb and ij < mj < j:
                        selt[m, i] = 1.0
        in_maps.append({
            "x_ext": x_e, "w_in": w_in, "w_out": w_out,
            "wu_s": np.ascontiguousarray(wu_t[c * (D // NCORE):(c + 1) * (D // NCORE)]),
            "wg_s": np.ascontiguousarray(wg_t[c * (D // NCORE):(c + 1) * (D // NCORE)]),
            "wd_s": np.ascontiguousarray(wd_t[c * (EE // NCORE):(c + 1) * (EE // NCORE)]),
            "conv_wt": conv_wt, "conv_bt": conv_bt, "dt_bias": dt_b, "a_log": a_log,
            "d_row": d_row, "nw_row": nw_row,
            "ident_b": ident_b, "ident_f": ident_f, "triu_in": triu,
            "selt_in": selt, "pmask_in": pmask,
        })

    res = run_bass_kernel_spmd(nc, in_maps, core_ids=list(range(NCORE)))
    outs = [res.results[c]["out_d"] for c in range(NCORE)]
    full = np.stack([np.concatenate(outs[0:4], axis=0),
                     np.concatenate(outs[4:8], axis=0)], axis=0)
    return full.astype(np.float32)


# version 08
# speedup vs baseline: 1.0872x; 1.0167x over previous
import contextlib
import numpy as np
import ml_dtypes
import concourse.bass as bass
import concourse.mybir as mybir
from concourse.bass_utils import run_bass_kernel_spmd

F32 = mybir.dt.float32
BF16 = mybir.dt.bfloat16
AF = mybir.ActivationFunctionType
ALU = mybir.AluOpType
AX = mybir.AxisListType

NCORE = 8
TMAIN = 1024
THALO = 128
D = 2048
KD = 16
OUT_IN = 8512
INTER = 4096
CONVD = 4352
XBCD = 4416
NCT = 35
NN = 128
HH = 64
PH = 64
CHK = 256
NCHU = 4
EE = 8192
WSZ = float(D * EE)
EPS_LN = 1e-5
EPS_RMS = 1e-8
EPS_Q = 1e-8

ENGS = ("tensor", "vector", "scalar", "gpsimd", "sync")


class Sem:
    __slots__ = ("h", "name", "n")

    def __init__(self, h, name):
        self.h = h
        self.name = name
        self.n = 0


class Ph:
    def __init__(self):
        self.q = {e: [] for e in ENGS}

    def add(self, eng, fn):
        self.q[eng].append(fn)

    def emit(self, block):
        for e in ENGS:
            if self.q[e]:
                getattr(block, e)(lambda eng, fl=tuple(self.q[e]): [f(eng) for f in fl])
        self.q = {e: [] for e in ENGS}


class B:
    def __init__(self, mk):
        self.mk = mk
        self.streams = {}
        self.lastwait = {}
        self.guards = {}

    def q(self, name):
        s = self.streams.get(name)
        if s is None:
            s = Sem(self.mk(name), name)
            self.streams[name] = s
        return s

    def wait(self, ph, eng, sem, tok):
        if not tok:
            return
        key = (eng, sem.name)
        if self.lastwait.get(key, 0) >= tok:
            return
        self.lastwait[key] = tok
        ph.add(eng, lambda e, _h=sem.h, _t=tok: e.wait_ge(_h, _t))

    SELFSEM = {"vector": "selfDV", "scalar": "selfAC", "gpsimd": "selfPO"}

    def op(self, ph, eng, fn, inc=None, k=1):
        ss = self.SELFSEM.get(eng)
        if ss is not None:
            last = getattr(self, "_lastdone", None)
            if last is None:
                last = self._lastdone = {}
            lp = last.get(eng)
            if lp is not None:
                ph.add(eng, lambda e, _h=lp[0], _p=lp[1]: e.wait_ge(_h, _p))
            if inc is not None:
                inc.n += k
                tok = inc.n
                last[eng] = (inc.h, tok)
                ph.add(eng, lambda e, _f=fn, _h=inc.h, _k=k: _f(e).then_inc(_h, _k))
                return tok
            sem = self.q(ss)
            sem.n += 1
            last[eng] = (sem.h, sem.n)
            ph.add(eng, lambda e, _f=fn, _sh=sem.h: _f(e).then_inc(_sh, 1))
            return None
        if inc is not None:
            inc.n += k
            tok = inc.n
            ph.add(eng, lambda e, _f=fn, _h=inc.h, _k=k: _f(e).then_inc(_h, _k))
            return tok
        ph.add(eng, fn)
        return None

    def fence(self, ph, comp_sems, dma_streams):
        pairs = [(s, s.n) for s in comp_sems]
        pairs += [(self.q(n), self.q(n).n) for n in dma_streams if n in self.streams]
        for eng in ENGS:
            for s, t in pairs:
                self.wait(ph, eng, s, t)

    def op_cc(self, ph, fn, sem):
        sem.n += 1
        ph.add("gpsimd", lambda e, _f=fn, _h=sem.h: _f(e).then_inc(_h))
        return sem.n

    def dma(self, ph, eng, stream, dst, src, **kw):
        sem = self.q(stream) if isinstance(stream, str) else stream
        sem.n += 16
        tok = sem.n
        ph.add(eng, lambda e, _d=dst, _s=src, _h=sem.h, _kw=kw:
               e.dma_start(out=_d, in_=_s, **_kw).then_inc(_h, 16))
        return tok

    def guard(self, ph, eng, key):
        for s, t in self.guards.get(key, ()):
            self.wait(ph, eng, s, t)

    def release(self, key, *pairs):
        self.guards[key] = [p for p in pairs if p[1]]


def build_nc():
    nc = bass.Bass("TRN2", target_bir_lowering=False)
    ph = Ph()

    ap = lambda t: t.ap()
    x_ext = ap(nc.dram_tensor("x_ext", [THALO + TMAIN, D], F32, kind="ExternalInput"))
    w_in = ap(nc.dram_tensor("w_in", [D, OUT_IN], F32, kind="ExternalInput"))
    w_out = ap(nc.dram_tensor("w_out", [INTER, D], F32, kind="ExternalInput"))
    wu_s = ap(nc.dram_tensor("wu_s", [D // NCORE, EE], F32, kind="ExternalInput"))
    wg_s = ap(nc.dram_tensor("wg_s", [D // NCORE, EE], F32, kind="ExternalInput"))
    wd_s = ap(nc.dram_tensor("wd_s", [EE // NCORE, D], F32, kind="ExternalInput"))
    conv_wt = ap(nc.dram_tensor("conv_wt", [CONVD, 4], F32, kind="ExternalInput"))
    conv_bt = ap(nc.dram_tensor("conv_bt", [CONVD, 1], F32, kind="ExternalInput"))
    dt_bias = ap(nc.dram_tensor("dt_bias", [HH, 1], F32, kind="ExternalInput"))
    a_log = ap(nc.dram_tensor("a_log", [HH, 1], F32, kind="ExternalInput"))
    d_row = ap(nc.dram_tensor("d_row", [1, INTER], F32, kind="ExternalInput"))
    nw_row = ap(nc.dram_tensor("nw_row", [1, INTER], F32, kind="ExternalInput"))
    ident_b = ap(nc.dram_tensor("ident_b", [128, 128], BF16, kind="ExternalInput"))
    ident_f = ap(nc.dram_tensor("ident_f", [128, 128], F32, kind="ExternalInput"))
    triu_in = ap(nc.dram_tensor("triu_in", [128, 128], F32, kind="ExternalInput"))
    selt_in = ap(nc.dram_tensor("selt_in", [8, 8], F32, kind="ExternalInput"))
    pmask_in = ap(nc.dram_tensor("pmask_in", [8, 1], F32, kind="ExternalInput"))
    out_d = ap(nc.dram_tensor("out_d", [TMAIN, D], F32, kind="ExternalOutput"))

    x_tm_d = ap(nc.dram_tensor("x_tm_d", [TMAIN, INTER], BF16))
    s_st_d = ap(nc.dram_tensor("s_st_d", [NCHU * 128, INTER], BF16))
    y_d = ap(nc.dram_tensor("y_d", [TMAIN, INTER], BF16))
    z_d = ap(nc.dram_tensor("z_d", [TMAIN, INTER], BF16))
    s_d = ap(nc.dram_tensor("s_d", [TMAIN, D], BF16))
    cc1_i = nc.dram_tensor("cc1_i", [4, 1], F32)
    cc1_o = nc.dram_tensor("cc1_o", [4, 1], F32)
    tau_u_s = nc.dram_tensor("tau_u_s", [D // NCORE, EE], BF16)
    tau_g_s = nc.dram_tensor("tau_g_s", [D // NCORE, EE], BF16)
    tau_d_s = nc.dram_tensor("tau_d_s", [EE // NCORE, D], BF16)
    tau_u = nc.dram_tensor("tau_u", [D, EE], BF16)
    tau_g = nc.dram_tensor("tau_g", [D, EE], BF16)
    tau_d = nc.dram_tensor("tau_d", [EE, D], BF16)
    scr_d = nc.dram_tensor("scr_d", [4, 1], F32)
    w8_d = nc.dram_tensor("w8_d", [8, HH], F32)
    csf_d = nc.dram_tensor("csf_d", [NCHU * HH, CHK], F32)
    st_i = nc.dram_tensor("st_i", [129, INTER], F32)
    st_o = nc.dram_tensor("st_o", [NCORE * 129, INTER], F32)

    stack = contextlib.ExitStack()
    b = B(lambda name: stack.enter_context(nc.semaphore(name)))
    sem = {}
    for name in ("cPE", "cDV", "cAC", "cPO",
                 "cc1", "cc2", "cc2b", "cc2c", "cc3", "qOUT"):
        sem[name] = Sem(stack.enter_context(nc.semaphore(name)), name)
    cPE, cDV, cAC, cPO = sem["cPE"], sem["cDV"], sem["cAC"], sem["cPO"]
    cc1, cc2, cc3, qOUT = sem["cc1"], sem["cc2"], sem["cc3"], sem["qOUT"]
    cc2b, cc2c = sem["cc2b"], sem["cc2c"]

    sb = lambda n, s, d: stack.enter_context(nc.sbuf_tensor(n, s, d))
    pst = lambda n, s, d: stack.enter_context(nc.psum_tensor(n, s, d))

    xn_fm = sb("xn_fm", [128, KD * TMAIN], BF16)
    xn_halo = sb("xn_halo", [128, KD * 4], BF16)
    bfm = sb("bfm", [128, TMAIN], BF16)
    cfm = sb("cfm", [128, TMAIN], BF16)
    btm = sb("btm", [128, NCHU * 2 * 128], BF16)
    dt_fm = sb("dt_fm", [HH, TMAIN], F32)
    csneg_fm = sb("csneg_fm", [HH, TMAIN], F32)
    csneg_tm = sb("csneg_tm", [128, NCHU * 2 * HH], F32)
    dt_tm = sb("dt_tm", [128, NCHU * 2 * HH], F32)
    dtdec_tm = sb("dtdec_tm", [128, NCHU * 2 * HH], F32)
    dec_bc = sb("dec_bc", [128, NCHU * HH], F32)
    identb = sb("identb", [128, 128], BF16)
    identf = sb("identf", [128, 128], F32)
    triu = sb("triu", [128, 128], F32)
    ta = sb("ta", [128, 256], F32)
    tb_c = sb("tb_c", [128, 256], F32)
    onesf = sb("onesf", [128, 128], F32)
    alog_c = sb("alog_c", [HH, 1], F32)
    expa_c = sb("expa_c", [HH, 1], F32)
    dtb_c = sb("dtb_c", [HH, 1], F32)
    selt = sb("selt", [8, 8], F32)
    pmask = sb("pmask", [8, 1], F32)
    sc4 = sb("sc4", [4, 1], F32)
    scrow = sb("scrow", [1, 4], F32)
    scbc = sb("scbc", [128, 4], F32)
    sgsd_c = sb("sgsd_c", [128, 1], F32)
    negrow = sb("negrow", [1, HH], F32)
    decrow = sb("decrow", [1, HH], F32)
    lcrow = sb("lcrow", [1, HH], F32)
    cpool = sb("cpool", [128, 4], F32)
    qac = sb("qac", [128, 26], F32)
    qsc = sb("qsc", [128, 4], F32)
    cc1s = sb("cc1s", [4, 1], F32)
    qtiles = []
    for (_wn, _tn, _npc, _ncc, _wi) in (("wu_s", "tau_u_s", 2, 4, 0),
                                        ("wg_s", "tau_g_s", 2, 4, 1),
                                        ("wd_s", "tau_d_s", 8, 1, 2)):
        for _pc in range(_npc):
            for _cc in range(_ncc):
                qtiles.append((_wn, _tn, _pc, _cc, _wi))
    _cvals = [EPS_LN, 1.0, EPS_RMS]
    for _j, _v in enumerate(_cvals):
        nc.const_aps.register(F32, _v, cpool[:, _j:_j + 1])

    ps_a = pst("ps_a", [128, 512], F32)
    ps_b = pst("ps_b", [128, 512], F32)
    ps_c = pst("ps_c", [128, 512], F32)
    ps_d = pst("ps_d", [128, 512], F32)
    ps_e = pst("ps_e", [128, 512], F32)
    ps_bc = pst("ps_bc", [128, 512], F32)
    ps_tb = pst("ps_tb", [128, 1024], BF16)
    ps_tb2 = pst("ps_tb2", [128, 1024], BF16)

    block = stack.enter_context(nc.Block())

    pname = {id(ps_a): "ps_a", id(ps_b): "ps_b", id(ps_c): "ps_c", id(ps_d): "ps_d",
             id(ps_e): "ps_e", id(ps_bc): "ps_bc", id(ps_tb): "ps_tb",
             id(ps_tb2): "ps_tb2"}
    tbanks = [ps_tb, ps_tb2]

    def pguard(p, eng="tensor"):
        b.guard(ph, eng, pname[id(p)])

    def prel(p, s, t):
        b.release(pname[id(p)], (s, t))

    b.dma(ph, "sync", "cn", identb[:, :], ident_b)
    b.dma(ph, "sync", "cn", identf[:, :], ident_f)
    b.dma(ph, "sync", "cn", triu[:, :], triu_in)
    b.dma(ph, "sync", "cn", alog_c[:, :], a_log)
    b.dma(ph, "sync", "cn", dtb_c[:, :], dt_bias)
    b.dma(ph, "sync", "cn", selt[:, :], selt_in)
    t0 = b.dma(ph, "sync", "cn", pmask[:, :], pmask_in)
    b.wait(ph, "vector", b.q("cn"), t0)
    for _j, _v in enumerate(_cvals):
        b.op(ph, "vector", lambda e, _jj=_j, _vv=_v: e.memset(cpool[:, _jj:_jj + 1], _vv))
    b.op(ph, "vector", lambda e: e.memset(onesf[:, :], 1.0))
    b.op(ph, "vector", lambda e: e.tensor_copy(ta[:, 0:128], triu[:, :]))
    b.op(ph, "vector", lambda e: e.memset(ta[:, 128:256], 1.0))
    b.op(ph, "vector", lambda e: e.memset(tb_c[:, 0:128], 0.0))
    t_consts = b.op(ph, "vector", lambda e: e.tensor_copy(tb_c[:, 128:256], triu[:, :]),
                    inc=cDV)
    b.wait(ph, "scalar", b.q("cn"), t0)
    t_expa = b.op(ph, "scalar", lambda e: e.activation(expa_c[:, :], alog_c[:, :], AF.Exp),
                  inc=cAC)

    _es1 = contextlib.ExitStack()
    with _es1:
        xin = _es1.enter_context(nc.sbuf_tensor("xin", [128, 2 * D], F32))
        lnscr = _es1.enter_context(nc.sbuf_tensor("lnscr", [128, D], F32))
        lncols = _es1.enter_context(nc.sbuf_tensor("lncols", [128, 16], F32))
        xnt = _es1.enter_context(nc.sbuf_tensor("xnt", [128, 2 * D], BF16))

        x_tok = [0] * 9
        x_tok[0] = b.dma(ph, "sync", "x10", xin[:, 0:D], x_ext[0:128, :])
        x_tok[1] = b.dma(ph, "sync", "x11", xin[:, D:2 * D], x_ext[128:256, :])
        for tt in range(9):
            par = tt % 2
            xi = xin[:, par * D:(par + 1) * D]
            xo = xnt[:, par * D:(par + 1) * D]
            sm = lncols[:, 0:1]; mcol = lncols[:, 1:2]; sq = lncols[:, 2:3]
            var = lncols[:, 3:4]; msq = lncols[:, 4:5]; rstd = lncols[:, 5:6]
            nbias = lncols[:, 6:7]
            b.wait(ph, "vector", b.q(f"x1{par}"), x_tok[tt])
            b.op(ph, "vector", lambda e, _x=xi, _o=sm: e.reduce_sum(_o, _x, axis=AX.X))
            b.op(ph, "vector", lambda e, _o=mcol, _i=sm:
                 e.tensor_scalar(_o, _i, 1.0 / D, None, ALU.mult))
            b.wait(ph, "scalar", b.q(f"x1{par}"), x_tok[tt])
            t_sq = b.op(ph, "scalar", lambda e, _x=xi, _a=sq:
                        e.activation(lnscr[:, :], _x, AF.Square, accum_out=_a), inc=cAC)
            b.wait(ph, "vector", cAC, t_sq)
            b.op(ph, "vector", lambda e, _o=var, _i=sq:
                 e.tensor_scalar(_o, _i, 1.0 / D, None, ALU.mult))
            b.op(ph, "vector", lambda e, _o=msq, _m=mcol:
                 e.tensor_tensor(_o, _m, _m, ALU.mult))
            t_var = b.op(ph, "vector", lambda e, _o=var, _m=msq:
                         e.tensor_tensor(_o, _o, _m, ALU.subtract), inc=cDV)
            b.wait(ph, "scalar", cDV, t_var)
            b.op(ph, "scalar", lambda e, _o=var: e.activation(_o, _o, AF.Ln, bias=EPS_LN))
            t_rs = b.op(ph, "scalar", lambda e, _o=rstd, _i=var:
                        e.activation(_o, _i, AF.Exp, scale=-0.5), inc=cAC)
            b.wait(ph, "vector", cAC, t_rs)
            t_nb = b.op(ph, "vector", lambda e, _o=nbias, _m=mcol, _r=rstd:
                        e.scalar_tensor_tensor(_o, _m, -1.0, _r, ALU.mult, ALU.mult),
                        inc=cDV)
            b.guard(ph, "vector", ("xnt", par))
            t_xn = b.op(ph, "vector", lambda e, _x=xi, _o=xo, _r=rstd, _nb=nbias:
                        e.tensor_scalar(_o, _x, _r, _nb, ALU.mult, ALU.add), inc=cDV)
            if tt + 2 < 9:
                b.wait(ph, "sync", cDV, t_xn)
                x_tok[tt + 2] = b.dma(ph, "sync", f"x1{par}", xi,
                                      x_ext[(tt + 2) * 128:(tt + 3) * 128, :])
            b.wait(ph, "tensor", cDV, t_xn)
            last_pe = 0
            for k in range(KD):
                bk = tbanks[k % 2]
                pguard(bk)
                t_pe = b.op(ph, "tensor",
                            lambda e, _i=xo[:, k * 128:(k + 1) * 128], _bk=bk:
                            e.transpose(_bk[:, 0:128], _i, identb[:, :]), inc=cPE)
                last_pe = t_pe
                b.wait(ph, "vector", cPE, t_pe)
                if tt == 0:
                    dst = xn_halo[:, k * 4:k * 4 + 3]
                    src = bk[:, 125:128]
                else:
                    dst = xn_fm[:, k * TMAIN + (tt - 1) * 128:k * TMAIN + tt * 128]
                    src = bk[:, 0:128]
                t_ev = b.op(ph, "vector", lambda e, _o=dst, _s=src:
                            e.tensor_copy(_o, _s), inc=cDV)
                prel(bk, cDV, t_ev)
            b.release(("xnt", par), (cPE, last_pe))

        _es2 = contextlib.ExitStack()
        with _es2:
            wt1 = _es2.enter_context(nc.sbuf_tensor("wt1", [128, 2 * KD * 128], BF16))
            pre = _es2.enter_context(nc.sbuf_tensor("pre", [128, 2 * 1027], F32))
            cacc = _es2.enter_context(nc.sbuf_tensor("cacc", [128, 4 * TMAIN], F32))
            cvo = _es2.enter_context(nc.sbuf_tensor("cvo", [128, 2 * TMAIN], BF16))
            xstg = _es2.enter_context(nc.sbuf_tensor("xstg", [128, 2 * TMAIN], BF16))
            cw = _es2.enter_context(nc.sbuf_tensor("cw", [128, 2 * 4], F32))
            cb = _es2.enter_context(nc.sbuf_tensor("cb", [128, 2], F32))
            dtraw = _es2.enter_context(nc.sbuf_tensor("dtraw", [HH, TMAIN], F32))
            dtscr = _es2.enter_context(nc.sbuf_tensor("dtscr", [HH, TMAIN], F32))
            qtl1 = _es2.enter_context(nc.sbuf_tensor("qtl1", [128, 2 * D], F32))

            w_in_r = w_in.rearrange("(k p) c -> p k c", p=128)
            wsrcs = {"wu_s": wu_s, "wg_s": wg_s, "wd_s": wd_s}

            def qld1(i):
                wn, tn, pc, cc, wi = qtiles[i]
                return b.dma(ph, "sync", f"ql{i % 2}", qtl1[:, (i % 2) * D:(i % 2 + 1) * D],
                             wsrcs[wn][pc * 128:(pc + 1) * 128, cc * D:(cc + 1) * D])

            q1_tok = [0] * len(qtiles)
            q1_tok[0] = qld1(0)
            q1_tok[1] = qld1(1)
            abs_tok = [0] * len(qtiles)

            def wt_dma(ct):
                ncols = 128 if ct < 34 else 64
                par = ct % 2
                return b.dma(ph, "gpsimd", f"wm{par}",
                             wt1[:, par * KD * 128:par * KD * 128 + KD * ncols]
                             .rearrange("p (k c) -> p k c", c=ncols),
                             w_in_r[:, :, INTER + ct * 128:INTER + ct * 128 + ncols])

            wt_tok = [0] * NCT
            wt_tok[0] = wt_dma(0)
            wt_tok[1] = wt_dma(1)
            pe_mm_tok = [0] * NCT
            conv_tok = [0] * NCT
            cv_tok = [0] * NCT
            stg_dma = [0] * NCT
            cw_tok = [0] * NCT
            cb_tok = [0] * NCT
            cw_tok[0] = b.dma(ph, "scalar", "cw0", cw[:, 0:4], conv_wt[0:128, :])
            cb_tok[0] = b.dma(ph, "scalar", "cb0", cb[:, 0:1], conv_bt[0:128, :])
            cw_tok[1] = b.dma(ph, "scalar", "cw1", cw[:, 4:8], conv_wt[128:256, :])
            cb_tok[1] = b.dma(ph, "scalar", "cb1", cb[:, 1:2], conv_bt[128:256, :])

            for ct in range(NCT):
                par = ct % 2
                ncols = 128 if ct < 34 else 64
                bank0 = [ps_a, ps_c][par]
                bank1 = [ps_b, ps_d][par]
                b.wait(ph, "tensor", b.q(f"wm{par}"), wt_tok[ct])
                pguard(bank0)
                pguard(bank1)
                pguard(ps_e)
                t_mm = 0
                for k in range(KD):
                    w_ap = wt1[:, par * KD * 128 + k * ncols:par * KD * 128 + (k + 1) * ncols]
                    st = (k == 0); sp = (k == KD - 1)
                    b.op(ph, "tensor", lambda e, _w=w_ap, _o=bank0[0:ncols, :],
                         _r=xn_fm[:, k * TMAIN:k * TMAIN + 512], _st=st, _sp=sp:
                         e.matmul(_o, _w, _r, start=_st, stop=_sp))
                    b.op(ph, "tensor", lambda e, _w=w_ap, _o=bank1[0:ncols, :],
                         _r=xn_fm[:, k * TMAIN + 512:k * TMAIN + 1024], _st=st, _sp=sp:
                         e.matmul(_o, _w, _r, start=_st, stop=_sp))
                    t_mm = b.op(ph, "tensor", lambda e, _w=w_ap, _o=ps_e[0:ncols, 0:3],
                                _r=xn_halo[:, k * 4:k * 4 + 3], _st=st, _sp=sp:
                                e.matmul(_o, _w, _r, start=_st, stop=_sp),
                                inc=(cPE if sp else None))
                pe_mm_tok[ct] = t_mm
                if ct + 2 < NCT:
                    b.wait(ph, "gpsimd", cPE, t_mm)
                    wt_tok[ct + 2] = wt_dma(ct + 2)
                if ct < len(qtiles):
                    i = ct
                    b.wait(ph, "scalar", b.q(f"ql{i % 2}"), q1_tok[i])
                    abs_tok[i] = b.op(ph, "scalar",
                                      lambda e, _i=qtl1[:, (i % 2) * D:(i % 2 + 1) * D],
                                      _a=qac[:, i:i + 1]:
                                      e.activation(_i, _i, AF.Abs, accum_out=_a), inc=cAC)
                    if i + 2 < len(qtiles):
                        b.wait(ph, "sync", cAC, abs_tok[i])
                        q1_tok[i + 2] = qld1(i + 2)
                b.wait(ph, "scalar", cPE, t_mm)
                if ct < 34:
                    prb = pre[:, par * 1027:(par + 1) * 1027]
                    b.guard(ph, "scalar", ("pre", par))
                    t_evh = b.op(ph, "scalar", lambda e, _o=prb[:, 0:3]:
                                 e.copy(_o, ps_e[:, 0:3]), inc=cAC)
                    prel(ps_e, cAC, t_evh)
                    b.op(ph, "scalar", lambda e, _o=prb[:, 3:515], _s=bank0[:, :]:
                         e.copy(_o, _s))
                    t_evp = b.op(ph, "scalar", lambda e, _o=prb[:, 515:1027], _s=bank1[:, :]:
                                 e.copy(_o, _s), inc=cAC)
                    prel(bank0, cAC, t_evp)
                    prel(bank1, cAC, t_evp)
                    a0 = cacc[:, (par * 2) * TMAIN:(par * 2 + 1) * TMAIN]
                    a1 = cacc[:, (par * 2 + 1) * TMAIN:(par * 2 + 2) * TMAIN]
                    b.wait(ph, "vector", cAC, t_evp)
                    b.wait(ph, "vector", b.q(f"cw{par}"), cw_tok[ct])
                    if ct >= 2:
                        b.wait(ph, "vector", cAC, conv_tok[ct - 2])
                    wc = cw[:, par * 4:par * 4 + 4]
                    b.op(ph, "vector", lambda e, _o=a0, _i=prb[:, 0:1024], _w=wc[:, 0:1]:
                         e.tensor_scalar(_o, _i, _w, None, ALU.mult))
                    b.op(ph, "vector", lambda e, _o=a1, _i=prb[:, 1:1025], _w=wc[:, 1:2],
                         _a=a0: e.scalar_tensor_tensor(_o, _i, _w, _a, ALU.mult, ALU.add))
                    b.op(ph, "vector", lambda e, _o=a0, _i=prb[:, 2:1026], _w=wc[:, 2:3],
                         _a=a1: e.scalar_tensor_tensor(_o, _i, _w, _a, ALU.mult, ALU.add))
                    cv_tok[ct] = b.op(ph, "vector",
                                      lambda e, _o=a1, _i=prb[:, 3:1027], _w=wc[:, 3:4],
                                      _a=a0:
                                      e.scalar_tensor_tensor(_o, _i, _w, _a, ALU.mult,
                                                             ALU.add), inc=cDV)
                    b.release(("pre", par), (cDV, cv_tok[ct]))
                    if ct < 32:
                        cdst = cvo[:, par * TMAIN:(par + 1) * TMAIN]
                        b.guard(ph, "scalar", ("cvo", par))
                    elif ct == 32:
                        cdst = bfm[:, :]
                    else:
                        cdst = cfm[:, :]
                    b.wait(ph, "scalar", cDV, cv_tok[ct])
                    b.wait(ph, "scalar", b.q(f"cb{par}"), cb_tok[ct])
                    conv_tok[ct] = b.op(ph, "scalar",
                                        lambda e, _o=cdst, _i=a1, _b=cb[:, par:par + 1]:
                                        e.activation(_o, _i, AF.Silu, bias=_b), inc=cAC)
                    if ct + 2 < 34:
                        b.wait(ph, "scalar", cDV, cv_tok[ct])
                        cw_tok[ct + 2] = b.dma(ph, "scalar", f"cw{par}", cw[:, par * 4:par * 4 + 4],
                                               conv_wt[(ct + 2) * 128:(ct + 3) * 128, :])
                        b.wait(ph, "scalar", cAC, conv_tok[ct])
                        cb_tok[ct + 2] = b.dma(ph, "scalar", f"cb{par}", cb[:, par:par + 1],
                                               conv_bt[(ct + 2) * 128:(ct + 3) * 128, :])
                else:
                    b.op(ph, "scalar", lambda e: e.copy(dtraw[:, 0:512], bank0[0:64, :]))
                    t_evp = b.op(ph, "scalar",
                                 lambda e: e.copy(dtraw[:, 512:1024], bank1[0:64, :]),
                                 inc=cAC)
                    prel(bank0, cAC, t_evp)
                    prel(bank1, cAC, t_evp)
                    t_evh = b.op(ph, "scalar", lambda e: e.copy(lnscr[:, 0:3], ps_e[:, 0:3]),
                                 inc=cAC)
                    prel(ps_e, cAC, t_evh)
                    conv_tok[ct] = t_evp

                tct = ct - 1
                if 0 <= tct < 32:
                    tpar = tct % 2
                    b.wait(ph, "tensor", cAC, conv_tok[tct])
                    b.guard(ph, "vector", ("xstg", tpar))
                    last_pe = 0
                    ev_last = 0
                    for tt in range(8):
                        bk = tbanks[tt % 2]
                        pguard(bk)
                        t_pe = b.op(ph, "tensor",
                                    lambda e, _i=cvo[:, tpar * TMAIN + tt * 128:
                                                     tpar * TMAIN + (tt + 1) * 128], _bk=bk:
                                    e.transpose(_bk[:, 0:128], _i, identb[:, :]), inc=cPE)
                        last_pe = t_pe
                        b.wait(ph, "vector", cPE, t_pe)
                        ev_last = b.op(ph, "vector",
                                       lambda e, _o=xstg[:, tpar * TMAIN + tt * 128:
                                                         tpar * TMAIN + (tt + 1) * 128],
                                       _bk=bk: e.tensor_copy(_o, _bk[:, 0:128]), inc=cDV)
                        prel(bk, cDV, ev_last)
                    b.release(("cvo", tpar), (cPE, last_pe))
                    b.wait(ph, "sync", cDV, ev_last)
                    stg_dma[tct] = b.dma(ph, "sync", f"sxw{tpar}",
                                         x_tm_d.rearrange("(t p) (c q) -> p t q c",
                                                          p=128, q=128)[:, :, :, tct],
                                         xstg[:, tpar * TMAIN:(tpar + 1) * TMAIN]
                                         .rearrange("p (t q) -> p t q", q=128))
                    b.release(("xstg", tpar), (b.q(f"sxw{tpar}"), stg_dma[tct]))
            x_tm_fin = (b.q("sxw0").n, b.q("sxw1").n)

            b.wait(ph, "scalar", cAC, conv_tok[34])
            b.op(ph, "scalar", lambda e: e.activation(dtscr[:, :], dtraw[:, :], AF.Abs,
                                                      bias=dtb_c[:, :]))
            b.op(ph, "scalar", lambda e: e.activation(dtscr[:, :], dtscr[:, :], AF.Exp,
                                                      scale=-1.0))
            t_l = b.op(ph, "scalar", lambda e: e.activation(dtscr[:, :], dtscr[:, :],
                                                            AF.Ln, bias=1.0), inc=cAC)
            b.wait(ph, "vector", cAC, t_l)
            b.op(ph, "vector", lambda e: e.tensor_scalar(dt_fm[:, :], dtraw[:, :],
                                                         dtb_c[:, :], 0.0, ALU.add, ALU.max))
            b.op(ph, "vector", lambda e: e.tensor_tensor(dt_fm[:, :], dt_fm[:, :],
                                                         dtscr[:, :], ALU.add))
            dt_done = b.op(ph, "vector", lambda e: e.tensor_scalar(dt_fm[:, :], dt_fm[:, :],
                                                                   100.0, None, ALU.min),
                           inc=cDV)
            b.wait(ph, "vector", cAC, abs_tok[-1])
            for wi in range(3):
                b.op(ph, "vector", lambda e, _o=qsc[:, wi:wi + 1],
                     _i=qac[:, wi * 8:wi * 8 + 8]: e.reduce_sum(_o, _i, axis=AX.X))
            t_q = b.op(ph, "vector", lambda e: e.memset(qsc[:, 3:4], 0.0), inc=cDV)
            b.wait(ph, "tensor", cDV, t_q)
            pguard(ps_bc)
            t_pr = b.op(ph, "tensor", lambda e: e.matmul(ps_bc[0:4, 0:1], qsc[:, :],
                                                         onesf[:, 0:1], start=True,
                                                         stop=True), inc=cPE)
            b.wait(ph, "vector", cPE, t_pr)
            t_ev = b.op(ph, "vector", lambda e: e.tensor_copy(cc1s[:, :], ps_bc[0:4, 0:1]),
                        inc=cDV)
            prel(ps_bc, cDV, t_ev)
            b.wait(ph, "gpsimd", cDV, t_ev)
            t_st = b.dma(ph, "gpsimd", "ccs", cc1_i.ap()[:, :], cc1s[:, :])
            b.wait(ph, "gpsimd", b.q("ccs"), t_st)
            b.op_cc(ph, lambda e: e.collective_compute(
                "AllReduce", ALU.add, replica_groups=[list(range(NCORE))],
                ins=[cc1_i.ap().opt()], outs=[cc1_o.ap().opt()]), cc1)
            b.wait(ph, "gpsimd", cc1, 1)
            t_rb = b.dma(ph, "gpsimd", "ccs", sc4[:, :], cc1_o.ap()[:, :])
            b.wait(ph, "vector", b.q("ccs"), t_rb)
            t_sc = b.op(ph, "vector", lambda e: e.tensor_scalar(sc4[:, :], sc4[:, :],
                                                                1.0 / WSZ, EPS_Q, ALU.mult,
                                                                ALU.add), inc=cDV)
            b.wait(ph, "scalar", cDV, t_sc)
            t_sw = b.dma(ph, "scalar", "scr", scr_d.ap()[:, :], sc4[:, :])
            b.wait(ph, "scalar", b.q("scr"), t_sw)
            t_scr = b.dma(ph, "scalar", "scr", scrow[:, :],
                          scr_d.ap().rearrange("a o -> o a"))
            b.wait(ph, "tensor", b.q("scr"), t_scr)
            pguard(ps_bc)
            t_bc2 = b.op(ph, "tensor", lambda e: e.matmul(ps_bc[:, 0:4], onesf[0:1, :],
                                                          scrow[0:1, :], start=True,
                                                          stop=True), inc=cPE)
            b.wait(ph, "vector", cPE, t_bc2)
            t_ev = b.op(ph, "vector", lambda e: e.tensor_copy(scbc[:, :], ps_bc[:, 0:4]),
                        inc=cDV)
            prel(ps_bc, cDV, t_ev)
            b.op(ph, "vector", lambda e:
                 e.tensor_tensor(sgsd_c[:, :], scbc[:, 1:2], scbc[:, 2:3], ALU.mult))
            scales_done = b.op(ph, "vector", lambda e:
                               e.reciprocal(qsc[:, 0:3], scbc[:, 0:3]), inc=cDV)
            ph.emit(block)

        _es1.close()
        b.fence(ph, [cPE, cDV, cAC, cPO], ["sxw0", "sxw1", "ql0", "ql1"])
        _es4 = contextlib.ExitStack()
        with _es4:
            nda = _es4.enter_context(nc.sbuf_tensor("nda", [HH, CHK], F32))
            ndat = _es4.enter_context(nc.sbuf_tensor("ndat", [128, 2 * HH], F32))
            decf = _es4.enter_context(nc.sbuf_tensor("decf", [HH, CHK], F32))
            ddf = _es4.enter_context(nc.sbuf_tensor("ddf", [HH, CHK], F32))
            ncol = _es4.enter_context(nc.sbuf_tensor("ncol", [HH, 8], F32))
            xch = _es4.enter_context(nc.sbuf_tensor("xch", [128, 2 * INTER], BF16))
            xdd = _es4.enter_context(nc.sbuf_tensor("xdd", [128, 2 * INTER], BF16))
            stile = _es4.enter_context(nc.sbuf_tensor("stile", [128, INTER], BF16))
            run = _es4.enter_context(nc.sbuf_tensor("run", [128, INTER], F32))
            zrow = _es4.enter_context(nc.sbuf_tensor("zrow", [1, INTER], F32))
            qtl2 = _es4.enter_context(nc.sbuf_tensor("qtl2", [128, 2 * D], F32))
            qto = _es4.enter_context(nc.sbuf_tensor("qto", [128, 2 * D], BF16))
            qtn = _es4.enter_context(nc.sbuf_tensor("qtn", [128, D], BF16))

            tdsts = {"tau_u_s": tau_u_s, "tau_g_s": tau_g_s, "tau_d_s": tau_d_s}
            wsrcs2 = {"wu_s": wu_s, "wg_s": wg_s, "wd_s": wd_s}

            def qld2(i):
                wn, tn, pc, cc, wi = qtiles[i]
                return b.dma(ph, "sync", f"ql{i % 2}", qtl2[:, (i % 2) * D:(i % 2 + 1) * D],
                             wsrcs2[wn][pc * 128:(pc + 1) * 128, cc * D:(cc + 1) * D])

            q2_tok = [0] * len(qtiles)
            st_tok2 = [0] * len(qtiles)
            q2_tok[0] = qld2(0)
            q2_tok[1] = qld2(1)

            def quant2(i):
                wn, tn, pc, cc, wi = qtiles[i]
                par = i % 2
                scol = qsc[:, wi:wi + 1]
                qo = qto[:, par * D:(par + 1) * D]
                b.wait(ph, "vector", b.q(f"ql{par}"), q2_tok[i])
                b.wait(ph, "vector", cDV, scales_done)
                if i >= 2:
                    b.wait(ph, "vector", b.q(f"ts{par}"), st_tok2[i - 2])
                b.op(ph, "vector", lambda e, _i=qtl2[:, par * D:(par + 1) * D], _s=scol,
                     _o=qo: e.tensor_scalar(_o, _i, _s, 0.5, ALU.mult, ALU.is_gt))
                b.op(ph, "vector", lambda e, _i=qtl2[:, par * D:(par + 1) * D], _s=scol:
                     e.tensor_scalar(qtn[:, :], _i, _s, -0.5, ALU.mult, ALU.is_lt))
                tau_tok = b.op(ph, "vector", lambda e, _o=qo:
                               e.tensor_tensor(_o, _o, qtn[:, :], ALU.subtract), inc=cDV)
                if i + 2 < len(qtiles):
                    b.wait(ph, "sync", cDV, tau_tok)
                    q2_tok[i + 2] = qld2(i + 2)
                b.wait(ph, "sync", cDV, tau_tok)
                st_tok2[i] = b.dma(ph, "sync", f"ts{par}",
                                   tdsts[tn].ap()[pc * 128:(pc + 1) * 128,
                                                  cc * D:(cc + 1) * D], qo)

            b.wait(ph, "vector", cDV, t_consts)
            run_tok = b.op(ph, "vector", lambda e: e.memset(run[:, :], 0.0), inc=cDV)
            b.op(ph, "vector", lambda e: e.memset(lcrow[:, :], 0.0))
            t_zr = b.op(ph, "vector", lambda e: e.memset(zrow[:, :], 0.0), inc=cDV)
            b.wait(ph, "sync", cDV, t_zr)
            b.dma(ph, "sync", "sti", st_i.ap()[128:129, HH:INTER], zrow[0:1, HH:INTER])
            s_dma_toks = []
            xdd_tok = 0

            def fp32_T(src, dst_psum_region, inc_sem=None):
                kdim = src.shape[0]
                return b.op(ph, "tensor", lambda e, _s=src, _o=dst_psum_region, _k=kdim:
                            e.transpose(_o, _s, identf[0:_k, 0:_k]), inc=inc_sem)

            for c in range(NCHU):
                for i in range(c * 6, min((c + 1) * 6, len(qtiles))):
                    quant2(i)
                b.wait(ph, "scalar", b.q("sxw0"), x_tm_fin[0])
                b.wait(ph, "scalar", b.q("sxw1"), x_tm_fin[1])
                b.wait(ph, "scalar", cDV, xdd_tok)
                xch_tok = [0, 0]
                for t in range(2):
                    xch_tok[t] = b.dma(ph, "scalar", "xr3", xch[:, t * INTER:(t + 1) * INTER],
                                       x_tm_d[(c * 2 + t) * 128:(c * 2 + t + 1) * 128, :])
                b.wait(ph, "vector", cDV, dt_done)
                b.wait(ph, "vector", cAC, t_expa)
                t_nda = b.op(ph, "vector", lambda e, _c=c:
                             e.tensor_scalar(nda[:, :], dt_fm[:, _c * CHK:(_c + 1) * CHK],
                                             expa_c[:, :], None, ALU.mult), inc=cDV)
                b.wait(ph, "tensor", cDV, t_nda)
                for t in range(2):
                    pguard(ps_bc)
                    t1 = fp32_T(nda[:, t * 128:(t + 1) * 128], ps_bc[:, 0:64], cPE)
                    pguard(ps_e)
                    t2 = fp32_T(dt_fm[:, c * CHK + t * 128:c * CHK + (t + 1) * 128],
                                ps_e[:, 0:64], cPE)
                    b.wait(ph, "vector", cPE, t1)
                    t_e1 = b.op(ph, "vector", lambda e, _t=t:
                                e.tensor_copy(ndat[:, _t * 64:(_t + 1) * 64], ps_bc[:, 0:64]),
                                inc=cDV)
                    prel(ps_bc, cDV, t_e1)
                    b.wait(ph, "vector", cPE, t2)
                    t_e2 = b.op(ph, "vector", lambda e, _t=t, _c=c:
                                e.tensor_copy(dt_tm[:, (_c * 2 + _t) * HH:(_c * 2 + _t + 1) * HH],
                                              ps_e[:, 0:64]), inc=cDV)
                    prel(ps_e, cDV, t_e2)
                b.wait(ph, "tensor", cDV, t_e1)
                pguard(ps_e)
                b.op(ph, "tensor", lambda e: e.matmul(ps_e[0:HH, 0:CHK], ndat[:, 0:HH],
                                                      ta[:, :], start=True, stop=False))
                b.op(ph, "tensor", lambda e: e.matmul(ps_e[0:HH, 0:CHK], ndat[:, HH:128],
                                                      tb_c[:, :], start=False, stop=True))
                b.op(ph, "tensor", lambda e: e.matmul(ps_e[:, 256:320], triu[:, :],
                                                      ndat[:, 0:HH], start=True, stop=True))
                b.op(ph, "tensor", lambda e: e.matmul(ps_e[:, 320:384], onesf[:, :],
                                                      ndat[:, 0:HH], start=True, stop=False))
                t_cs = b.op(ph, "tensor", lambda e: e.matmul(ps_e[:, 320:384], triu[:, :],
                                                             ndat[:, HH:128], start=False,
                                                             stop=True), inc=cPE)
                b.wait(ph, "vector", cPE, t_cs)
                t_csfm = b.op(ph, "vector", lambda e, _c=c:
                              e.tensor_copy(csneg_fm[:, _c * CHK:(_c + 1) * CHK],
                                            ps_e[0:HH, 0:CHK]), inc=cDV)
                b.wait(ph, "sync", cDV, t_csfm)
                b.dma(ph, "sync", "csf", csf_d.ap()[c * HH:(c + 1) * HH, :],
                      csneg_fm[:, c * CHK:(c + 1) * CHK])
                t_cst = b.op(ph, "vector", lambda e, _c=c:
                             e.tensor_copy(csneg_tm[:, _c * 2 * HH:(_c + 1) * 2 * HH],
                                           ps_e[:, 256:384]), inc=cDV)
                prel(ps_e, cDV, t_cst)
                t_nc = b.op(ph, "vector", lambda e, _c=c:
                            e.tensor_scalar(ncol[:, _c:_c + 1],
                                            csneg_fm[:, _c * CHK + 255:_c * CHK + 256],
                                            -1.0, None, ALU.mult), inc=cDV)
                b.wait(ph, "sync", cDV, t_cst)
                b.guard(ph, "sync", ("ngr",))
                t_nr = b.dma(ph, "sync", "ngr", negrow[:, :],
                             csneg_tm[127:128, (c * 2 + 1) * HH:(c * 2 + 2) * HH])
                b.wait(ph, "scalar", b.q("ngr"), t_nr)
                t_dr = b.op(ph, "scalar", lambda e: e.activation(decrow[:, :], negrow[:, :],
                                                                 AF.Exp, scale=-1.0), inc=cAC)
                b.wait(ph, "vector", b.q("ngr"), t_nr)
                t_lcr = b.op(ph, "vector", lambda e:
                             e.scalar_tensor_tensor(lcrow[:, :], negrow[:, :], -1.0, lcrow[:, :],
                                                    ALU.mult, ALU.add), inc=cDV)
                b.release(("ngr",), (cAC, t_dr), (cDV, t_lcr))
                b.wait(ph, "tensor", cAC, t_dr)
                pguard(ps_bc)
                t_db = b.op(ph, "tensor", lambda e: e.matmul(ps_bc[:, 0:HH], onesf[0:1, :],
                                                             decrow[0:1, :], start=True,
                                                             stop=True), inc=cPE)
                b.wait(ph, "vector", cPE, t_db)
                t_dbe = b.op(ph, "vector", lambda e, _c=c:
                             e.tensor_copy(dec_bc[:, _c * HH:(_c + 1) * HH], ps_bc[:, 0:HH]),
                             inc=cDV)
                prel(ps_bc, cDV, t_dbe)
                b.wait(ph, "scalar", cDV, t_nc)
                t_df = b.op(ph, "scalar", lambda e, _c=c:
                            e.activation(decf[:, :], csneg_fm[:, _c * CHK:(_c + 1) * CHK],
                                         AF.Exp, bias=ncol[:, _c:_c + 1]), inc=cAC)
                b.wait(ph, "vector", cAC, t_df)
                t_dd = b.op(ph, "vector", lambda e, _c=c:
                            e.tensor_tensor(ddf[:, :], dt_fm[:, _c * CHK:(_c + 1) * CHK],
                                            decf[:, :], ALU.mult), inc=cDV)
                b.wait(ph, "tensor", cDV, t_dd)
                for t in range(2):
                    pguard(ps_bc)
                    t3 = fp32_T(ddf[:, t * 128:(t + 1) * 128], ps_bc[:, 0:64], cPE)
                    b.wait(ph, "vector", cPE, t3)
                    t_dde = b.op(ph, "vector", lambda e, _t=t, _c=c:
                                 e.tensor_copy(dtdec_tm[:, (_c * 2 + _t) * HH:
                                                        (_c * 2 + _t + 1) * HH],
                                               ps_bc[:, 0:64]), inc=cDV)
                    prel(ps_bc, cDV, t_dde)
                b.wait(ph, "tensor", cAC, conv_tok[32])
                for t in range(2):
                    bk = tbanks[t]
                    pguard(bk)
                    t_bt = b.op(ph, "tensor", lambda e, _c=c, _t=t, _bk=bk:
                                e.transpose(_bk[:, 0:128],
                                            bfm[:, _c * CHK + _t * 128:_c * CHK + (_t + 1) * 128],
                                            identb[:, :]), inc=cPE)
                    b.wait(ph, "vector", cPE, t_bt)
                    t_bte = b.op(ph, "vector", lambda e, _c=c, _t=t, _bk=bk:
                                 e.tensor_copy(btm[:, _c * 256 + _t * 128:_c * 256 + (_t + 1) * 128],
                                               _bk[:, 0:128]), inc=cDV)
                    prel(bk, cDV, t_bte)
                b.wait(ph, "vector", b.q("xr3"), xch_tok[1])
                b.wait(ph, "vector", cDV, t_dde)
                b.guard(ph, "vector", ("xdd",))
                for t in range(2):
                    for h in range(HH):
                        tk = b.op(ph, "vector", lambda e, _t=t, _h=h, _c=c:
                                  e.tensor_scalar(
                                      xdd[:, _t * INTER + _h * PH:_t * INTER + (_h + 1) * PH],
                                      xch[:, _t * INTER + _h * PH:_t * INTER + (_h + 1) * PH],
                                      dtdec_tm[:, (_c * 2 + _t) * HH + _h:
                                               (_c * 2 + _t) * HH + _h + 1],
                                      None, ALU.mult),
                                  inc=(cDV if (t == 1 and h == HH - 1) else None))
                        if t == 1 and h == HH - 1:
                            xdd_tok = tk
                b.wait(ph, "tensor", cDV, xdd_tok)
                b.wait(ph, "tensor", cDV, t_bte)
                st_ev_last = 0
                for hg in range(8):
                    bank = [ps_c, ps_d][hg % 2]
                    pguard(bank)
                    t_smm = 0
                    for hh in range(8):
                        h = hg * 8 + hh
                        for t in range(2):
                            t_smm = b.op(ph, "tensor",
                                         lambda e, _h=h, _t=t, _c=c, _bk=bank, _hh=hh:
                                         e.matmul(_bk[:, _hh * PH:(_hh + 1) * PH],
                                                  btm[:, (_c * 2 + _t) * 128:(_c * 2 + _t + 1) * 128],
                                                  xdd[:, _t * INTER + _h * PH:_t * INTER + (_h + 1) * PH],
                                                  start=(_t == 0), stop=(_t == 1)),
                                         inc=(cPE if (hh == 7 and t == 1) else None))
                    b.wait(ph, "scalar", cPE, t_smm)
                    b.guard(ph, "scalar", ("stile", hg))
                    t_se = b.op(ph, "scalar", lambda e, _hg=hg, _bk=bank:
                                e.copy(stile[:, _hg * 512:(_hg + 1) * 512], _bk[:, :]),
                                inc=cAC)
                    prel(bank, cAC, t_se)
                    st_ev_last = t_se
                b.release(("xdd",), (cPE, t_smm))
                b.wait(ph, "vector", cAC, st_ev_last)
                b.wait(ph, "vector", cDV, t_dbe)
                for h in range(HH):
                    run_tok = b.op(ph, "vector", lambda e, _h=h, _c=c:
                                   e.scalar_tensor_tensor(run[:, _h * PH:(_h + 1) * PH],
                                                          run[:, _h * PH:(_h + 1) * PH],
                                                          dec_bc[:, _c * HH + _h:_c * HH + _h + 1],
                                                          stile[:, _h * PH:(_h + 1) * PH],
                                                          ALU.mult, ALU.add),
                                   inc=(cDV if h == HH - 1 else None))
                b.wait(ph, "sync", cAC, st_ev_last)
                t_sd = b.dma(ph, "sync", "ssw", s_st_d[c * 128:(c + 1) * 128, :], stile[:, :])
                s_dma_toks.append(t_sd)
                for hg in range(8):
                    b.release(("stile", hg), (b.q("ssw"), t_sd), (cDV, run_tok))
            b.wait(ph, "gpsimd", b.q("ts0"), b.q("ts0").n)
            b.wait(ph, "gpsimd", b.q("ts1"), b.q("ts1").n)
            b.op_cc(ph, lambda e: e.collective_compute(
                "AllGather", ALU.bypass, replica_groups=[list(range(NCORE))],
                ins=[tau_u_s.ap().opt()], outs=[tau_u.ap().opt()]), cc2)
            b.op_cc(ph, lambda e: e.collective_compute(
                "AllGather", ALU.bypass, replica_groups=[list(range(NCORE))],
                ins=[tau_g_s.ap().opt()], outs=[tau_g.ap().opt()]), cc2b)
            b.op_cc(ph, lambda e: e.collective_compute(
                "AllGather", ALU.bypass, replica_groups=[list(range(NCORE))],
                ins=[tau_d_s.ap().opt()], outs=[tau_d.ap().opt()]), cc2c)
            b.wait(ph, "sync", cDV, run_tok)
            t_acc = b.dma(ph, "sync", "sti", st_i.ap()[0:128, :], run[:, :])
            b.wait(ph, "sync", cDV, cDV.n)
            t_lc2 = b.dma(ph, "sync", "sti", st_i.ap()[128:129, 0:HH], lcrow[:, :])
            b.wait(ph, "gpsimd", b.q("sti"), t_lc2)
            b.op_cc(ph, lambda e: e.collective_compute(
                "AllGather", ALU.bypass, replica_groups=[list(range(NCORE))],
                ins=[st_i.ap().opt()], outs=[st_o.ap().opt()]), cc3)
            ph.emit(block)

        b.fence(ph, [cPE, cDV, cAC, cPO], ["sti", "ssw", "csf", "ts0", "ts1"])
        _es5 = contextlib.ExitStack()
        with _es5:
            la8 = _es5.enter_context(nc.sbuf_tensor("la8", [8, HH], F32))
            w8 = _es5.enter_context(nc.sbuf_tensor("w8", [8, HH], F32))
            w8r = _es5.enter_context(nc.sbuf_tensor("w8r", [1, 8 * HH], F32))
            wbc = _es5.enter_context(nc.sbuf_tensor("wbc", [128, 8 * HH], F32))
            dbc = _es5.enter_context(nc.sbuf_tensor("dbc", [128, INTER], BF16))
            gac = _es5.enter_context(nc.sbuf_tensor("gac", [128, INTER], BF16))
            st = _es5.enter_context(nc.sbuf_tensor("st", [128, INTER], F32))
            stb = _es5.enter_context(nc.sbuf_tensor("stb", [128, INTER], BF16))
            xch4 = _es5.enter_context(nc.sbuf_tensor("xch4", [128, 2 * INTER], BF16))
            xdt4 = _es5.enter_context(nc.sbuf_tensor("xdt4", [128, 2 * INTER], BF16))
            sch4 = _es5.enter_context(nc.sbuf_tensor("sch4", [128, INTER], BF16))
            ytl = _es5.enter_context(nc.sbuf_tensor("ytl", [128, 2 * INTER], BF16))
            gtm = _es5.enter_context(nc.sbuf_tensor("gtm", [128, 384], BF16))
            csr = _es5.enter_context(nc.sbuf_tensor("csr", [1, 2 * 2048], F32))
            seg = _es5.enter_context(nc.sbuf_tensor("seg", [128, 2 * 384], F32))
            etl = _es5.enter_context(nc.sbuf_tensor("etl", [128, 2 * 384], BF16))
            evt = _es5.enter_context(nc.sbuf_tensor("evt", [128, 2 * 256], BF16))
            gl = _es5.enter_context(nc.sbuf_tensor("gl", [128, 2 * 384], BF16))
            glv = _es5.enter_context(nc.sbuf_tensor("glv", [128, 2 * 256], BF16))
            wz = _es5.enter_context(nc.sbuf_tensor("wz", [128, 2 * 8 * 512], BF16))
            zst = _es5.enter_context(nc.sbuf_tensor("zst", [128, 2 * 512], BF16))

            bcast_bank = [ps_bc, ps_e]

            t_dr2 = b.dma(ph, "scalar", "d4", csr[0:1, 0:INTER], d_row)
            b.wait(ph, "tensor", b.q("d4"), t_dr2)
            dbc_last = 0
            for j in range(8):
                bk = bcast_bank[j % 2]
                pguard(bk)
                t_dj = b.op(ph, "tensor", lambda e, _j=j, _bk=bk:
                            e.matmul(_bk[:, :], onesf[0:1, :],
                                     csr[0:1, _j * 512:(_j + 1) * 512],
                                     start=True, stop=True), inc=cPE)
                b.wait(ph, "vector", cPE, t_dj)
                dbc_last = b.op(ph, "vector", lambda e, _j=j, _bk=bk:
                                e.tensor_copy(dbc[:, _j * 512:(_j + 1) * 512], _bk[:, :]),
                                inc=cDV)
                prel(bk, cDV, dbc_last)
            b.wait(ph, "scalar", cc3, 1)
            t_la = b.dma(ph, "scalar", "la", la8[:, :],
                         st_o.ap().rearrange("(r q) f -> r q f", q=129)[:, 128, 0:HH])
            b.wait(ph, "tensor", b.q("la"), t_la)
            pguard(ps_bc)
            t_ld8 = b.op(ph, "tensor", lambda e: e.matmul(ps_bc[0:8, 0:HH], selt[:, :],
                                                          la8[:, :], start=True, stop=True),
                         inc=cPE)
            b.wait(ph, "scalar", cPE, t_ld8)
            t_w8 = b.op(ph, "scalar", lambda e: e.activation(w8[:, :], ps_bc[0:8, 0:HH],
                                                             AF.Exp), inc=cAC)
            prel(ps_bc, cAC, t_w8)
            b.wait(ph, "vector", cAC, t_w8)
            t_w8m = b.op(ph, "vector", lambda e: e.tensor_scalar(w8[:, :], w8[:, :],
                                                                 pmask[:, :], None, ALU.mult),
                         inc=cDV)
            b.wait(ph, "sync", cDV, t_w8m)
            t_w8w = b.dma(ph, "sync", "w8", w8_d.ap()[:, :], w8[:, :])
            b.wait(ph, "sync", b.q("w8"), t_w8w)
            t_w8r = b.dma(ph, "sync", "w8", w8r[0:1, :],
                          w8_d.ap().rearrange("r h -> (r h)").rearrange("(o f) -> o f", o=1))
            b.wait(ph, "tensor", b.q("w8"), t_w8r)
            wbc_last = 0
            for j in range(4):
                bk = bcast_bank[j % 2]
                pguard(bk)
                t_wj = b.op(ph, "tensor", lambda e, _j=j, _bk=bk:
                            e.matmul(_bk[:, 0:128], onesf[0:1, :],
                                     w8r[0:1, _j * 128:(_j + 1) * 128], start=True,
                                     stop=True), inc=cPE)
                b.wait(ph, "vector", cPE, t_wj)
                wbc_last = b.op(ph, "vector", lambda e, _j=j, _bk=bk:
                                e.tensor_copy(wbc[:, _j * 128:(_j + 1) * 128], _bk[:, 0:128]),
                                inc=cDV)
                prel(bk, cDV, wbc_last)
            def wz_dma(zc, half):
                par = half
                return b.dma(ph, "gpsimd", f"wz{par}",
                             wz[:, par * 8 * 512:(par + 1) * 8 * 512]
                             .rearrange("p (k c) -> p k c", k=8),
                             w_in_r[:, half * 8:(half + 1) * 8, zc * 512:(zc + 1) * 512])

            wz_tok = {}
            wz_tok[(0, 0)] = wz_dma(0, 0)
            wz_tok[(0, 1)] = wz_dma(0, 1)
            z_dma = []
            for zc in range(8):
                zl_half0 = 0
                t_zl = 0
                for tt in range(8):
                    bank = [ps_a, ps_b][tt % 2]
                    pguard(bank)
                    for k in range(KD):
                        half = k // 8
                        if tt == 0:
                            b.wait(ph, "tensor", b.q(f"wz{half}"), wz_tok[(zc, half)])
                        t_zl = b.op(ph, "tensor",
                                    lambda e, _k=k, _tt=tt, _bk=bank, _h2=half:
                                    e.matmul(_bk[:, :],
                                             xn_fm[:, _k * TMAIN + _tt * 128:
                                                   _k * TMAIN + (_tt + 1) * 128],
                                             wz[:, _h2 * 8 * 512 + (_k % 8) * 512:
                                                _h2 * 8 * 512 + (_k % 8 + 1) * 512],
                                             start=(_k == 0), stop=(_k == KD - 1)),
                                    inc=(cPE if k in (7, KD - 1) else None))
                        if k == 7 and tt == 7:
                            zl_half0 = cPE.n
                    b.wait(ph, "scalar", cPE, t_zl)
                    b.guard(ph, "scalar", ("zst", tt % 2))
                    t_ze = b.op(ph, "scalar", lambda e, _tt=tt, _bk=bank:
                                e.copy(zst[:, (_tt % 2) * 512:(_tt % 2 + 1) * 512],
                                       _bk[:, :]), inc=cAC)
                    prel(bank, cAC, t_ze)
                    b.wait(ph, "sync", cAC, t_ze)
                    t_zd = b.dma(ph, "sync", f"szw{tt % 2}",
                                 z_d[tt * 128:(tt + 1) * 128, zc * 512:(zc + 1) * 512],
                                 zst[:, (tt % 2) * 512:(tt % 2 + 1) * 512])
                    z_dma.append(t_zd)
                    b.release(("zst", tt % 2), (b.q(f"szw{tt % 2}"), t_zd))
                if zc + 1 < 8:
                    b.wait(ph, "gpsimd", cPE, zl_half0)
                    wz_tok[(zc + 1, 0)] = wz_dma(zc + 1, 0)
                    b.wait(ph, "gpsimd", cPE, t_zl)
                    wz_tok[(zc + 1, 1)] = wz_dma(zc + 1, 1)
            z_fin = (b.q("szw0").n, b.q("szw1").n)

            st_tok = b.op(ph, "vector", lambda e: e.memset(st[:, 0:INTER // 2], 0.0),
                          inc=cDV)
            st_tok_po = b.op(ph, "gpsimd", lambda e: e.memset(st[:, INTER // 2:INTER], 0.0),
                             inc=cPO)
            for i in range(NCORE):
                t_gr = b.dma(ph, "gpsimd", "gac", gac[:, :], st_o.ap()[i * 129:i * 129 + 128, :])
                b.wait(ph, "vector", b.q("gac"), t_gr)
                b.wait(ph, "vector", cDV, wbc_last)
                b.wait(ph, "gpsimd", b.q("gac"), t_gr)
                b.wait(ph, "gpsimd", cDV, wbc_last)
                for h in range(HH):
                    tk = b.op(ph, "vector", lambda e, _i=i, _h=h:
                              e.scalar_tensor_tensor(st[:, _h * PH:(_h + 1) * PH],
                                                     gac[:, _h * PH:(_h + 1) * PH],
                                                     wbc[:, _i * HH + _h:_i * HH + _h + 1],
                                                     st[:, _h * PH:(_h + 1) * PH],
                                                     ALU.mult, ALU.add),
                              inc=(cDV if h == HH - 1 else None))
                    if h == HH - 1:
                        st_tok = tk
                st_tok_po = 0
                b.wait(ph, "scalar", cDV, st_tok)

            y_dma_toks = []
            for c in range(NCHU):
                b.guard(ph, "vector", ("stb",))
                b.wait(ph, "vector", cDV, st_tok)
                if st_tok_po:
                    b.wait(ph, "vector", cPO, st_tok_po)
                t_stb = b.op(ph, "vector", lambda e: e.tensor_copy(stb[:, :], st[:, :]),
                             inc=cDV)
                b.guard(ph, "scalar", ("xch4",))
                xc_tok = [0, 0]
                for t in range(2):
                    xc_tok[t] = b.dma(ph, "scalar", "xr4", xch4[:, t * INTER:(t + 1) * INTER],
                                      x_tm_d[(c * 2 + t) * 128:(c * 2 + t + 1) * 128, :])
                b.wait(ph, "scalar", b.q("ssw"), b.q("ssw").n)
                b.guard(ph, "scalar", ("sch4",))
                sc_tok = b.dma(ph, "scalar", "ssr", sch4[:, :], s_st_d[c * 128:(c + 1) * 128, :])
                b.wait(ph, "vector", b.q("xr4"), xc_tok[1])
                b.guard(ph, "vector", ("xdt4",))
                xdt_tok = 0
                for t in range(2):
                    for h in range(HH):
                        tk = b.op(ph, "vector", lambda e, _t=t, _h=h, _c=c:
                                  e.tensor_scalar(
                                      xdt4[:, _t * INTER + _h * PH:_t * INTER + (_h + 1) * PH],
                                      xch4[:, _t * INTER + _h * PH:_t * INTER + (_h + 1) * PH],
                                      dt_tm[:, (_c * 2 + _t) * HH + _h:
                                            (_c * 2 + _t) * HH + _h + 1],
                                      None, ALU.mult),
                                  inc=(cDV if (t == 1 and h == HH - 1) else None))
                        if t == 1 and h == HH - 1:
                            xdt_tok = tk
                b.wait(ph, "vector", cDV, dbc_last)
                b.guard(ph, "vector", ("ytl",))
                yi_tok = 0
                for t in range(2):
                    yi_tok = b.op(ph, "vector", lambda e, _t=t:
                                  e.tensor_tensor(ytl[:, _t * INTER:(_t + 1) * INTER],
                                                  xch4[:, _t * INTER:(_t + 1) * INTER],
                                                  dbc[:, :], ALU.mult), inc=cDV)
                b.release(("xch4",), (cDV, yi_tok))
                pguard(ps_e)
                b.wait(ph, "tensor", cAC, conv_tok[33])
                t_gmm = 0
                for sh in range(2):
                    t_gmm = b.op(ph, "tensor", lambda e, _c=c, _sh=sh:
                                 e.matmul(ps_e[:, _sh * 256:(_sh + 1) * 256],
                                          bfm[:, _c * CHK + _sh * 128:_c * CHK + (_sh + 1) * 128],
                                          cfm[:, _c * CHK:(_c + 1) * CHK],
                                          start=True, stop=True),
                                 inc=(cPE if sh == 1 else None))
                b.wait(ph, "vector", cPE, t_gmm)
                b.guard(ph, "vector", ("gtm",))
                b.op(ph, "vector", lambda e: e.tensor_tensor(gtm[:, 0:128], ps_e[:, 0:128],
                                                             triu[:, :], ALU.mult))
                b.op(ph, "vector", lambda e: e.tensor_copy(gtm[:, 128:256], ps_e[:, 128:256]))
                t_gt = b.op(ph, "vector", lambda e:
                            e.tensor_tensor(gtm[:, 256:384], ps_e[:, 384:512], triu[:, :],
                                            ALU.mult), inc=cDV)
                prel(ps_e, cDV, t_gt)
                cs_tok = [0, 0]
                cs_last_bcm = [0, 0]
                t_ymm_l = [0] * (HH + 2)
                t_glv_l = [0] * (HH + 2)

                def csr_load(hg):
                    par2 = hg % 2
                    b.guard(ph, "sync", ("csr", par2))
                    b.wait(ph, "sync", b.q("csf"), b.q("csf").n)
                    cs_tok[par2] = b.dma(ph, "sync", f"cs{par2}",
                                         csr[0:1, par2 * 2048:(par2 + 1) * 2048],
                                         csf_d.ap()[c * HH + hg * 8:c * HH + (hg + 1) * 8, :]
                                         .rearrange("h l -> (h l)")
                                         .rearrange("(o f) -> o f", o=1))

                def front(h):
                    hg, hh = divmod(h, 8)
                    par2 = hg % 2
                    par = h % 2
                    bcb = bcast_bank[par]
                    b.wait(ph, "tensor", b.q(f"cs{par2}"), cs_tok[par2])
                    pguard(bcb)
                    t_bcm = b.op(ph, "tensor", lambda e, _hh=hh, _par2=par2, _bk=bcb:
                                 e.matmul(_bk[:, 0:256], onesf[0:1, :],
                                          csr[0:1, _par2 * 2048 + _hh * 256:
                                              _par2 * 2048 + (_hh + 1) * 256],
                                          start=True, stop=True), inc=cPE)
                    cs_last_bcm[par2] = t_bcm
                    b.wait(ph, "vector", cPE, t_bcm)
                    b.guard(ph, "vector", ("seg", par))
                    b.op(ph, "vector", lambda e, _par=par, _c=c, _h=h, _bk=bcb:
                         e.tensor_scalar(seg[:, _par * 384:_par * 384 + 256],
                                         _bk[:, 0:256],
                                         csneg_tm[:, (_c * 2) * HH + _h:(_c * 2) * HH + _h + 1],
                                         0.0, ALU.subtract, ALU.max))
                    t_seg = b.op(ph, "vector", lambda e, _par=par, _c=c, _h=h, _bk=bcb:
                                 e.tensor_scalar(seg[:, _par * 384 + 256:_par * 384 + 384],
                                                 _bk[:, 128:256],
                                                 csneg_tm[:, (_c * 2 + 1) * HH + _h:
                                                          (_c * 2 + 1) * HH + _h + 1],
                                                 0.0, ALU.subtract, ALU.max), inc=cDV)
                    b.wait(ph, "scalar", cDV, t_seg)
                    b.guard(ph, "scalar", ("etl", par))
                    b.op(ph, "scalar", lambda e, _par=par:
                         e.activation(etl[:, _par * 384:(_par + 1) * 384],
                                      seg[:, _par * 384:(_par + 1) * 384], AF.Exp,
                                      scale=-1.0))
                    b.guard(ph, "scalar", ("evt", par))
                    t_ev2 = b.op(ph, "scalar", lambda e, _par=par, _bk=bcb:
                                 e.activation(evt[:, _par * 256:(_par + 1) * 256],
                                              _bk[:, 0:256], AF.Exp, scale=-1.0),
                                 inc=cAC)
                    b.release(("seg", par), (cAC, t_ev2))
                    prel(bcb, cAC, t_ev2)
                    b.wait(ph, "vector", cAC, t_ev2)
                    b.wait(ph, "vector", cDV, t_gt)
                    b.guard(ph, "vector", ("gl", par))
                    b.op(ph, "vector", lambda e, _par=par:
                         e.tensor_tensor(gl[:, _par * 384:(_par + 1) * 384],
                                         etl[:, _par * 384:(_par + 1) * 384],
                                         gtm[:, :], ALU.mult))
                    t_glv = b.op(ph, "vector", lambda e, _par=par, _c=c:
                                 e.tensor_tensor(glv[:, _par * 256:(_par + 1) * 256],
                                                 evt[:, _par * 256:(_par + 1) * 256],
                                                 cfm[:, _c * CHK:(_c + 1) * CHK],
                                                 ALU.mult), inc=cDV)
                    b.release(("etl", par), (cDV, t_glv))
                    b.release(("evt", par), (cDV, t_glv))
                    t_glv_l[h] = t_glv

                def ymms(h, bank0, bank1):
                    hg, hh = divmod(h, 8)
                    par = h % 2
                    b.wait(ph, "tensor", cDV, t_glv_l[h])
                    b.wait(ph, "tensor", cDV, xdt_tok)
                    b.wait(ph, "tensor", cDV, t_stb)
                    b.op(ph, "tensor", lambda e, _par=par, _h=h, _bk=bank0, _hh=hh:
                         e.matmul(_bk[:, _hh * PH:(_hh + 1) * PH],
                                  gl[:, _par * 384:_par * 384 + 128],
                                  xdt4[:, _h * PH:(_h + 1) * PH],
                                  start=True, stop=False))
                    b.op(ph, "tensor", lambda e, _par=par, _h=h, _bk=bank0, _hh=hh:
                         e.matmul(_bk[:, _hh * PH:(_hh + 1) * PH],
                                  glv[:, _par * 256:_par * 256 + 128],
                                  stb[:, _h * PH:(_h + 1) * PH], start=False,
                                  stop=True))
                    b.op(ph, "tensor", lambda e, _par=par, _h=h, _bk=bank1, _hh=hh:
                         e.matmul(_bk[:, _hh * PH:(_hh + 1) * PH],
                                  gl[:, _par * 384 + 128:_par * 384 + 256],
                                  xdt4[:, _h * PH:(_h + 1) * PH],
                                  start=True, stop=False))
                    b.op(ph, "tensor", lambda e, _par=par, _h=h, _bk=bank1, _hh=hh:
                         e.matmul(_bk[:, _hh * PH:(_hh + 1) * PH],
                                  gl[:, _par * 384 + 256:_par * 384 + 384],
                                  xdt4[:, INTER + _h * PH:INTER + (_h + 1) * PH],
                                  start=False, stop=False))
                    t = b.op(ph, "tensor", lambda e, _par=par, _h=h, _bk=bank1, _hh=hh:
                             e.matmul(_bk[:, _hh * PH:(_hh + 1) * PH],
                                      glv[:, _par * 256 + 128:_par * 256 + 256],
                                      stb[:, _h * PH:(_h + 1) * PH], start=False,
                                      stop=True), inc=cPE)
                    t_ymm_l[h] = t
                    b.release(("gl", par), (cPE, t))

                csr_load(0)
                csr_load(1)
                front(0)
                t_ymm = 0
                for h in range(HH):
                    hg, hh = divmod(h, 8)
                    par2 = hg % 2
                    bank0 = [ps_a, ps_c][par2]
                    bank1 = [ps_b, ps_d][par2]
                    if hh == 0:
                        pguard(bank0)
                        pguard(bank1)
                    if h + 1 < HH:
                        front(h + 1)
                    ymms(h, bank0, bank1)
                    t_ymm = t_ymm_l[h]
                    if hh == 7:
                        b.release(("csr", par2), (cPE, cs_last_bcm[par2]))
                        if hg + 2 < 8:
                            csr_load(hg + 2)
                        b.wait(ph, "vector", cPE, t_ymm)
                        b.wait(ph, "vector", cDV, yi_tok)
                        b.op(ph, "vector", lambda e, _hg=hg, _bk=bank0:
                             e.tensor_tensor(ytl[:, _hg * 512:(_hg + 1) * 512],
                                             ytl[:, _hg * 512:(_hg + 1) * 512], _bk[:, :],
                                             ALU.add))
                        t_ye = b.op(ph, "vector", lambda e, _hg=hg, _bk=bank1:
                                    e.tensor_tensor(ytl[:, INTER + _hg * 512:INTER + (_hg + 1) * 512],
                                                    ytl[:, INTER + _hg * 512:INTER + (_hg + 1) * 512],
                                                    _bk[:, :], ALU.add), inc=cDV)
                        prel(bank0, cDV, t_ye)
                        prel(bank1, cDV, t_ye)
                b.release(("stb",), (cPE, t_ymm))
                b.release(("xdt4",), (cPE, t_ymm))
                b.wait(ph, "gpsimd", cPE, t_ymm)
                b.release(("gtm",), (cDV, t_glv_l[HH - 1]))
                b.wait(ph, "gpsimd", cDV, cDV.n)
                ty0 = b.dma(ph, "gpsimd", "syw", y_d[(c * 2) * 128:(c * 2 + 1) * 128, :],
                            ytl[:, 0:INTER])
                ty1 = b.dma(ph, "gpsimd", "syw", y_d[(c * 2 + 1) * 128:(c * 2 + 2) * 128, :],
                            ytl[:, INTER:2 * INTER])
                y_dma_toks += [ty0, ty1]
                b.release(("ytl",), (b.q("syw"), ty1))
                if c < NCHU - 1:
                    b.wait(ph, "vector", b.q("ssr"), sc_tok)
                    b.wait(ph, "vector", cPE, t_ymm)
                    for h in range(HH):
                        tk = b.op(ph, "vector", lambda e, _h=h, _c=c:
                                  e.scalar_tensor_tensor(st[:, _h * PH:(_h + 1) * PH],
                                                         st[:, _h * PH:(_h + 1) * PH],
                                                         dec_bc[:, _c * HH + _h:
                                                                _c * HH + _h + 1],
                                                         sch4[:, _h * PH:(_h + 1) * PH],
                                                         ALU.mult, ALU.add),
                                  inc=(cDV if h == HH - 1 else None))
                        if h == HH - 1:
                            st_tok = tk
                    st_tok_po = 0
                    b.release(("sch4",), (cDV, st_tok))
            y_done = y_dma_toks[-1]
            ph.emit(block)

        b.fence(ph, [cPE, cDV, cAC, cPO], ["szw0", "szw1", "syw"])
        _es7 = contextlib.ExitStack()
        with _es7:
            ztl = _es7.enter_context(nc.sbuf_tensor("ztl", [128, INTER], F32))
            ytl6 = _es7.enter_context(nc.sbuf_tensor("ytl6", [128, INTER], F32))
            sil = _es7.enter_context(nc.sbuf_tensor("sil", [128, INTER], F32))
            nwb = _es7.enter_context(nc.sbuf_tensor("nwb", [128, INTER], BF16))
            nc6 = _es7.enter_context(nc.sbuf_tensor("nc6", [128, 8], F32))
            yn = _es7.enter_context(nc.sbuf_tensor("yn", [128, INTER], BF16))
            ynf = _es7.enter_context(nc.sbuf_tensor("ynf", [128, 32 * 512], BF16))
            wo = _es7.enter_context(nc.sbuf_tensor("wo", [128, 2 * 8 * 512], BF16))
            stg6 = _es7.enter_context(nc.sbuf_tensor("stg6", [128, 2048], BF16))

            t_ms = b.op(ph, "vector", lambda e: e.memset(sil[:, :], 0.0), inc=cDV)
            b.wait(ph, "scalar", cDV, t_ms)
            t_nw = b.dma(ph, "scalar", "nw", sil[0:1, 0:INTER], nw_row)
            b.wait(ph, "tensor", b.q("nw"), t_nw)
            nwb_last = 0
            nwb_pe = 0
            for j in range(8):
                bk = [ps_bc, ps_e][j % 2]
                pguard(bk)
                t_nj = b.op(ph, "tensor", lambda e, _j=j, _bk=bk:
                            e.matmul(_bk[:, :], onesf[0:1, :],
                                     sil[0:1, _j * 512:(_j + 1) * 512], start=True,
                                     stop=True), inc=cPE)
                nwb_pe = t_nj
                b.wait(ph, "vector", cPE, t_nj)
                nwb_last = b.op(ph, "vector", lambda e, _j=j, _bk=bk:
                                e.tensor_copy(nwb[:, _j * 512:(_j + 1) * 512], _bk[:, :]),
                                inc=cDV)
                prel(bk, cDV, nwb_last)

            w_out_r = w_out.rearrange("(k p) c -> p k c", p=128)
            s_dma6 = []
            for grp in range(2):
                tr_done = 0
                for ti in range(4):
                    tt = grp * 4 + ti
                    b.wait(ph, "gpsimd", b.q("szw0"), z_fin[0])
                    b.wait(ph, "gpsimd", b.q("szw1"), z_fin[1])
                    b.wait(ph, "gpsimd", b.q("syw"), y_done)
                    b.guard(ph, "gpsimd", ("ztl",))
                    t_z = b.dma(ph, "gpsimd", "zr6", ztl[:, :], z_d[tt * 128:(tt + 1) * 128, :])
                    b.guard(ph, "gpsimd", ("ytl6",))
                    t_y = b.dma(ph, "gpsimd", "yr6", ytl6[:, :], y_d[tt * 128:(tt + 1) * 128, :])
                    b.wait(ph, "scalar", b.q("zr6"), t_z)
                    b.wait(ph, "scalar", cPE, nwb_pe)
                    b.guard(ph, "scalar", ("sil",))
                    t_sil = b.op(ph, "scalar", lambda e: e.activation(sil[:, :], ztl[:, :],
                                                                      AF.Silu), inc=cAC)
                    b.release(("ztl",), (cAC, t_sil))
                    b.wait(ph, "vector", cAC, t_sil)
                    b.wait(ph, "vector", b.q("yr6"), t_y)
                    t_yf = b.op(ph, "vector", lambda e: e.tensor_tensor(ytl6[:, :],
                                                                        ytl6[:, :],
                                                                        sil[:, :], ALU.mult),
                                inc=cDV)
                    b.wait(ph, "scalar", cDV, t_yf)
                    t_sq6 = b.op(ph, "scalar", lambda e:
                                 e.activation(sil[:, :], ytl6[:, :], AF.Square,
                                              accum_out=nc6[:, 0:1]), inc=cAC)
                    b.release(("sil",), (cAC, t_sq6))
                    b.wait(ph, "vector", cAC, t_sq6)
                    t_v6 = b.op(ph, "vector", lambda e:
                                e.tensor_scalar(nc6[:, 1:2], nc6[:, 0:1], 1.0 / INTER,
                                                None, ALU.mult), inc=cDV)
                    b.wait(ph, "scalar", cDV, t_v6)
                    b.op(ph, "scalar", lambda e: e.activation(nc6[:, 2:3], nc6[:, 1:2],
                                                              AF.Ln, bias=EPS_RMS))
                    t_rs6 = b.op(ph, "scalar", lambda e: e.activation(nc6[:, 3:4],
                                                                      nc6[:, 2:3], AF.Exp,
                                                                      scale=-0.5), inc=cAC)
                    b.wait(ph, "vector", cAC, t_rs6)
                    b.wait(ph, "vector", cDV, nwb_last)
                    b.guard(ph, "vector", ("yn",))
                    t_yn = b.op(ph, "vector", lambda e:
                                e.scalar_tensor_tensor(yn[:, :], ytl6[:, :], nc6[:, 3:4],
                                                       nwb[:, :], ALU.mult, ALU.mult),
                                inc=cDV)
                    b.release(("ytl6",), (cDV, t_yn))
                    b.wait(ph, "tensor", cDV, t_yn)
                    b.guard(ph, "tensor", ("ynf",))
                    last_pe = 0
                    for k in range(32):
                        bk = tbanks[k % 2]
                        pguard(bk)
                        t_pe6 = b.op(ph, "tensor", lambda e, _k=k, _bk=bk:
                                     e.transpose(_bk[:, 0:128], yn[:, _k * 128:(_k + 1) * 128],
                                                 identb[:, :]), inc=cPE)
                        last_pe = t_pe6
                        b.wait(ph, "vector", cPE, t_pe6)
                        t_ev6 = b.op(ph, "vector", lambda e, _k=k, _ti=ti, _bk=bk:
                                     e.tensor_copy(ynf[:, _k * 512 + _ti * 128:
                                                       _k * 512 + (_ti + 1) * 128],
                                                   _bk[:, 0:128]), inc=cDV)
                        prel(bk, cDV, t_ev6)
                        tr_done = t_ev6
                    b.release(("yn",), (cPE, last_pe))
                b.wait(ph, "tensor", cDV, tr_done)
                banks6 = [ps_a, ps_b, ps_c, ps_d]
                order = [(dtile, kg) for dtile in range(4) for kg in range(4)]

                def wo_dma(idx):
                    dtile, kg = order[idx]
                    par = idx % 2
                    return b.dma(ph, "gpsimd", f"wo{par}",
                                 wo[:, par * 8 * 512:(par + 1) * 8 * 512]
                                 .rearrange("p (k c) -> p k c", k=8),
                                 w_out_r[:, kg * 8:(kg + 1) * 8,
                                         dtile * 512:(dtile + 1) * 512])

                wo_tok = {0: wo_dma(0), 1: wo_dma(1)}
                grp_tok = {}
                m2_last = 0
                for idx, (dtile, kg) in enumerate(order):
                    par = idx % 2
                    if kg == 0:
                        for p in banks6:
                            pguard(p)
                    b.wait(ph, "tensor", b.q(f"wo{par}"), wo_tok[idx])
                    for kk in range(8):
                        k = kg * 8 + kk
                        for ti in range(4):
                            m2_last = b.op(ph, "tensor",
                                           lambda e, _k=k, _kk=kk, _ti=ti, _par=par,
                                           _bk=banks6[ti], _st=(k == 0), _sp=(k == 31):
                                           e.matmul(_bk[:, :],
                                                    ynf[:, _k * 512 + _ti * 128:
                                                        _k * 512 + (_ti + 1) * 128],
                                                    wo[:, _par * 8 * 512 + _kk * 512:
                                                       _par * 8 * 512 + (_kk + 1) * 512],
                                                    start=_st, stop=_sp),
                                           inc=(cPE if (kk == 7 and ti == 3) else None))
                    grp_tok[idx] = m2_last
                    if idx + 2 < len(order):
                        b.wait(ph, "gpsimd", cPE, m2_last)
                        wo_tok[idx + 2] = wo_dma(idx + 2)
                    if kg == 3:
                        b.wait(ph, "vector", cPE, m2_last)
                        b.guard(ph, "vector", ("stg6",))
                        t_se6 = 0
                        for ti in range(4):
                            t_se6 = b.op(ph, "vector", lambda e, _ti=ti, _bk=banks6[ti]:
                                         e.tensor_copy(stg6[:, _ti * 512:(_ti + 1) * 512],
                                                       _bk[:, :]), inc=cDV)
                            prel(banks6[ti], cDV, t_se6)
                        b.wait(ph, "sync", cDV, t_se6)
                        t_s6 = b.dma(ph, "sync", "ssd",
                                     s_d.rearrange("(u p) q -> p u q", p=128)
                                     [:, grp * 4:(grp + 1) * 4, dtile * 512:(dtile + 1) * 512],
                                     stg6[:, :].rearrange("p (u q) -> p u q", u=4))
                        s_dma6.append(t_s6)
                        b.release(("stg6",), (b.q("ssd"), t_s6))
                b.release(("ynf",), (cPE, m2_last))
            s_done6 = s_dma6[-1]
            ph.emit(block)

        b.fence(ph, [cPE, cDV, cAC, cPO], ["ssd"])
        _es8 = contextlib.ExitStack()
        with _es8:
            tut = _es8.enter_context(nc.sbuf_tensor("tut", [128, 2 * KD * 128], BF16))
            tgt = _es8.enter_context(nc.sbuf_tensor("tgt", [128, 2 * KD * 128], BF16))
            hid = _es8.enter_context(nc.sbuf_tensor("hid", [128, 64 * 512], BF16))
            ups = _es8.enter_context(nc.sbuf_tensor("ups", [128, 2 * 512], F32))
            tdt = _es8.enter_context(nc.sbuf_tensor("tdt", [128, 2 * 8 * 512], BF16))
            gt7 = _es8.enter_context(nc.sbuf_tensor("gt7", [128, 512], BF16))
            st7 = _es8.enter_context(nc.sbuf_tensor("st7", [128, 2 * 512], BF16))
            xr7 = _es8.enter_context(nc.sbuf_tensor("xr7", [128, 2 * 512], F32))
            ot7 = _es8.enter_context(nc.sbuf_tensor("ot7", [128, 512], F32))

            tau_u_r = tau_u.ap().rearrange("(k p) c -> p k c", p=128)
            tau_g_r = tau_g.ap().rearrange("(k p) c -> p k c", p=128)
            tau_d_r = tau_d.ap().rearrange("(g p) c -> p g c", p=128)
            b.wait(ph, "sync", cc2, 1)
            b.wait(ph, "sync", cc2b, 1)
            b.wait(ph, "sync", cc2c, 1)

            def tu_dma(eg):
                par = eg % 2
                return (b.dma(ph, "sync", f"tu{par}",
                              tut[:, par * KD * 128:(par + 1) * KD * 128]
                              .rearrange("p (k c) -> p k c", k=KD),
                              tau_u_r[:, :, eg * 128:(eg + 1) * 128]),
                        b.dma(ph, "sync", f"tg{par}",
                              tgt[:, par * KD * 128:(par + 1) * KD * 128]
                              .rearrange("p (k c) -> p k c", k=KD),
                              tau_g_r[:, :, eg * 128:(eg + 1) * 128]))

            m4_last = 0
            for slab in range(2):
                tu_tok = [None] * 64
                tu_tok[0] = tu_dma(0)
                tu_tok[1] = tu_dma(1)
                if slab == 1:
                    b.wait(ph, "vector", cPE, m4_last)
                hid_done = 0
                for eg in range(64):
                    par = eg % 2
                    ubank = [ps_a, ps_c][par]
                    gbank = [ps_b, ps_d][par]
                    b.wait(ph, "tensor", b.q(f"tu{par}"), tu_tok[eg][0])
                    b.wait(ph, "tensor", b.q(f"tg{par}"), tu_tok[eg][1])
                    pguard(ubank)
                    pguard(gbank)
                    t_ug = 0
                    for k in range(KD):
                        rhs = xn_fm[:, k * TMAIN + slab * 512:k * TMAIN + slab * 512 + 512]
                        b.op(ph, "tensor", lambda e, _k=k, _par=par, _bk=ubank, _r=rhs:
                             e.matmul(_bk[:, :],
                                      tut[:, _par * KD * 128 + _k * 128:
                                          _par * KD * 128 + (_k + 1) * 128],
                                      _r, start=(_k == 0), stop=(_k == KD - 1)))
                        t_ug = b.op(ph, "tensor", lambda e, _k=k, _par=par, _bk=gbank, _r=rhs:
                                    e.matmul(_bk[:, :],
                                             tgt[:, _par * KD * 128 + _k * 128:
                                                 _par * KD * 128 + (_k + 1) * 128],
                                             _r, start=(_k == 0), stop=(_k == KD - 1)),
                                    inc=(cPE if k == KD - 1 else None))
                    if eg + 2 < 64:
                        b.wait(ph, "sync", cPE, t_ug)
                        tu_tok[eg + 2] = tu_dma(eg + 2)
                    b.wait(ph, "scalar", cPE, t_ug)
                    b.wait(ph, "scalar", cDV, scales_done)
                    b.guard(ph, "scalar", ("ups", par))
                    t_us = b.op(ph, "scalar", lambda e, _par=par, _bk=ubank:
                                e.activation(ups[:, _par * 512:(_par + 1) * 512], _bk[:, :],
                                             AF.Silu, scale=scbc[:, 0:1]), inc=cAC)
                    prel(ubank, cAC, t_us)
                    b.wait(ph, "vector", cAC, t_us)
                    hid_done = b.op(ph, "vector", lambda e, _par=par, _eg=eg, _bk=gbank:
                                    e.tensor_tensor(hid[:, _eg * 512:(_eg + 1) * 512],
                                                    ups[:, _par * 512:(_par + 1) * 512],
                                                    _bk[:, :], ALU.mult), inc=cDV)
                    prel(gbank, cDV, hid_done)
                    b.release(("ups", par), (cDV, hid_done))
                banks7 = [ps_a, ps_b, ps_c, ps_d]
                b.wait(ph, "tensor", cDV, hid_done)

                def td_dma(dtile, egr):
                    par = egr % 2
                    return b.dma(ph, "sync", f"td{par}",
                                 tdt[:, par * 8 * 512:(par + 1) * 8 * 512]
                                 .rearrange("p (k c) -> p k c", k=8),
                                 tau_d_r[:, egr * 8:(egr + 1) * 8,
                                         dtile * 512:(dtile + 1) * 512])

                for dtile in range(4):
                    td_tok = [0] * 8
                    td_tok[0] = td_dma(dtile, 0)
                    td_tok[1] = td_dma(dtile, 1)
                    for p in banks7:
                        pguard(p)
                    for egr in range(8):
                        par = egr % 2
                        b.wait(ph, "tensor", b.q(f"td{par}"), td_tok[egr])
                        for ee_ in range(8):
                            e_idx = egr * 8 + ee_
                            for ts in range(4):
                                m4_last = b.op(ph, "tensor",
                                               lambda e, _e=e_idx, _ee=ee_, _ts=ts, _par=par,
                                               _bk=banks7[ts], _st=(e_idx == 0),
                                               _sp=(e_idx == 63):
                                               e.matmul(_bk[:, :],
                                                        hid[:, _e * 512 + _ts * 128:
                                                            _e * 512 + (_ts + 1) * 128],
                                                        tdt[:, _par * 8 * 512 + _ee * 512:
                                                            _par * 8 * 512 + (_ee + 1) * 512],
                                                        start=_st, stop=_sp),
                                               inc=(cPE if (ee_ == 7 and ts == 3) else None))
                        if egr + 2 < 8:
                            b.wait(ph, "sync", cPE, m4_last)
                            td_tok[egr + 2] = td_dma(dtile, egr + 2)
                    for ts in range(4):
                        tt_g = slab * 4 + ts
                        b.wait(ph, "scalar", b.q("ssd"), s_done6)
                        b.guard(ph, "scalar", ("st7", ts % 2))
                        t_s7 = b.dma(ph, "scalar", f"s7{ts % 2}",
                                     st7[:, (ts % 2) * 512:(ts % 2 + 1) * 512],
                                     s_d[tt_g * 128:(tt_g + 1) * 128,
                                         dtile * 512:(dtile + 1) * 512])
                        b.guard(ph, "scalar", ("xr7", ts % 2))
                        t_x7 = b.dma(ph, "scalar", f"x7{ts % 2}",
                                     xr7[:, (ts % 2) * 512:(ts % 2 + 1) * 512],
                                     x_ext[THALO + tt_g * 128:THALO + (tt_g + 1) * 128,
                                           dtile * 512:(dtile + 1) * 512])
                        b.wait(ph, "scalar", cPE, m4_last)
                        b.guard(ph, "scalar", ("gt7",))
                        t_g7 = b.op(ph, "scalar", lambda e, _bk=banks7[ts]:
                                    e.activation(gt7[:, :], _bk[:, :], AF.Silu,
                                                 scale=sgsd_c[:, 0:1]), inc=cAC)
                        prel(banks7[ts], cAC, t_g7)
                        b.wait(ph, "vector", cAC, t_g7)
                        b.wait(ph, "vector", b.q(f"s7{ts % 2}"), t_s7)
                        b.wait(ph, "vector", b.q(f"x7{ts % 2}"), t_x7)
                        b.guard(ph, "vector", ("ot7",))
                        t_m1 = b.op(ph, "vector", lambda e, _ts=ts:
                                    e.tensor_tensor(ot7[:, :], gt7[:, :],
                                                    st7[:, (_ts % 2) * 512:(_ts % 2 + 1) * 512],
                                                    ALU.mult), inc=cDV)
                        b.release(("gt7",), (cDV, t_m1))
                        b.release(("st7", ts % 2), (cDV, t_m1))
                        t_o7 = b.op(ph, "vector", lambda e, _ts=ts:
                                    e.tensor_tensor(ot7[:, :], ot7[:, :],
                                                    xr7[:, (_ts % 2) * 512:(_ts % 2 + 1) * 512],
                                                    ALU.add), inc=cDV)
                        b.release(("xr7", ts % 2), (cDV, t_o7))
                        b.wait(ph, "sync", cDV, t_o7)
                        t_od = b.dma(ph, "sync", qOUT,
                                     out_d[tt_g * 128:(tt_g + 1) * 128,
                                           dtile * 512:(dtile + 1) * 512], ot7[:, :])
                        b.release(("ot7",), (qOUT, t_od))
            b.wait(ph, "sync", qOUT, qOUT.n)
            ph.emit(block)

    stack.close()
    return nc


_NC_CACHE = None


def kernel(L0_attn_hidden_states, L0_attn_W_in, L0_attn_conv_w, L0_attn_conv_b,
           L0_attn_dt_bias, L0_attn_A_log, L0_attn_D, L0_attn_norm_weight,
           L0_attn_W_out, L0_ffn_W_up, L0_ffn_W_gate, L0_ffn_W_down):
    global _NC_CACHE
    x = np.ascontiguousarray(np.asarray(L0_attn_hidden_states, np.float32))
    w_in = np.ascontiguousarray(np.asarray(L0_attn_W_in, np.float32))
    w_out = np.ascontiguousarray(np.asarray(L0_attn_W_out, np.float32))
    wu_t = np.ascontiguousarray(np.asarray(L0_ffn_W_up, np.float32).T)
    wg_t = np.ascontiguousarray(np.asarray(L0_ffn_W_gate, np.float32).T)
    wd_t = np.ascontiguousarray(np.asarray(L0_ffn_W_down, np.float32).T)
    conv_wt = np.ascontiguousarray(np.asarray(L0_attn_conv_w, np.float32).T)
    conv_bt = np.ascontiguousarray(np.asarray(L0_attn_conv_b, np.float32).reshape(CONVD, 1))
    dt_b = np.ascontiguousarray(np.asarray(L0_attn_dt_bias, np.float32).reshape(HH, 1))
    a_log = np.ascontiguousarray(np.asarray(L0_attn_A_log, np.float32).reshape(HH, 1))
    d_row = np.ascontiguousarray(
        np.repeat(np.asarray(L0_attn_D, np.float32), PH).reshape(1, INTER))
    nw_row = np.ascontiguousarray(np.asarray(L0_attn_norm_weight, np.float32)
                                  .reshape(1, INTER))
    ident_b = np.eye(128, dtype=ml_dtypes.bfloat16)
    ident_f = np.eye(128, dtype=np.float32)
    triu = np.ascontiguousarray(np.triu(np.ones((128, 128), np.float32)))

    if _NC_CACHE is None:
        _NC_CACHE = build_nc()
    nc = _NC_CACHE

    in_maps = []
    for c in range(NCORE):
        bb, j = divmod(c, 4)
        xs = x[bb, j * TMAIN:(j + 1) * TMAIN]
        if j > 0:
            halo = x[bb, j * TMAIN - THALO:j * TMAIN]
        else:
            halo = np.zeros((THALO, D), np.float32)
        x_e = np.ascontiguousarray(np.concatenate([halo, xs], axis=0))
        selt = np.zeros((8, 8), np.float32)
        pmask = np.zeros((8, 1), np.float32)
        for i in range(8):
            ib, ij = divmod(i, 4)
            if ib == bb and ij < j:
                pmask[i, 0] = 1.0
                for m in range(8):
                    mb, mj = divmod(m, 4)
                    if mb == bb and ij < mj < j:
                        selt[m, i] = 1.0
        in_maps.append({
            "x_ext": x_e, "w_in": w_in, "w_out": w_out,
            "wu_s": np.ascontiguousarray(wu_t[c * (D // NCORE):(c + 1) * (D // NCORE)]),
            "wg_s": np.ascontiguousarray(wg_t[c * (D // NCORE):(c + 1) * (D // NCORE)]),
            "wd_s": np.ascontiguousarray(wd_t[c * (EE // NCORE):(c + 1) * (EE // NCORE)]),
            "conv_wt": conv_wt, "conv_bt": conv_bt, "dt_bias": dt_b, "a_log": a_log,
            "d_row": d_row, "nw_row": nw_row,
            "ident_b": ident_b, "ident_f": ident_f, "triu_in": triu,
            "selt_in": selt, "pmask_in": pmask,
        })

    res = run_bass_kernel_spmd(nc, in_maps, core_ids=list(range(NCORE)))
    outs = [res.results[c]["out_d"] for c in range(NCORE)]
    full = np.stack([np.concatenate(outs[0:4], axis=0),
                     np.concatenate(outs[4:8], axis=0)], axis=0)
    return full.astype(np.float32)
